# version 17
# baseline (speedup 1.0000x reference)
import contextlib
import numpy as np
import ml_dtypes

import concourse.bass as bass
import concourse.mybir as mybir
from concourse.bass_utils import run_bass_kernel_spmd
from concourse.aot_env import allow_ici_all_to_all

F32 = mybir.dt.float32
BF16 = mybir.dt.bfloat16
F16 = mybir.dt.float16
U8 = mybir.dt.uint8
AF = mybir.ActivationFunctionType
ALU = mybir.AluOpType

NC = 8
D = 1024
DS = 128
NST = 32
LN_EPS = 1e-5
BF = np.dtype(ml_dtypes.bfloat16)


def vap(view, roff, shape_steps):
    rs = view.ap[0][0]
    dims = [[rs if s == 'p' else s, c] for s, c in shape_steps]
    return bass.AP(view.tensor, view.offset + roff[0] * rs + roff[1], dims)


class KB:
    def __init__(self, L=8192, SUPW=2048, dbg=0):
        assert L % SUPW == 0
        self.L = L
        self.SUPW = SUPW
        self.nsb = L // SUPW
        self.nsup = 2 * self.nsb
        self.JS = L // NC
        self.C = min(256, self.JS)
        self.ncb = self.JS // self.C
        self.ncall = 2 * self.ncb
        self.QW = self.C
        self.NQ = self.JS // self.QW
        self.QB = self.QW + 2
        self.W2 = 2 * self.QB
        self.CHK = [(c, min(512, self.QB - c)) for c in range(0, self.QB, 512)]
        self.PB = (0, 1024, 2048, 3072)
        self.CW = min(512, SUPW)
        self.NCH = SUPW // self.CW
        self.CW2 = min(1024, SUPW)
        self.NCH2 = SUPW // self.CW2
        self.dbg = dbg
        self.marks = {}
        self._gc = {}

    def E(self, eng, fn, inc=None):
        if eng in ("v", "s", "g") and self.eng == eng and self.cnt[eng] > 0:
            self.h[eng].wait_ge(self.sem[eng], self.cnt[eng])
        ins = fn() if self.eng == eng else None
        if inc is not None:
            s, n = inc
            if ins is not None:
                ins.then_inc(self.sem[s], n)
            self.cnt[s] += n
        return ins

    def W(self, eng, s, val):
        if self.eng == eng and val > 0:
            self.h[self.eng].wait_ge(self.sem[s], val)

    def mark(self, key):
        if self.pass0:
            assert key not in self.marks, key
            self.marks[key] = dict(self.cnt)

    def wm(self, eng, s, key):
        self.W(eng, s, self.marks[key][s])

    def sw(self, eng):
        self.W(eng, eng, self.cnt[eng])

    def build(self):
        nc = bass.Bass("TRN2", target_bir_lowering=False, debug=False)
        self.nc = nc
        L = self.L
        dt = nc.dram_tensor
        TB = self.JS + 2
        self.i = {}
        for name, shape in [
            ("xT", [D, 2 * L]), ("xres", [D, 2 * TB]), ("xresi", [D, 2 * self.JS]),
            ("winT", [D, 320]),
            ("woutT", [D, D]), ("wrT", [D, D]), ("wkT", [D, D]), ("wvT", [D, D]),
            ("qw", [D, D]), ("vw", [D, D]), ("owT", [D, D]),
            ("outb", [128, 8]), ("mur", [128, 8]), ("muk", [128, 8]),
            ("alogp", [DS, NST]), ("dcol", [DS, 1]), ("convw", [DS, 4]),
            ("convb", [DS, 1]), ("leadmask", [DS, 1]),
        ]:
            self.i[name] = dt(name, shape, F32, kind="ExternalInput").ap()
        for name, shape in [
            ("ones128", [128, 128]), ("ysel", [128, 32 * 128]),
            ("hsel", [128, 8 * 16]), ("ssel", [16, 8 * 128]),
        ]:
            self.i[name] = dt(name, shape, BF16, kind="ExternalInput").ap()
        self.i["dsel"] = dt("dsel", [128, 32 * 128], F16, kind="ExternalInput").ap()
        self.o_out = dt("out", [D, 2 * self.JS], F32, kind="ExternalOutput").ap()
        if self.dbg:
            self.o_z = dt("dbg_z", [DS, 2 * L], F32, kind="ExternalOutput").ap()
        self.a2ai = [dt(f"a2ai{c}", [NC, DS, self.C], BF16, kind="Internal").ap()
                     for c in range(self.ncall)]
        self.a2ao = [dt(f"a2ao{c}", [NC, DS, self.C], BF16, kind="Internal").ap()
                     for c in range(self.ncall)]
        self.leadi = dt("leadi", [NC, DS, 4], BF16, kind="Internal").ap()
        self.leado = dt("leado", [NC, DS, 4], BF16, kind="Internal").ap()
        SUPW = self.SUPW
        self.d_uc = dt("d_uc", [DS, self.nsup * SUPW], F16, kind="Internal").ap()
        self.d_dsp = dt("d_dsp", [DS, self.nsup * SUPW], F16, kind="Internal").ap()
        self.d_bcb = dt("d_bcb", [64, self.nsup * SUPW], BF16, kind="Internal").ap()

        es = contextlib.ExitStack()
        with es:
            self._alloc(nc, es)
            self.sem = {}
            for s in ("sp", "sd", "g", "t", "v", "s", "sm", "p0", "p1",
                      "bc", "stg0", "stg1", "rc", "out", "rep0", "rep1"):
                self.sem[s] = es.enter_context(nc.semaphore(s))
            with nc.Block() as blk:
                self.h = {}
                self.pass0 = True
                me = self

                def reg(name):
                    def f(hdl):
                        me.h[name] = hdl
                        me.eng = name
                        me.cnt = {k: 0 for k in (
                            "sp", "sd", "g", "t", "v", "s", "sm", "p0", "p1",
                            "bc", "stg0", "stg1", "rc", "out", "rep0", "rep1")}
                        me._walk()
                        me.pass0 = False
                    return f
                blk.sync(reg('sp'))
                blk.scalar(reg('s'))
                blk.vector(reg('v'))
                blk.tensor(reg('t'))
                blk.gpsimd(reg('g'))
        return nc

    def _alloc(self, nc, es):
        SUPW, W2 = self.SUPW, self.W2

        def sb(name, shape, dtp):
            return es.enter_context(nc.sbuf_tensor(name, shape, dtp))

        self.ones = sb("ones", [128, 128], BF16)
        self.ysel = sb("yselt", [128, 32 * 128], BF16)
        self.hsel = sb("hselt", [128, 8 * 16], BF16)
        self.ssel = sb("sselt", [16, 8 * 128], BF16)
        self.alogp = sb("alogpt", [DS, NST], F32)
        self.adev = sb("adevt", [DS, NST], F32)
        self.dcol = sb("dcolt", [DS, 1], F32)
        self.convw = sb("convwt", [DS, 4], F32)
        self.convb = sb("convbt", [DS, 1], F32)
        self.maskc = sb("maskct", [DS, 1], F32)
        self.mur = sb("murt", [128, 8], F32)
        self.muk = sb("mukt", [128, 8], F32)
        self.outbc = sb("outbt", [128, 8], F32)
        self.winT = sb("winTt", [128, 8 * 320], BF16)
        self.wsumrow = sb("wsumrowt", [1, 320], BF16)
        self.hcarry = sb("hcarryt", [128, 32], F32)
        self.zcol0 = sb("zcol0", [DS, 4], BF16)
        self.dsel = sb("dselt", [128, 32 * 128], F16)
        self.mmat = sb("mmatt", [128, 8 * 64], BF16)
        self.sigg = sb("siggt", [16, self.QB], F32)
        self.sigb = sb("sigbt", [16, self.QB], BF16)
        self.qsq = sb("qsqt", [16, self.QB], F32)
        self.mrepB = sb("mrepBt", [128, W2], BF16)
        self.rrepB = sb("rrepBt", [128, W2], BF16)
        self.vtmpB = sb("vtmpBt", [128, W2], F32)
        self.save2 = sb("save2t", [128, 16], BF16)
        self.save4 = sb("save4t", [128, 16], BF16)
        self.wstage = [sb(f"wstg{i}", [128, D], F32) for i in range(2)]
        self.vwblk = sb("vwblk", [128, D], BF16)
        self.macc = sb("macc", [128, 512], F32)
        self.cpool = sb("cpool", [128, 8], F32)
        nc.const_aps.register(F32, LN_EPS, self.cpool[:, 0:1])
        nc.const_aps.register(F32, -0.5, self.cpool[:, 1:2])
        nc.const_aps.register(F32, 1.0 / D, self.cpool[:, 2:3])
        nc.const_aps.register(F32, -1.0 / D, self.cpool[:, 3:4])
        self.xresb = [sb(f"xresb{i}", [128, W2], F32) for i in range(2)]

        layA, layB = {}, {}

        def mk(lay, name, p, cols, dtp, off):
            nb = cols * mybir.dt.size(dtp)
            lay[name] = (off, p, cols, dtp)
            return off + (nb + 63) // 64 * 64

        off = 0
        CW = self.CW
        for spec in [("xstage0", 128, SUPW, F32), ("xstage1", 128, SUPW, F32)] + \
                    [(f"xbf{i}", 128, SUPW, BF16) for i in range(8)] + \
                    [("xsq0", 128, self.CW2, BF16), ("xsq1", 128, self.CW2, BF16),
                     ("vtmp", 128, SUPW, F32),
                     ("uraw", DS, SUPW + 4, F32),
                     ("uconv0", DS, SUPW, F16), ("uconv1", DS, SUPW, F16),
                     ("dsp0", DS, SUPW, F16), ("dsp1", DS, SUPW, F16),
                     ("dtmp", DS, SUPW, F16),
                     ("bcbuf", 64, SUPW, BF16),
                     ("brep0", 128, SUPW, BF16), ("crep0", 128, SUPW, BF16),
                     ("brep1", 128, SUPW, BF16), ("crep1", 128, SUPW, BF16),
                     ("urep0", 128, SUPW, F16), ("urep1", 128, SUPW, F16),
                     ("lam0", 128, SUPW, F16), ("lam1", 128, SUPW, F16),
                     ("xb", 128, SUPW, BF16), ("hbuf", 128, SUPW, BF16),
                     ("hc0", 128, SUPW, BF16), ("hc1", 128, SUPW, BF16),
                     ("zsup", DS, SUPW, BF16),
                     ("negm", 1, SUPW, BF16), ("rrep", 128, SUPW, BF16),
                     ("mwork", 128, self.CW2, F32)]:
            off = mk(layA, *spec, off)
        endA = off
        off = 0
        for spec in [(w, 128, 8 * 1024, BF16)
                     for w in ("wout", "wr", "wk", "wv", "qwb", "owb")] + \
                    [("zrecv", 128, 8 * W2, BF16), ("rres", 128, 8 * W2, F32),
                     ("xn", 128, 8 * W2, BF16), ("tmpa", 128, 8 * W2, BF16),
                     ("tmpb", 128, 8 * W2, BF16), ("tmpc", 128, 8 * W2, BF16)]:
            off = mk(layB, *spec, off)
        endB = off
        self.ARENA_B = max(endA, endB)
        self.arena = es.enter_context(nc.sbuf_tensor("arena", [128, self.ARENA_B], U8))
        self._layA, self._layB = layA, layB
        self._vcache = {}
        self.ps = es.enter_context(nc.psum_tensor("psar", [128, 4096], F32))

    def _view(self, lay, name):
        key = (id(lay), name)
        if key not in self._vcache:
            off, p, cols, dtp = lay[name]
            esz = mybir.dt.size(dtp)
            assert off % esz == 0
            v = bass.AP(self.arena, 0,
                        [[self.ARENA_B, p], [1, cols * esz]])
            v = bass.AP(self.arena, off, [[self.ARENA_B, p], [1, cols * esz]]).bitcast(dtp)
            self._vcache[key] = v
        return self._vcache[key]

    def a(self, name):
        return self._view(self._layA, name)

    def b(self, name):
        return self._view(self._layB, name)

    def _walk(self):
        self._uidx = 0
        self._qidx = 0
        self._qc = {}
        self._startup()
        for sup in range(self.nsup):
            self._super(sup)
        self._exchange_and_wprep()
        for qq in range(self.NQ):
            if qq == 0:
                self._quarter(qq)
            else:
                self._quarterI(qq)
        self._finish()

    def _startup(self):
        E, h = self.E, self.h
        loads = [
            (self.ones[:], self.i["ones128"]), (self.ysel[:], self.i["ysel"]),
            (self.hsel[:], self.i["hsel"]), (self.ssel[:], self.i["ssel"]),
            (self.alogp[:], self.i["alogp"]), (self.dcol[:], self.i["dcol"]),
            (self.convw[:], self.i["convw"]), (self.convb[:], self.i["convb"]),
            (self.maskc[:], self.i["leadmask"]),
            (self.dsel[:], self.i["dsel"]),
            (self.mur[:], self.i["mur"]),
            (self.muk[:], self.i["muk"]),
            (self.outbc[:], self.i["outb"]),
        ]
        for dst, src in loads:
            E('sp', lambda d=dst, s=src: h['sp'].dma_start(out=d, in_=s), inc=('sm', 16))
        self.mark("smalls")
        for i in range(8):
            if i >= 2:
                self.wm('sp', 's', f"wic{i-2}")
            E('sp', lambda i=i: h['sp'].dma_start(
                out=self.a(f"xstage{i%2}")[:, 0:320],
                in_=self.i["winT"][128 * i:128 * (i + 1), :]), inc=(f"p{i%2}", 16))
            self.mark(f"wil{i}")
            self.wm('s', f"p{i%2}", f"wil{i}")
            E('s', lambda i=i: h['s'].copy(
                self.winT[:, 320 * i:320 * (i + 1)], self.a(f"xstage{i%2}")[:, 0:320]),
                inc=('s', 1))
            self.mark(f"wic{i}")
        self.wm('s', 'sm', "smalls")
        E('s', lambda: h['s'].activation(self.adev[:], self.alogp[:], AF.Exp), inc=('s', 1))
        self.mark("adevexp")
        self.wm('v', 's', "adevexp")
        E('v', lambda: h['v'].tensor_scalar_mul(self.adev[:], self.adev[:], -1.0), inc=('v', 1))
        pa = self._ugate()
        self.wm('t', 's', "wic7")
        self.wm('t', 'sm', "smalls")
        for i in range(8):
            E('t', lambda i=i, pa=pa: h['t'].matmul(
                self.ps[0:1, pa:pa + 320], self.ones[:, 0:1],
                self.winT[:, 320 * i:320 * (i + 1)],
                start=(i == 0), stop=(i == 7)), inc=('t', 1))
        self.mark("wsmm")
        self.wm('s', 't', "wsmm")
        E('s', lambda pa=pa: h['s'].copy(self.wsumrow[0:1, :], self.ps[0:1, pa:pa + 320]),
          inc=('s', 1))
        self.mark("wsumrow")
        self._udone('s', "wsumrow")
        E('v', lambda: h['v'].memset(self.cpool[:, 0:1], LN_EPS), inc=('v', 1))
        E('v', lambda: h['v'].memset(self.cpool[:, 1:2], -0.5), inc=('v', 1))
        E('v', lambda: h['v'].memset(self.cpool[:, 2:3], 1.0 / D), inc=('v', 1))
        E('v', lambda: h['v'].memset(self.cpool[:, 3:4], -1.0 / D), inc=('v', 1))
        E('v', lambda: h['v'].memset(self.zcol0[:], 0.0), inc=('v', 1))
        E('v', lambda: h['v'].memset(self.a("uraw")[:, 0:4], 0.0), inc=('v', 1))
        E('v', lambda: h['v'].memset(self.hcarry[:], 0.0), inc=('v', 1))
        self.mark("zc0")
        self.wm('sp', 'v', "zc0")
        E('sp', lambda: h['sp'].dma_start(out=self.leadi[0], in_=self.zcol0[:]),
          inc=('stg0', 16))
        self.mark("startup")
        for n in range(self.nsup):
            for u in self._prep_units(n):
                u()
        self._reload(0)()
        self.mark("preamble")

    def _ugate(self):
        k = self._uidx
        if k >= 2:
            sem, name = self._gc[k - 2]
            self.wm('t', sem, name)
        return 2048 + (k % 2) * 1024

    def _udone(self, sem, name):
        self._gc[self._uidx] = (sem, name)
        self._uidx += 1

    def _prep_units(self, n):
        E, h, a = self.E, self.h, self.a
        SUPW, CW2, NCH2 = self.SUPW, self.CW2, self.NCH2
        par = n % 2
        nsub = CW2 // 512
        units = []

        def u_load(i):
            def f():
                if i >= 2:
                    self.wm('sp', 's', f"cast_{n}_{i-2}")
                elif n > 0:
                    self.wm('sp', 's', f"cast_{n-1}_{6+i}")
                else:
                    self.wm('sp', 's', f"wic{6+i}")
                E('sp', lambda i=i: h['sp'].dma_start(
                    out=a(f"xstage{i%2}")[:],
                    in_=self.i["xT"][128 * i:128 * (i + 1),
                                     n * SUPW:(n + 1) * SUPW]),
                    inc=(f"p{i%2}", 16))
                self.mark(f"xl_{n}_{i}")
                self.wm('s', f"p{i%2}", f"xl_{n}_{i}")
                if n > 0:
                    self.wm('s', 't', f"prj_{n-1}_b_{NCH2-1}")
                E('s', lambda i=i: h['s'].copy(a(f"xbf{i}")[:],
                                               a(f"xstage{i%2}")[:]), inc=('s', 1))
                self.mark(f"cast_{n}_{i}")
            return f
        for i in range(8):
            units.append(u_load(i))

        def u_sum(c):
            def f():
                pa = self._ugate()
                for i in range(8):
                    self.wm('t', 's', f"cast_{n}_{i}")
                    for sub in range(nsub):
                        E('t', lambda i=i, c=c, sub=sub, pa=pa: h['t'].matmul(
                            self.ps[:, pa + sub * 512:pa + sub * 512 + 512],
                            self.ones[:],
                            a(f"xbf{i}")[:, c * CW2 + sub * 512:
                                         c * CW2 + sub * 512 + 512],
                            start=(i == 0), stop=(i == 7)), inc=('t', 1))
                self.mark(f"summ_{n}_{c}")
                self.wm('s', 't', f"summ_{n}_{c}")
                if n == 0 and c == 0:
                    self.wm('s', 'v', "zc0")
                E('s', lambda c=c, pa=pa: h['s'].activation(
                    a("negm")[0:1, c * CW2:(c + 1) * CW2],
                    self.ps[0:1, pa:pa + CW2],
                    AF.Copy, scale=-1.0 / D), inc=('s', 1))
                self.mark(f"negm_{n}_{c}")
                E('s', lambda pa=pa: h['s'].activation(
                    a("mwork")[:], self.ps[:, pa:pa + CW2],
                    AF.Square, scale=1.0 / D), inc=('s', 1))
                self.mark(f"msq_{n}_{c}")
                self._udone('s', f"msq_{n}_{c}")
            return f

        def u_sumsq(c):
            def f():
                pa = self._ugate()
                for i in range(8):
                    g = c * 8 + i
                    if g >= 2:
                        pc, pi = divmod(g - 2, 8)
                        self.wm('s', 't', f"smm_{n}_{pc}_{pi}")
                    elif n > 0:
                        self.wm('s', 't', f"smm_{n-1}_{NCH2-1}_{6+g}")
                    E('s', lambda c=c, i=i: h['s'].activation(
                        a(f"xsq{i%2}")[:], a(f"xbf{i}")[:, c * CW2:(c + 1) * CW2],
                        AF.Square), inc=('s', 1))
                    self.mark(f"sq_{n}_{c}_{i}")
                    self.wm('t', 's', f"sq_{n}_{c}_{i}")
                    for sub in range(nsub):
                        E('t', lambda i=i, sub=sub, pa=pa: h['t'].matmul(
                            self.ps[:, pa + sub * 512:pa + sub * 512 + 512],
                            self.ones[:],
                            a(f"xsq{i%2}")[:, sub * 512:sub * 512 + 512],
                            start=(i == 0), stop=(i == 7)), inc=('t', 1))
                    self.mark(f"smm_{n}_{c}_{i}")
                self.wm('v', 't', f"smm_{n}_{c}_7")
                self.wm('v', 's', f"msq_{n}_{c}")
                E('v', lambda pa=pa: h['v'].scalar_tensor_tensor(
                    a("mwork")[:], self.ps[:, pa:pa + CW2], 1.0 / D,
                    a("mwork")[:], ALU.mult, ALU.subtract), inc=('v', 1))
                self.mark(f"vrow_{n}_{c}")
                self.wm('s', 'v', f"vrow_{n}_{c}")
                E('s', lambda: h['s'].activation(
                    a("mwork")[:], a("mwork")[:],
                    AF.Ln, bias=LN_EPS), inc=('s', 1))
                E('s', lambda c=c: h['s'].activation(
                    a("rrep")[:, c * CW2:(c + 1) * CW2], a("mwork")[:],
                    AF.Exp, scale=-0.5), inc=('s', 1))
                self.mark(f"rstd_{n}_{c}")
                self._udone('v', f"vrow_{n}_{c}")
            return f

        for c in range(NCH2):
            units.append(u_sum(c))
            units.append(u_sumsq(c))

        def u_proj(g, c0w, gw, c):
            def f():
                pa = self._ugate()
                for i in range(8):
                    self.wm('t', 's', f"cast_{n}_{i}")
                    for sub in range(nsub):
                        E('t', lambda i=i, c=c, sub=sub, pa=pa: h['t'].matmul(
                            self.ps[0:gw, pa + sub * 512:pa + sub * 512 + 512],
                            self.winT[:, 320 * i + c0w:320 * i + c0w + gw],
                            a(f"xbf{i}")[:, c * CW2 + sub * 512:
                                         c * CW2 + sub * 512 + 512],
                            start=(i == 0), stop=False), inc=('t', 1))
                self.wm('t', 's', f"negm_{n}_{c}")
                for sub in range(nsub):
                    E('t', lambda c=c, sub=sub, pa=pa: h['t'].matmul(
                        self.ps[0:gw, pa + sub * 512:pa + sub * 512 + 512],
                        self.wsumrow[0:1, c0w:c0w + gw],
                        a("negm")[0:1, c * CW2 + sub * 512:
                                  c * CW2 + sub * 512 + 512],
                        start=False, stop=True), inc=('t', 1))
                self.mark(f"prj_{n}_{g}_{c}")
                self.wm('v', 't', f"prj_{n}_{g}_{c}")
                self.wm('v', 's', f"rstd_{n}_{c}")
                if g == 'b':
                    if c == 0 and n > 0:
                        self.wm('v', 'rep0', f"ppb_{n-1}")
                    dst = a("bcbuf")[0:64, c * CW2:(c + 1) * CW2]
                elif g == 'u':
                    dst = a("uraw")[:, 4 + c * CW2:4 + (c + 1) * CW2]
                else:
                    dst = a("dtmp")[:, c * CW2:(c + 1) * CW2]
                E('v', lambda dst=dst, c=c, pa=pa: h['v'].tensor_mul(
                    dst, self.ps[0:gw, pa:pa + CW2],
                    a("rrep")[0:gw, c * CW2:(c + 1) * CW2]), inc=('v', 1))
                self.mark(f"dst_{n}_{g}_{c}")
                self._udone('v', f"dst_{n}_{g}_{c}")
                if g == 'd':
                    self.wm('s', 'v', f"dst_{n}_d_{c}")
                    E('s', lambda c=c: h['s'].activation(
                        a("vtmp")[:, c * CW2:(c + 1) * CW2],
                        a("dtmp")[:, c * CW2:(c + 1) * CW2], AF.Exp), inc=('s', 1))
                    if n >= 2:
                        self.wm('s', 'rep0', f"ppd_{n-2}")
                    E('s', lambda c=c: h['s'].activation(
                        a(f"dsp{par}")[:, c * CW2:(c + 1) * CW2],
                        a("vtmp")[:, c * CW2:(c + 1) * CW2],
                        AF.Ln, bias=1.0), inc=('s', 1))
                    self.mark(f"sp_{n}_{c}")
            return f

        for (g, c0w, gw) in (('u', 0, 128), ('d', 128, 128), ('b', 256, 64)):
            for c in range(NCH2):
                units.append(u_proj(g, c0w, gw, c))

        def u_convA():
            def f():
                if n % self.nsb == 0:
                    E('v', lambda: h['v'].memset(a("uraw")[:, 1:4], 0.0),
                      inc=('v', 1))
                self.wm('v', 's', f"sp_{n}_{NCH2-1}")
                if n >= 2:
                    self.wm('v', 'rep0', f"ppu_{n-2}")
                E('v', lambda: h['v'].tensor_scalar(
                    a("vtmp")[:, 0:SUPW], a("uraw")[:, 1:1 + SUPW],
                    self.convw[:, 0:1], self.convb[:, 0:1], ALU.mult, ALU.add),
                    inc=('v', 1))
                E('v', lambda: h['v'].scalar_tensor_tensor(
                    a("vtmp")[:, 0:SUPW], a("uraw")[:, 2:2 + SUPW],
                    self.convw[:, 1:2], a("vtmp")[:, 0:SUPW],
                    ALU.mult, ALU.add), inc=('v', 1))
            return f

        def u_convB():
            def f():
                E('v', lambda: h['v'].scalar_tensor_tensor(
                    a("vtmp")[:, 0:SUPW], a("uraw")[:, 3:3 + SUPW],
                    self.convw[:, 2:3], a("vtmp")[:, 0:SUPW],
                    ALU.mult, ALU.add), inc=('v', 1))
                E('v', lambda: h['v'].scalar_tensor_tensor(
                    a(f"uconv{par}")[:], a("uraw")[:, 4:4 + SUPW],
                    self.convw[:, 3:4], a("vtmp")[:, 0:SUPW],
                    ALU.mult, ALU.add), inc=('v', 1))
                E('v', lambda: h['v'].tensor_copy(
                    a("uraw")[:, 1:4], a("uraw")[:, 1 + SUPW:4 + SUPW]),
                    inc=('v', 1))
                self.mark(f"conv_{n}")
            return f

        def u_store():
            def f():
                if n >= 1:
                    self.wm('sp', 'rep0', f"ppb_{n-1}")
                self.wm('sp', 'v', f"conv_{n}")
                E('sp', lambda: h['sp'].dma_start(
                    out=self.d_uc[:, n * SUPW:(n + 1) * SUPW],
                    in_=a(f"uconv{par}")[:]), inc=('rep0', 16))
                self.mark(f"ppu_{n}")
                self.wm('sp', 's', f"sp_{n}_{NCH2-1}")
                E('sp', lambda: h['sp'].dma_start(
                    out=self.d_dsp[:, n * SUPW:(n + 1) * SUPW],
                    in_=a(f"dsp{par}")[:]), inc=('rep0', 16))
                self.mark(f"ppd_{n}")
                self.wm('sp', 'v', f"dst_{n}_b_{NCH2-1}")
                E('sp', lambda: h['sp'].dma_start(
                    out=self.d_bcb[:, n * SUPW:(n + 1) * SUPW],
                    in_=a("bcbuf")[0:64, :]), inc=('rep0', 16))
                self.mark(f"ppb_{n}")
            return f

        units.append(u_convA())
        units.append(u_convB())
        units.append(u_store())
        return units

    def _reload(self, n):
        E, h, a = self.E, self.h, self.a
        SUPW = self.SUPW
        par = n % 2

        def f():
            self.wm('sp', 'rep0', f"ppb_{n}")
            if n >= 1:
                self.wm('sp', 'bc', f"bcrep_{n-1}")
            if n >= 2:
                self.wm('sp', 'v', f"zev_{n-2}")
                self.wm('sp', 'v', f"hC_{n-2}_31")
                self.wm('sp', 't', f"Amm_{n-2}_31_d{self.NCH2-1}")
            E('sp', lambda: h['sp'].dma_start(
                out=a(f"uconv{par}")[:],
                in_=self.d_uc[:, n * SUPW:(n + 1) * SUPW]), inc=('bc', 16))
            E('sp', lambda: h['sp'].dma_start(
                out=a(f"dsp{par}")[:],
                in_=self.d_dsp[:, n * SUPW:(n + 1) * SUPW]), inc=('bc', 16))
            for gi in range(4):
                E('sp', lambda gi=gi: h['sp'].dma_start(
                    out=a(f"brep{par}")[32 * gi:32 * (gi + 1), :],
                    in_=self.d_bcb[0:32, n * SUPW:(n + 1) * SUPW]), inc=('bc', 16))
                E('sp', lambda gi=gi: h['sp'].dma_start(
                    out=a(f"crep{par}")[32 * gi:32 * (gi + 1), :],
                    in_=self.d_bcb[32:64, n * SUPW:(n + 1) * SUPW]), inc=('bc', 16))
            self.mark(f"bcrep_{n}")
        return f

    def _ymm(self, sup, t):
        E, h, a = self.E, self.h, self.a
        CW, NCH = self.CW, self.NCH
        self.wm('t', 'v', f"hC_{sup}_{t}")
        if t == 0 and sup > 0:
            self.wm('t', 'v', f"zev_{sup-1}")
        for c in range(NCH):
            E('t', lambda t=t, c=c: h['t'].matmul(
                self.ps[:, c * CW:(c + 1) * CW],
                self.ysel[:, 128 * t:128 * (t + 1)],
                a(f"hc{t%2}")[:, c * CW:(c + 1) * CW],
                start=(t == 0), stop=(t == 31)), inc=('t', 1))
        self.mark(f"ymm_{sup}_{t}")

    def _super(self, sup):
        E, h, a = self.E, self.h, self.a
        SUPW, CW2, NCH2 = self.SUPW, self.CW2, self.NCH2
        nsub = CW2 // 512
        par = sup % 2
        batch_start = (sup % self.nsb) == 0
        nt = 32
        sched = [[] for _ in range(nt)]
        if sup + 1 < self.nsup:
            sched[20].append(self._reload(sup + 1))
        ucv, dspv = a(f"uconv{par}"), a(f"dsp{par}")
        for t in range(nt):
            buf = t % 2
            for kind, src_v in (('u', ucv), ('d', dspv)):
                for hh in range(NCH2):
                    pa = self._ugate()
                    if t == 0 and kind == 'u' and hh == 0:
                        self.wm('t', 'bc', f"bcrep_{sup}")
                    for sub in range(nsub):
                        hs = hh * CW2 + sub * 512
                        src = vap(src_v, (0, hs), [['p', 128], [1, 512]])
                        E('t', lambda pa=pa, sub=sub, src=src, t=t: h['t'].matmul(
                            self.ps[:, pa + sub * 512:pa + sub * 512 + 512],
                            self.dsel[:, 128 * t:128 * (t + 1)],
                            src, start=True, stop=True), inc=('t', 1))
                    self.mark(f"Amm_{sup}_{t}_{kind}{hh}")
                    self.wm('s', 't', f"Amm_{sup}_{t}_{kind}{hh}")
                    if kind == 'u':
                        if t >= 2:
                            self.wm('s', 'v', f"xb_{sup}_{t-2}")
                        elif sup > 0:
                            self.wm('s', 'v', f"xb_{sup-1}_{30+t}")
                        E('s', lambda buf=buf, hh=hh, pa=pa: h['s'].copy(
                            a(f"urep{buf}")[:, hh * CW2:(hh + 1) * CW2],
                            self.ps[:, pa:pa + CW2]), inc=('s', 1))
                    else:
                        if t >= 2:
                            self.wm('s', 'v', f"scan_{sup}_{t-2}")
                        elif sup > 0:
                            self.wm('s', 'v', f"scan_{sup-1}_{30+t}")
                        E('s', lambda buf=buf, hh=hh, pa=pa, t=t: h['s'].activation(
                            a(f"lam{buf}")[:, hh * CW2:(hh + 1) * CW2],
                            self.ps[:, pa:pa + CW2],
                            AF.Exp, scale=self.adev[:, t:t + 1]), inc=('s', 1))
                    self.mark(f"Acons_{sup}_{t}_{kind}{hh}")
                    self._udone('s', f"Acons_{sup}_{t}_{kind}{hh}")
            self.wm('v', 's', f"Acons_{sup}_{t}_u{NCH2-1}")
            if t == 0:
                self.wm('v', 'bc', f"bcrep_{sup}")
            E('v', lambda buf=buf: h['v'].tensor_mul(
                a("xb")[:], a(f"urep{buf}")[:], a(f"brep{par}")[:]), inc=('v', 1))
            self.mark(f"xb_{sup}_{t}")
            if t >= 1:
                pb = (t - 1) % 2
                if t >= 3:
                    self.wm('v', 't', f"ymm_{sup}_{t-3}")
                elif sup > 0:
                    self.wm('v', 't', f"ymm_{sup-1}_{29+t}")
                E('v', lambda pb=pb: h['v'].tensor_mul(
                    a(f"hc{pb}")[:], a("hbuf")[:], a(f"crep{par}")[:]),
                    inc=('v', 1))
                self.mark(f"hC_{sup}_{t-1}")
            self.wm('v', 's', f"Acons_{sup}_{t}_d{NCH2-1}")
            init = 0.0 if batch_start else self.hcarry[:, t:t + 1]
            E('v', lambda buf=buf, init=init: h['v'].tensor_tensor_scan(
                a("hbuf")[:], a(f"lam{buf}")[:], a("xb")[:], init,
                ALU.mult, ALU.add), inc=('v', 1))
            self.mark(f"scan_{sup}_{t}")
            E('v', lambda t=t: h['v'].tensor_copy(
                self.hcarry[:, t:t + 1], a("hbuf")[:, SUPW - 1:SUPW]),
                inc=('v', 1))
            if t >= 2:
                self._ymm(sup, t - 2)
            for cl in sched[t]:
                cl()
        self.wm('v', 't', f"ymm_{sup}_29")
        E('v', lambda: h['v'].tensor_mul(
            a("hc1")[:], a("hbuf")[:], a(f"crep{par}")[:]), inc=('v', 1))
        self.mark(f"hC_{sup}_31")
        self._ymm(sup, 30)
        self._ymm(sup, 31)

        self.wm('v', 't', f"ymm_{sup}_31")
        if sup > 0:
            self.wm('v', f"stg{(sup-1) // self.nsb}", f"staged_{sup-1}")
        E('v', lambda: h['v'].scalar_tensor_tensor(
            a("zsup")[:], a(f"uconv{par}")[:], self.dcol[:, 0:1],
            self.ps[0:DS, 0:SUPW], ALU.mult, ALU.add), inc=('v', 1))
        self.mark(f"zev_{sup}")
        self.wm('sp', 'v', f"zev_{sup}")
        if self.dbg:
            E('sp', lambda: h['sp'].dma_start(
                out=self.o_z[:, sup * SUPW:(sup + 1) * SUPW], in_=a("zsup")[:]),
                inc=('out', 16))
        bidx = sup // self.nsb
        q = sup % self.nsb
        nslot2 = max(1, SUPW // self.JS)
        spl = SUPW // nslot2
        for mi in range(nslot2):
            m = nslot2 * q + mi
            for c in range(self.ncb):
                E('sp', lambda mi=mi, m=m, c=c: h['sp'].dma_start(
                    out=self.a2ai[bidx * self.ncb + c][m],
                    in_=a("zsup")[:, mi * spl + c * self.C: mi * spl + (c + 1) * self.C]),
                    inc=(f"stg{bidx}", 16))
        for mi in range(nslot2):
            m = nslot2 * q + mi + 1
            if m <= 7:
                E('sp', lambda mi=mi, m=m: h['sp'].dma_start(
                    out=self.leadi[m][:, 2 * bidx:2 * bidx + 2],
                    in_=a("zsup")[:, (mi + 1) * spl - 2:(mi + 1) * spl]),
                    inc=(f"stg{bidx}", 16))
        self.mark(f"staged_{sup}")

    def _exchange_and_wprep(self):
        E, h = self.E, self.h
        for c in range(self.ncall):
            bidx = c // self.ncb
            self.wm('g', f"stg{bidx}", f"staged_{(bidx + 1) * self.nsb - 1}")
            with allow_ici_all_to_all():
                E('g', lambda c=c: h['g'].collective_compute(
                    "AllToAll", ALU.bypass, replica_groups=[list(range(NC))],
                    ins=[self.a2ai[c]], outs=[self.a2ao[c]]), inc=('g', 1))
            self.mark(f"a2a_{c}")
        self.wm('g', 'stg0', f"staged_{self.nsup - 1}")
        self.wm('g', 'stg1', f"staged_{self.nsup - 1}")
        with allow_ici_all_to_all():
            E('g', lambda: h['g'].collective_compute(
                "AllToAll", ALU.bypass, replica_groups=[list(range(NC))],
                ins=[self.leadi], outs=[self.leado]), inc=('g', 1))
        self.mark("a2a_lead")

        wlist = [("wout", "woutT"), ("wr", "wrT"), ("wk", "wkT"), ("wv", "wvT"),
                 ("qwb", "qw"), ("owb", "owT")]
        n = 0
        for wname, iname in wlist:
            for i in range(8):
                if n >= 2:
                    self.wm('sp', 's', f"wc_{n-2}")
                E('sp', lambda iname=iname, i=i, n=n: h['sp'].dma_start(
                    out=self.wstage[n % 2][:],
                    in_=self.i[iname][128 * i:128 * (i + 1), :]), inc=(f"p{n%2}", 16))
                self.mark(f"wl_{n}")
                self.wm('s', f"p{n%2}", f"wl_{n}")
                if n == 0:
                    self.wm('s', 'stg0', f"staged_{self.nsup - 1}")
                    self.wm('s', 'stg1', f"staged_{self.nsup - 1}")
                    self.wm('s', 'v', f"zev_{self.nsup - 1}")
                    self.wm('s', 't', f"ymm_{self.nsup - 1}_31")
                E('s', lambda wname=wname, i=i, n=n: h['s'].copy(
                    self.b(wname)[:, 1024 * i:1024 * (i + 1)], self.wstage[n % 2][:]),
                    inc=('s', 1))
                self.mark(f"wc_{n}")
                n += 1
        self.mark("wprep")
        for i in range(8):
            if n >= 2:
                self.wm('sp', 's', f"wc_{n-2}")
            E('sp', lambda i=i, n=n: h['sp'].dma_start(
                out=self.wstage[n % 2][:],
                in_=self.i["vw"][128 * i:128 * (i + 1), :]), inc=(f"p{n%2}", 16))
            self.mark(f"wl_{n}")
            self.wm('s', f"p{n%2}", f"wl_{n}")
            if i > 0:
                self.wm('s', 't', f"mmmb_{i-1}")
            E('s', lambda n=n: h['s'].copy(self.vwblk[:], self.wstage[n % 2][:]),
              inc=('s', 1))
            self.mark(f"wc_{n}")
            self.wm('t', 's', f"wc_{n}")
            if i > 0:
                self.wm('t', 'v', f"mac_{i-1}")
            for hh in range(16):
                po, pc = 64 * (hh % 2), 64 * (hh // 2)
                E('t', lambda hh=hh, i=i, po=po, pc=pc: h['t'].matmul(
                    self.ps[po:po + 64, pc:pc + 64],
                    self.b("qwb")[:, 1024 * i + 64 * hh:1024 * i + 64 * (hh + 1)],
                    self.vwblk[:, 64 * hh:64 * (hh + 1)],
                    start=True, stop=True),
                    inc=('t', 1))
            self.mark(f"mmmb_{i}")
            self.wm('v', 't', f"mmmb_{i}")
            if i == 0:
                E('v', lambda: h['v'].tensor_copy(self.macc[:], self.ps[:, 0:512]),
                  inc=('v', 1))
            else:
                E('v', lambda: h['v'].tensor_add(self.macc[:], self.macc[:],
                                                 self.ps[:, 0:512]), inc=('v', 1))
            self.mark(f"mac_{i}")
            n += 1
        self.mark("mmm")
        self.wm('v', 't', "mmm")
        E('v', lambda: h['v'].tensor_scalar_mul(self.mmat[:], self.macc[:],
                                                0.125), inc=('v', 1))
        for tname in ("tmpa", "tmpb", "tmpc"):
            E('v', lambda tname=tname: h['v'].memset(self.b(tname)[:], 0.0),
              inc=('v', 1))
        self.mark("mmat")

    def lv(self, t, eg, bb=None):
        base = self.b(t)
        W2, QB = self.W2, self.QB
        if bb is None:
            return base[:, eg * W2:(eg + 1) * W2]
        return base[:, eg * W2 + bb * QB:eg * W2 + bb * QB + QB]

    def lv2(self, t, eg):
        v = self.b(t)
        return vap(v, (0, eg * self.W2), [['p', 128], [self.QB, 2], [1, self.QB]])

    def psv(self, col0):
        return bass.AP(self.ps, col0, [[4096, 128], [1024, 2], [1, self.QB]])

    def _quarter(self, qq):
        E, h = self.E, self.h
        QB, QW, W2 = self.QB, self.QW, self.W2
        lv, lv2 = self.lv, self.lv2
        zr = self.b("zrecv")

        ncq = self.QW // self.C
        for bb in range(2):
            for u in range(ncq):
                c = bb * self.ncb + qq * ncq + u
                self.wm('sp', 'g', f"a2a_{c}")
                if qq == 0 and u == 0:
                    self.wm('sp', 'g', "a2a_lead")
                if qq > 0 and u == 0:
                    self.wm('sp', 't', f"opmm_{qq-1}_7")
                dst = vap(zr, (0, bb * QB + 2 + u * self.C),
                          [['p', 128], [W2, 8], [1, self.C]])
                E('sp', lambda c=c, dst=dst: h['sp'].dma_start(
                    out=dst, in_=self.a2ao[c].rearrange("s p t -> p s t")), inc=('rc', 16))
            dl = vap(zr, (0, bb * QB), [['p', 128], [W2, 8], [1, 2]])
            if qq == 0:
                E('sp', lambda dl=dl, bb=bb: h['sp'].dma_start(
                    out=dl,
                    in_=self.leado.rearrange("s p t -> p s t")[:, :, 2 * bb:2 * bb + 2]),
                    inc=('rc', 16))
            else:
                cprev = bb * self.ncb + qq * ncq - 1
                E('sp', lambda cprev=cprev, dl=dl: h['sp'].dma_start(
                    out=dl,
                    in_=self.a2ao[cprev].rearrange("s p t -> p s t")[:, :, self.C - 2:self.C]),
                    inc=('rc', 16))
        self.mark(f"zr_{qq}")

        for eg in range(8):
            if eg >= 2:
                self.wm('sp', 'v', f"r1_{qq}_{eg-2}")
            elif qq > 0:
                self.wm('sp', 'v', f"r1_{qq-1}_{6+eg}")
            else:
                self.wm('sp', 's', "wc_55")
            src = bass.AP(self.i["xres"].tensor,
                          128 * eg * 2 * (self.JS + 2) + qq * QW,
                          [[2 * (self.JS + 2), 128], [(self.JS + 2), 2], [1, QB]])
            E('sp', lambda src=src, eg=eg: h['sp'].dma_start(
                out=self.xresb[eg % 2][:].rearrange("p (b t) -> p b t", b=2), in_=src),
                inc=(f"p{eg%2}", 16))
            self.mark(f"xr_{qq}_{eg}")
            self.wm('t', 'rc', f"zr_{qq}")
            if qq == 0 and eg == 0:
                self.wm('t', 'v', "mmat")
            if eg >= 1:
                self.wm('t', 'v', f"r1_{qq}_{eg-1}")
            elif qq > 0:
                self.wm('t', 'v', f"r1_{qq-1}_7")
            for r in range(8):
                for bb in range(2):
                    for c0, cw in self.CHK:
                        E('t', lambda eg=eg, r=r, bb=bb, c0=c0, cw=cw: h['t'].matmul(
                            self.ps[:, bb * 1024 + c0:bb * 1024 + c0 + cw],
                            self.b("wout")[:, 1024 * r + 128 * eg:1024 * r + 128 * (eg + 1)],
                            lv("zrecv", r, bb)[:, c0:c0 + cw],
                            start=(r == 0), stop=(r == 7)),
                            inc=('t', 1))
            self.mark(f"opmm_{qq}_{eg}")
            self.wm('v', 't', f"opmm_{qq}_{eg}")
            self.wm('v', f"p{eg%2}", f"xr_{qq}_{eg}")
            if qq == 0 and eg == 0:
                self.wm('v', 's', "wprep")
            if qq > 0 and eg == 0:
                self.wm('v', 'out', f"out_{qq-1}")
            E('v', lambda eg=eg: h['v'].tensor_add(
                lv2("rres", eg), self.xresb[eg % 2][:].rearrange("p (b t) -> p b t", b=2),
                self.psv(0)), inc=('v', 1))
            self.mark(f"r1_{qq}_{eg}")

        self._layer_norm(qq, "l2", f"r1_{qq}_7")
        self._save_old(qq, self.save2)
        self._ffn(qq, "f1", "l2")
        self._layer_norm(qq, "l3", f"fo_{qq}_f1_7")
        self._attnb(qq)
        self._layer_norm(qq, "l4", f"r3_{qq}_7")
        self._save_old(qq, self.save4)
        self._ffn(qq, "f2", "l4")

        for eg in range(8):
            self.wm('sp', 'v', f"fo_{qq}_f2_{eg}")
            rr = self.b("rres")
            src = vap(rr, (0, eg * W2 + 2), [['p', 128], [QB, 2], [1, QW]])
            dst = bass.AP(self.o_out.tensor, 128 * eg * 2 * self.JS + qq * 2 * QW,
                          [[2 * self.JS, 128], [QW, 2], [1, QW]])
            E('sp', lambda src=src, dst=dst: h['sp'].dma_start(out=dst, in_=src),
              inc=('out', 16))
        self.mark(f"out_{qq}")

    def _save_old(self, qq, dstbuf):
        if self.NQ <= 1:
            return
        E, h = self.E, self.h
        xnv = self.b("xn")
        src = vap(xnv, (0, self.QB - 1), [['p', 128], [self.W2, 8], [self.QB, 2]])
        E('v', lambda src=src, dstbuf=dstbuf: h['v'].tensor_copy(
            dstbuf[:].rearrange("p (a b) -> p a b", a=8), src), inc=('v', 1))

    def rl(self, t, eg):
        return self.b(t)[:, eg * self.W2 + 4:eg * self.W2 + 4 + 2 * self.QW]

    def rl2(self, t, eg, b):
        return vap(self.b(t), (0, eg * self.W2 + 4 + b),
                   [['p', 128], [2, self.QW]])

    def shl(self, t, eg):
        return self.b(t)[:, eg * self.W2 + 2:eg * self.W2 + 2 + 2 * self.QW]

    def _stgate(self, extra=None):
        k = self._qidx
        if k >= 4:
            sem, name = self._qc[k - 4]
            self.wm('t', sem, name)
        if extra is not None:
            sem, name = extra
            self.wm('t', sem, name)
        return 512 * (k % 4)

    def _stdone(self, sem, name):
        self._qc[self._qidx] = (sem, name)
        self._qidx += 1

    def _quarterI(self, qq):
        E, h = self.E, self.h
        QB, QW, W2 = self.QB, self.QW, self.W2
        Q2 = 2 * QW
        zr = self.b("zrecv")
        if self._qidx == 0:
            for k in range(4):
                self._qc[k] = ('v', f"fo_{qq-1}_f2_6")
            self._qidx = 4

        ncq = QW // self.C
        for bb in range(2):
            for u in range(ncq):
                c = bb * self.ncb + qq * ncq + u
                self.wm('sp', 'g', f"a2a_{c}")
                if u == 0 and bb == 0:
                    self.wm('sp', 't', f"opmm_{qq-1}_7")
                dst = vap(zr, (0, 4 + bb * QW + u * self.C),
                          [['p', 128], [W2, 8], [1, self.C]])
                E('sp', lambda c=c, dst=dst: h['sp'].dma_start(
                    out=dst, in_=self.a2ao[c].rearrange("s p t -> p s t")),
                    inc=('rc', 16))
        self.mark(f"zr_{qq}")

        for eg in range(8):
            if eg >= 2:
                self.wm('sp', 'v', f"r1_{qq}_{eg-2}")
            else:
                self.wm('sp', 'v', f"r1_{qq-1}_{6+eg}")
            src = bass.AP(self.i["xresi"].tensor,
                          128 * eg * 2 * self.JS + qq * Q2,
                          [[2 * self.JS, 128], [1, Q2]])
            E('sp', lambda src=src, eg=eg: h['sp'].dma_start(
                out=self.xresb[eg % 2][:, 0:Q2], in_=src), inc=(f"p{eg%2}", 16))
            self.mark(f"xr_{qq}_{eg}")
            base = self._stgate(extra=('rc', f"zr_{qq}"))
            for r in range(8):
                E('t', lambda eg=eg, r=r, base=base: h['t'].matmul(
                    self.ps[:, base:base + Q2],
                    self.b("wout")[:, 1024 * r + 128 * eg:1024 * r + 128 * (eg + 1)],
                    self.rl("zrecv", r), start=(r == 0), stop=(r == 7)),
                    inc=('t', 1))
            self.mark(f"opmm_{qq}_{eg}")
            self.wm('v', 't', f"opmm_{qq}_{eg}")
            self.wm('v', f"p{eg%2}", f"xr_{qq}_{eg}")
            if eg == 0:
                self.wm('v', 'out', f"out_{qq-1}")
            rr = self.b("rres")
            out3 = vap(rr, (0, eg * W2 + 4), [['p', 128], [1, 2], [2, QW]])
            xb3 = bass.AP(self.xresb[eg % 2], 0, [[W2, 128], [1, 2], [2, QW]])
            ps3 = bass.AP(self.ps, base, [[4096, 128], [QW, 2], [1, QW]])
            E('v', lambda out3=out3, xb3=xb3, ps3=ps3: h['v'].tensor_add(
                out3, xb3, ps3), inc=('v', 1))
            self.mark(f"r1_{qq}_{eg}")
            self._stdone('v', f"r1_{qq}_{eg}")

        self._lnI(qq, "l2", lambda eg: f"r1_{qq}_{eg}",
                  pb4=('s', f"lnm_l4_{qq-1}"), pb5=('v', f"lnv_l4_{qq-1}"),
                  srbuf=self.save2)
        self._ffnI(qq, "f1", "l2")
        self._lnI(qq, "l3", lambda eg: f"fo_{qq}_f1_{eg}",
                  pb4=('s', f"lnm_l2_{qq}"), pb5=('v', f"lnv_l2_{qq}"))
        self._attnbI(qq)
        self._lnI(qq, "l4", lambda eg: f"r3_{qq}_{eg}",
                  pb4=('v', f"zev_{qq}_6"), pb5=('v', f"zev_{qq}_7"),
                  srbuf=self.save4)
        self._ffnI(qq, "f2", "l4")

        for eg in range(8):
            self.wm('sp', 'v', f"fo_{qq}_f2_{eg}")
            src = self.rl("rres", eg)
            dst = bass.AP(self.o_out.tensor, 128 * eg * 2 * self.JS + qq * Q2,
                          [[2 * self.JS, 128], [1, Q2]])
            E('sp', lambda src=src, dst=dst: h['sp'].dma_start(out=dst, in_=src),
              inc=('out', 16))
        self.mark(f"out_{qq}")

    def _lnI(self, qq, tag, vdep, pb4, pb5, srbuf=None):
        E, h = self.E, self.h
        QW, W2 = self.QW, self.W2
        Q2 = 2 * QW
        rl = self.rl
        tg = f"{tag}_{qq}"
        for eg in range(8):
            self.wm('s', 'v', vdep(eg))
            E('s', lambda eg=eg: h['s'].copy(rl("tmpa", eg), rl("rres", eg)),
              inc=('s', 1))
            E('s', lambda eg=eg: h['s'].activation(rl("tmpb", eg), rl("tmpa", eg),
                                                   AF.Square), inc=('s', 1))
            self.mark(f"lncg_{tg}_{eg}")
            self.wm('t', 's', f"lncg_{tg}_{eg}")
            if eg == 0:
                self.wm('t', pb4[0], pb4[1])
                self.wm('t', pb5[0], pb5[1])
            E('t', lambda eg=eg: h['t'].matmul(
                self.ps[:, 2048:2048 + Q2], self.ones[:], rl("tmpa", eg),
                start=(eg == 0), stop=(eg == 7)), inc=('t', 1))
            E('t', lambda eg=eg: h['t'].matmul(
                self.ps[:, 2560:2560 + Q2], self.ones[:], rl("tmpb", eg),
                start=(eg == 0), stop=(eg == 7)), inc=('t', 1))
        self.mark(f"lnmm_{tg}")
        self.wm('s', 't', f"lnmm_{tg}")
        E('s', lambda: h['s'].activation(
            self.mrepB[:, 0:Q2], self.ps[:, 2048:2048 + Q2],
            AF.Copy, scale=1.0 / D), inc=('s', 1))
        E('s', lambda: h['s'].activation(self.vtmpB[:, 0:Q2], self.mrepB[:, 0:Q2],
                                         AF.Square), inc=('s', 1))
        self.mark(f"lnm_{tg}")
        self.wm('v', 's', f"lnm_{tg}")
        self.wm('v', 't', f"lnmm_{tg}")
        E('v', lambda: h['v'].scalar_tensor_tensor(
            self.vtmpB[:, 0:Q2], self.ps[:, 2560:2560 + Q2], 1.0 / D,
            self.vtmpB[:, 0:Q2], ALU.mult, ALU.subtract), inc=('v', 1))
        self.mark(f"lnv_{tg}")
        self.wm('s', 'v', f"lnv_{tg}")
        E('s', lambda: h['s'].activation(self.vtmpB[:, 0:Q2], self.vtmpB[:, 0:Q2],
                                         AF.Ln, bias=LN_EPS), inc=('s', 1))
        E('s', lambda: h['s'].activation(self.rrepB[:, 0:Q2], self.vtmpB[:, 0:Q2],
                                         AF.Exp, scale=-0.5), inc=('s', 1))
        self.mark(f"lnr_{tg}")
        self.wm('v', 's', f"lnr_{tg}")
        for eg in range(8):
            E('v', lambda eg=eg: h['v'].tensor_sub(rl("tmpa", eg), rl("rres", eg),
                                                   self.mrepB[:, 0:Q2]), inc=('v', 1))
            E('v', lambda eg=eg: h['v'].tensor_mul(rl("xn", eg), rl("tmpa", eg),
                                                   self.rrepB[:, 0:Q2]), inc=('v', 1))
            self.mark(f"lnxn_{tg}_{eg}")
        if srbuf is not None:
            xnv = self.b("xn")
            dstl = vap(xnv, (0, 2), [['p', 128], [W2, 8], [1, 2]])
            E('v', lambda dstl=dstl, srbuf=srbuf: h['v'].tensor_copy(
                dstl, srbuf[:].rearrange("p (a b) -> p a b", a=8)), inc=('v', 1))
            srcl = vap(xnv, (0, 2 + Q2 - 2), [['p', 128], [W2, 8], [1, 2]])
            E('v', lambda srcl=srcl, srbuf=srbuf: h['v'].tensor_copy(
                srbuf[:].rearrange("p (a b) -> p a b", a=8), srcl), inc=('v', 1))
        self.mark(f"ln_{tg}")

    def _ffnI(self, qq, tag, lntag):
        E, h = self.E, self.h
        QW, W2 = self.QW, self.W2
        Q2 = 2 * QW
        rl, shl = self.rl, self.shl
        tg = f"{qq}_{tag}"
        self.wm('v', 't', f"lnmm_{lntag}_{qq}")
        for eg in range(8):
            E('v', lambda eg=eg: h['v'].tensor_sub(
                rl("tmpa", eg), rl("xn", eg), shl("xn", eg)), inc=('v', 1))
            E('v', lambda eg=eg: h['v'].scalar_tensor_tensor(
                rl("tmpb", eg), rl("tmpa", eg),
                self.mur[:, eg:eg + 1], shl("xn", eg), ALU.mult, ALU.add),
                inc=('v', 1))
            E('v', lambda eg=eg: h['v'].scalar_tensor_tensor(
                rl("tmpc", eg), rl("tmpa", eg),
                self.muk[:, eg:eg + 1], shl("xn", eg), ALU.mult, ALU.add),
                inc=('v', 1))
            self.mark(f"mixe_{tg}_{eg}")
        self.mark(f"mix_{tg}")
        for eg in range(8):
            base = self._stgate()
            for i in range(8):
                self.wm('t', 'v', f"mixe_{tg}_{i}")
                E('t', lambda eg=eg, i=i, base=base: h['t'].matmul(
                    self.ps[:, base:base + Q2],
                    self.b("wr")[:, 1024 * i + 128 * eg:1024 * i + 128 * (eg + 1)],
                    rl("tmpb", i), start=(i == 0), stop=(i == 7)), inc=('t', 1))
            self.mark(f"rmm_{tg}_{eg}")
            self.wm('s', 't', f"rmm_{tg}_{eg}")
            E('s', lambda eg=eg, base=base: h['s'].activation(
                rl("tmpa", eg), self.ps[:, base:base + Q2],
                AF.Sigmoid), inc=('s', 1))
            self.mark(f"rg_{tg}_{eg}")
            self._stdone('s', f"rg_{tg}_{eg}")
        for eg in range(8):
            base = self._stgate()
            for i in range(8):
                self.wm('t', 'v', f"mixe_{tg}_{i}")
                E('t', lambda eg=eg, i=i, base=base: h['t'].matmul(
                    self.ps[:, base:base + Q2],
                    self.b("wk")[:, 1024 * i + 128 * eg:1024 * i + 128 * (eg + 1)],
                    rl("tmpc", i), start=(i == 0), stop=(i == 7)), inc=('t', 1))
            self.mark(f"kmm_{tg}_{eg}")
            self.wm('v', 't', f"kmm_{tg}_{eg}")
            if eg == 0:
                self.wm('v', 't', f"rmm_{tg}_7")
            E('v', lambda eg=eg, base=base: h['v'].tensor_scalar_max(
                rl("tmpb", eg), self.ps[:, base:base + Q2], 0.0), inc=('v', 1))
            self.mark(f"kr_{tg}_{eg}")
            self.wm('s', 'v', f"kr_{tg}_{eg}")
            E('s', lambda eg=eg: h['s'].activation(rl("xn", eg), rl("tmpb", eg),
                                                   AF.Square), inc=('s', 1))
            self.mark(f"k2_{tg}_{eg}")
            self._stdone('v', f"kr_{tg}_{eg}")
        for eg in range(8):
            base = self._stgate()
            for i in range(8):
                self.wm('t', 's', f"k2_{tg}_{i}")
                E('t', lambda eg=eg, i=i, base=base: h['t'].matmul(
                    self.ps[:, base:base + Q2],
                    self.b("wv")[:, 1024 * i + 128 * eg:1024 * i + 128 * (eg + 1)],
                    rl("xn", i), start=(i == 0), stop=(i == 7)), inc=('t', 1))
            self.mark(f"vmm_{tg}_{eg}")
            self.wm('v', 't', f"vmm_{tg}_{eg}")
            if eg == 0:
                self.wm('v', 't', f"kmm_{tg}_7")
            E('v', lambda eg=eg, base=base: h['v'].tensor_mul(
                rl("tmpc", eg), rl("tmpa", eg),
                self.ps[:, base:base + Q2]), inc=('v', 1))
            E('v', lambda eg=eg: h['v'].tensor_add(rl("rres", eg), rl("rres", eg),
                                                   rl("tmpc", eg)), inc=('v', 1))
            self.mark(f"fo_{qq}_{tag}_{eg}")
            self._stdone('v', f"fo_{qq}_{tag}_{eg}")

    def _attnbI(self, qq):
        E, h = self.E, self.h
        QW, W2 = self.QW, self.W2
        Q2 = 2 * QW
        rl, rl2 = self.rl, self.rl2
        for eg in range(8):
            base = self._stgate()
            for i in range(8):
                self.wm('t', 'v', f"lnxn_l3_{qq}_{i}")
                E('t', lambda eg=eg, i=i, base=base: h['t'].matmul(
                    self.ps[:, base:base + Q2],
                    self.b("qwb")[:, 1024 * i + 128 * eg:1024 * i + 128 * (eg + 1)],
                    rl("xn", i), start=(i == 0), stop=(i == 7)), inc=('t', 1))
            self.mark(f"qmm_{qq}_{eg}")
            self.wm('v', 't', f"qmm_{qq}_{eg}")
            E('v', lambda eg=eg, base=base: h['v'].tensor_copy(
                rl("tmpa", eg), self.ps[:, base:base + Q2]), inc=('v', 1))
            self.mark(f"qev_{qq}_{eg}")
            self.wm('s', 'v', f"qev_{qq}_{eg}")
            E('s', lambda eg=eg: h['s'].activation(rl("tmpb", eg), rl("tmpa", eg),
                                                   AF.Square), inc=('s', 1))
            self.mark(f"q2_{qq}_{eg}")
            self._stdone('v', f"qev_{qq}_{eg}")
        for eg in range(8):
            self.wm('t', 's', f"q2_{qq}_{eg}")
            if eg == 0:
                self.wm('t', 'v', f"mx_{qq-1}_7")
            E('t', lambda eg=eg: h['t'].matmul(
                self.ps[0:16, 3072:3072 + Q2],
                self.hsel[:, 16 * eg:16 * (eg + 1)], rl("tmpb", eg),
                start=(eg == 0), stop=(eg == 7)), inc=('t', 1))
        self.mark(f"qsmm_{qq}")
        self.wm('v', 't', f"qsmm_{qq}")
        for eg in range(8):
            E('v', lambda eg=eg: h['v'].tensor_mul(
                rl2("tmpb", eg, 0), rl2("tmpa", eg, 0), rl2("tmpa", eg, 1)),
                inc=('v', 1))
        E('v', lambda: h['v'].tensor_copy(
            self.qsq[:, 0:QW], bass.AP(self.ps, 3072, [[4096, 16], [2, QW]])),
            inc=('v', 1))
        E('v', lambda: h['v'].scalar_tensor_tensor(
            self.qsq[:, 0:QW], bass.AP(self.ps, 3073, [[4096, 16], [2, QW]]),
            1.0, self.qsq[:, 0:QW], ALU.mult, ALU.add), inc=('v', 1))
        E('v', lambda: h['v'].tensor_scalar_mul(self.qsq[:, 0:QW],
                                                self.qsq[:, 0:QW], 0.125),
          inc=('v', 1))
        self.mark(f"qd_{qq}")
        self.wm('t', 'v', f"qd_{qq}")
        self.wm('t', 'v', f"gv_{qq-1}")
        for eg in range(8):
            E('t', lambda eg=eg: h['t'].matmul(
                self.ps[0:16, 3584:3584 + QW],
                self.hsel[:, 16 * eg:16 * (eg + 1)], rl2("tmpb", eg, 0),
                start=(eg == 0), stop=(eg == 7)), inc=('t', 1))
        self.mark(f"qdmm_{qq}")
        self.wm('v', 't', f"qdmm_{qq}")
        E('v', lambda: h['v'].scalar_tensor_tensor(
            self.sigg[:, 0:QW], self.ps[0:16, 3584:3584 + QW], -0.25,
            self.qsq[:, 0:QW], ALU.mult, ALU.add), inc=('v', 1))
        self.mark(f"gv_{qq}")
        self.wm('s', 'v', f"gv_{qq}")
        E('s', lambda: h['s'].activation(self.sigb[:, 0:QW], self.sigg[:, 0:QW],
                                         AF.Sigmoid), inc=('s', 1))
        self.mark(f"sg_{qq}")
        for eg in range(8):
            if eg == 0:
                self.wm('t', 'v', f"qd_{qq}")
                self.wm('t', 's', f"lnm_l3_{qq}")
                self.wm('t', 'v', f"lnv_l3_{qq}")
            else:
                self.wm('t', 'v', f"zev_{qq}_{eg-1}")
            zb = 2048 + 512 * (eg % 2)
            for hh in (2 * eg, 2 * eg + 1):
                po, pc = 64 * (hh % 2), 64 * (hh // 2)
                E('t', lambda hh=hh, po=po, pc=pc, eg=eg, zb=zb: h['t'].matmul(
                    self.ps[po:po + 64, zb:zb + Q2],
                    self.mmat[po:po + 64, pc:pc + 64],
                    self.rl("tmpa", eg)[po:po + 64, :],
                    start=True, stop=True), inc=('t', 1))
            self.mark(f"zmm_{qq}_{eg}")
            self.wm('v', 't', f"zmm_{qq}_{eg}")
            if eg == 0:
                self.wm('v', 't', f"qdmm_{qq}")
            E('v', lambda eg=eg, zb=zb: h['v'].tensor_copy(
                rl("tmpb", eg), self.ps[:, zb:zb + Q2]), inc=('v', 1))
            self.mark(f"zev_{qq}_{eg}")
        for eg in range(8):
            self.wm('t', 'v', f"zev_{qq}_{eg}")
            self.wm('t', 's', f"sg_{qq}")
            if eg >= 1:
                self.wm('t', 'v', f"mx_{qq}_{eg-1}")
            elif qq >= 2:
                self.wm('t', 'v', f"mx_{qq-1}_7")
            E('t', lambda eg=eg: h['t'].matmul(
                self.ps[:, 3840:3840 + QW],
                self.ssel[:, 128 * eg:128 * (eg + 1)], self.sigb[:, 0:QW],
                start=True, stop=True), inc=('t', 1))
            self.mark(f"smm_{qq}_{eg}")
            self.wm('v', 't', f"smm_{qq}_{eg}")
            E('v', lambda eg=eg: h['v'].tensor_sub(
                rl2("tmpa", eg, 0), rl2("tmpb", eg, 0), rl2("tmpb", eg, 1)),
                inc=('v', 1))
            E('v', lambda eg=eg: h['v'].tensor_mul(
                rl2("tmpa", eg, 0), self.ps[:, 3840:3840 + QW],
                rl2("tmpa", eg, 0)), inc=('v', 1))
            E('v', lambda eg=eg: h['v'].tensor_sub(
                rl2("tmpc", eg, 1), rl2("tmpb", eg, 0), rl2("tmpa", eg, 0)),
                inc=('v', 1))
            E('v', lambda eg=eg: h['v'].tensor_add(
                rl2("tmpc", eg, 0), rl2("tmpb", eg, 1), rl2("tmpa", eg, 0)),
                inc=('v', 1))
            self.mark(f"mx_{qq}_{eg}")
        for eg in range(8):
            base = self._stgate()
            for i in range(8):
                self.wm('t', 'v', f"mx_{qq}_{i}")
                E('t', lambda eg=eg, i=i, base=base: h['t'].matmul(
                    self.ps[:, base:base + Q2],
                    self.b("owb")[:, 1024 * i + 128 * eg:1024 * i + 128 * (eg + 1)],
                    rl("tmpc", i), start=(i == 0), stop=(i == 7)), inc=('t', 1))
            self.mark(f"owmm_{qq}_{eg}")
            self.wm('v', 't', f"owmm_{qq}_{eg}")
            E('v', lambda eg=eg, base=base: h['v'].scalar_tensor_tensor(
                self.rl("rres", eg), self.ps[:, base:base + Q2],
                self.outbc[:, eg:eg + 1],
                self.rl("rres", eg), ALU.add, ALU.add), inc=('v', 1))
            self.mark(f"r3_{qq}_{eg}")
            self._stdone('v', f"r3_{qq}_{eg}")

    def _layer_norm(self, qq, tag, vdep):
        E, h = self.E, self.h
        QB, W2 = self.QB, self.W2
        lv, lv2 = self.lv, self.lv2
        vap_ = vap
        tg = f"{tag}_{qq}"
        self.wm('s', 'v', vdep)
        for eg in range(8):
            E('s', lambda eg=eg: h['s'].copy(lv("tmpa", eg), lv("rres", eg)), inc=('s', 1))
            self.sw('s')
            E('s', lambda eg=eg: h['s'].activation(lv("tmpb", eg), lv("tmpa", eg),
                                                   AF.Square), inc=('s', 1))
        self.mark(f"lnc_{tg}")
        self.wm('t', 's', f"lnc_{tg}")
        for eg in range(8):
            for bb in range(2):
                for c0, cw in self.CHK:
                    E('t', lambda eg=eg, bb=bb, c0=c0, cw=cw: h['t'].matmul(
                        self.ps[:, 2048 + bb * 1024 + c0:2048 + bb * 1024 + c0 + cw],
                        self.ones[:], lv("tmpa", eg, bb)[:, c0:c0 + cw],
                        start=(eg == 0), stop=(eg == 7)),
                        inc=('t', 1))
        self.mark(f"lnmmx_{tg}")
        self.wm('s', 't', f"lnmmx_{tg}")
        E('s', lambda: h['s'].activation(
            vap_(self.mrepB[:], (0, 0), [['p', 128], [QB, 2], [1, QB]]),
            self.psv(2048), AF.Copy, scale=1.0 / D), inc=('s', 1))
        self.sw('s')
        E('s', lambda: h['s'].activation(self.vtmpB[:], self.mrepB[:], AF.Square),
          inc=('s', 1))
        self.mark(f"lnm_{tg}")
        self.wm('t', 's', f"lnm_{tg}")
        for eg in range(8):
            for bb in range(2):
                for c0, cw in self.CHK:
                    E('t', lambda eg=eg, bb=bb, c0=c0, cw=cw: h['t'].matmul(
                        self.ps[:, 2048 + bb * 1024 + c0:2048 + bb * 1024 + c0 + cw],
                        self.ones[:], lv("tmpb", eg, bb)[:, c0:c0 + cw],
                        start=(eg == 0), stop=(eg == 7)),
                        inc=('t', 1))
        self.mark(f"lnmm_{tg}")
        self.wm('v', 's', f"lnm_{tg}")
        self.wm('v', 't', f"lnmm_{tg}")
        E('v', lambda: h['v'].scalar_tensor_tensor(
            vap_(self.vtmpB[:], (0, 0), [['p', 128], [QB, 2], [1, QB]]),
            self.psv(2048), 1.0 / D,
            vap_(self.vtmpB[:], (0, 0), [['p', 128], [QB, 2], [1, QB]]),
            ALU.mult, ALU.subtract), inc=('v', 1))
        self.mark(f"lnv_{tg}")
        self.wm('s', 'v', f"lnv_{tg}")
        E('s', lambda: h['s'].activation(self.vtmpB[:], self.vtmpB[:], AF.Ln,
                                         bias=LN_EPS), inc=('s', 1))
        self.sw('s')
        E('s', lambda: h['s'].activation(self.rrepB[:], self.vtmpB[:], AF.Exp,
                                         scale=-0.5), inc=('s', 1))
        self.mark(f"lnr_{tg}")
        self.wm('v', 's', f"lnr_{tg}")
        for eg in range(8):
            E('v', lambda eg=eg: h['v'].tensor_sub(lv("tmpa", eg), lv("rres", eg),
                                                   self.mrepB[:]), inc=('v', 1))
            E('v', lambda eg=eg: h['v'].tensor_mul(lv("xn", eg), lv("tmpa", eg),
                                                   self.rrepB[:]), inc=('v', 1))
        xnv = self.b("xn")
        leadv = vap_(xnv, (0, 0), [['p', 128], [W2, 8], [QB, 2], [1, 2]])
        E('v', lambda leadv=leadv: h['v'].tensor_scalar_mul(leadv, leadv,
                                                            self.maskc[:, 0:1]),
          inc=('v', 1))
        self.mark(f"ln_{tg}")

    def _ffn(self, qq, tag, lntag):
        E, h = self.E, self.h
        QB, W2 = self.QB, self.W2
        lv, lv2 = self.lv, self.lv2
        tg = f"{qq}_{tag}"
        self.wm('v', 't', f"lnmm_{lntag}_{qq}")
        for eg in range(8):
            for bb in range(2):
                xn = lv("xn", eg, bb)
                E('v', lambda xn=xn, eg=eg, bb=bb: h['v'].tensor_sub(
                    self.lv("tmpa", eg, bb)[:, 1:QB], xn[:, 1:QB], xn[:, 0:QB - 1]),
                    inc=('v', 1))
                E('v', lambda xn=xn, eg=eg, bb=bb: h['v'].scalar_tensor_tensor(
                    self.lv("tmpb", eg, bb)[:, 1:QB], self.lv("tmpa", eg, bb)[:, 1:QB],
                    self.mur[:, eg:eg + 1], xn[:, 0:QB - 1], ALU.mult, ALU.add),
                    inc=('v', 1))
                E('v', lambda xn=xn, eg=eg, bb=bb: h['v'].scalar_tensor_tensor(
                    self.lv("tmpc", eg, bb)[:, 1:QB], self.lv("tmpa", eg, bb)[:, 1:QB],
                    self.muk[:, eg:eg + 1], xn[:, 0:QB - 1], ALU.mult, ALU.add),
                    inc=('v', 1))
        self.mark(f"mix_{tg}")
        for eg in range(8):
            self.wm('t', 'v', f"mix_{tg}")
            if eg >= 2:
                self.wm('t', 's', f"rg_{tg}_{eg-2}")
            base = 2048 * (eg % 2)
            for i in range(8):
                for bb in range(2):
                    for c0, cw in self.CHK:
                        E('t', lambda eg=eg, i=i, bb=bb, c0=c0, cw=cw, base=base:
                            h['t'].matmul(
                                self.ps[:, base + bb * 1024 + c0:base + bb * 1024 + c0 + cw],
                                self.b("wr")[:, 1024 * i + 128 * eg:1024 * i + 128 * (eg + 1)],
                                lv("tmpb", i, bb)[:, c0:c0 + cw], start=(i == 0), stop=(i == 7)),
                            inc=('t', 1))
            self.mark(f"rmm_{tg}_{eg}")
            self.wm('s', 't', f"rmm_{tg}_{eg}")
            E('s', lambda eg=eg, base=base: h['s'].activation(lv2("tmpa", eg),
                                                              self.psv(base),
                                                              AF.Sigmoid), inc=('s', 1))
            self.mark(f"rg_{tg}_{eg}")
        for eg in range(8):
            self.wm('t', 's', f"rg_{tg}_7")
            if eg >= 2:
                self.wm('t', 'v', f"kr_{tg}_{eg-2}")
            base = 2048 * (eg % 2)
            for i in range(8):
                for bb in range(2):
                    for c0, cw in self.CHK:
                        E('t', lambda eg=eg, i=i, bb=bb, c0=c0, cw=cw, base=base:
                            h['t'].matmul(
                                self.ps[:, base + bb * 1024 + c0:base + bb * 1024 + c0 + cw],
                                self.b("wk")[:, 1024 * i + 128 * eg:1024 * i + 128 * (eg + 1)],
                                lv("tmpc", i, bb)[:, c0:c0 + cw], start=(i == 0), stop=(i == 7)),
                            inc=('t', 1))
            self.mark(f"kmm_{tg}_{eg}")
            self.wm('v', 't', f"kmm_{tg}_{eg}")
            E('v', lambda eg=eg, base=base: h['v'].tensor_scalar_max(lv2("tmpb", eg),
                                                                     self.psv(base), 0.0),
              inc=('v', 1))
            self.mark(f"kr_{tg}_{eg}")
            self.wm('s', 'v', f"kr_{tg}_{eg}")
            E('s', lambda eg=eg: h['s'].activation(lv("xn", eg), lv("tmpb", eg),
                                                   AF.Square), inc=('s', 1))
            self.mark(f"k2_{tg}_{eg}")
        for eg in range(8):
            self.wm('t', 's', f"k2_{tg}_7")
            if eg >= 2:
                self.wm('t', 'v', f"fo_{qq}_{tag}_{eg-2}")
            base = 2048 * (eg % 2)
            for i in range(8):
                for bb in range(2):
                    for c0, cw in self.CHK:
                        E('t', lambda eg=eg, i=i, bb=bb, c0=c0, cw=cw, base=base:
                            h['t'].matmul(
                                self.ps[:, base + bb * 1024 + c0:base + bb * 1024 + c0 + cw],
                                self.b("wv")[:, 1024 * i + 128 * eg:1024 * i + 128 * (eg + 1)],
                                lv("xn", i, bb)[:, c0:c0 + cw], start=(i == 0), stop=(i == 7)),
                            inc=('t', 1))
            self.mark(f"vmm_{tg}_{eg}")
            self.wm('v', 't', f"vmm_{tg}_{eg}")
            E('v', lambda eg=eg, base=base: h['v'].tensor_mul(lv2("tmpc", eg),
                                                              lv2("tmpa", eg),
                                                              self.psv(base)), inc=('v', 1))
            self.sw('v')
            E('v', lambda eg=eg: h['v'].tensor_add(lv("rres", eg), lv("rres", eg),
                                                   lv("tmpc", eg)), inc=('v', 1))
            self.mark(f"fo_{qq}_{tag}_{eg}")

    def _attnb(self, qq):
        E, h = self.E, self.h
        QB, W2 = self.QB, self.W2
        lv, lv2 = self.lv, self.lv2
        for eg in range(8):
            self.wm('t', 'v', f"ln_l3_{qq}")
            if eg >= 2:
                self.wm('t', 'v', f"qev_{qq}_{eg-2}")
            base = 2048 * (eg % 2)
            for i in range(8):
                for bb in range(2):
                    for c0, cw in self.CHK:
                        E('t', lambda eg=eg, i=i, bb=bb, c0=c0, cw=cw, base=base:
                            h['t'].matmul(
                                self.ps[:, base + bb * 1024 + c0:base + bb * 1024 + c0 + cw],
                                self.b("qwb")[:, 1024 * i + 128 * eg:1024 * i + 128 * (eg + 1)],
                                lv("xn", i, bb)[:, c0:c0 + cw], start=(i == 0), stop=(i == 7)),
                            inc=('t', 1))
            self.mark(f"qmm_{qq}_{eg}")
            self.wm('v', 't', f"qmm_{qq}_{eg}")
            E('v', lambda eg=eg, base=base: h['v'].tensor_copy(lv2("tmpa", eg),
                                                               self.psv(base)),
              inc=('v', 1))
            self.mark(f"qev_{qq}_{eg}")
            self.wm('s', 'v', f"qev_{qq}_{eg}")
            E('s', lambda eg=eg: h['s'].activation(lv("tmpb", eg), lv("tmpa", eg),
                                                   AF.Square), inc=('s', 1))
            self.mark(f"q2_{qq}_{eg}")
        self.wm('t', 's', f"q2_{qq}_7")
        for eg in range(8):
            for bb in range(2):
                for c0, cw in self.CHK:
                    E('t', lambda eg=eg, bb=bb, c0=c0, cw=cw: h['t'].matmul(
                        self.ps[0:16, 2048 + bb * 1024 + c0:2048 + bb * 1024 + c0 + cw],
                        self.hsel[:, 16 * eg:16 * (eg + 1)], lv("tmpb", eg, bb)[:, c0:c0 + cw],
                        start=(eg == 0), stop=(eg == 7)),
                        inc=('t', 1))
        self.mark(f"qsmm_{qq}")
        self.wm('v', 't', f"qsmm_{qq}")
        for eg in range(8):
            E('v', lambda eg=eg: h['v'].tensor_mul(lv("tmpb", eg, 0), lv("tmpa", eg, 0),
                                                   lv("tmpa", eg, 1)), inc=('v', 1))
        E('v', lambda: h['v'].tensor_copy(self.qsq[:], self.ps[0:16, 2048:2048 + QB]),
          inc=('v', 1))
        E('v', lambda: h['v'].scalar_tensor_tensor(
            self.qsq[:], self.ps[0:16, 3072:3072 + QB], 1.0, self.qsq[:],
            ALU.mult, ALU.add), inc=('v', 1))
        E('v', lambda: h['v'].tensor_scalar_mul(self.qsq[:], self.qsq[:], 0.125),
          inc=('v', 1))
        self.mark(f"qd_{qq}")
        self.wm('t', 'v', f"qd_{qq}")
        for eg in range(8):
            for c0, cw in self.CHK:
                E('t', lambda eg=eg, c0=c0, cw=cw: h['t'].matmul(
                    self.ps[0:16, 2048 + c0:2048 + c0 + cw],
                    self.hsel[:, 16 * eg:16 * (eg + 1)], lv("tmpb", eg, 0)[:, c0:c0 + cw],
                    start=(eg == 0), stop=(eg == 7)),
                    inc=('t', 1))
        self.mark(f"qdmm_{qq}")
        self.wm('v', 't', f"qdmm_{qq}")
        E('v', lambda: h['v'].scalar_tensor_tensor(
            self.sigg[:], self.ps[0:16, 2048:2048 + QB], -0.25,
            self.qsq[:], ALU.mult, ALU.add), inc=('v', 1))
        self.mark(f"gv_{qq}")
        self.wm('s', 'v', f"gv_{qq}")
        E('s', lambda: h['s'].activation(self.sigb[:], self.sigg[:], AF.Sigmoid),
          inc=('s', 1))
        self.mark(f"sg_{qq}")
        for eg in range(8):
            if eg == 0:
                self.wm('t', 'v', f"qd_{qq}")
            else:
                self.wm('t', 'v', f"zev_{qq}_{eg-1}")
            for hh in (2 * eg, 2 * eg + 1):
                po, pc = 64 * (hh % 2), 64 * (hh // 2)
                for bb in range(2):
                    for c0, cw in self.CHK:
                        E('t', lambda eg=eg, hh=hh, po=po, pc=pc, bb=bb, c0=c0, cw=cw:
                            h['t'].matmul(
                                self.ps[po:po + 64, bb * 1024 + c0:bb * 1024 + c0 + cw],
                                self.mmat[po:po + 64, pc:pc + 64],
                                lv("tmpa", eg, bb)[po:po + 64, c0:c0 + cw],
                                start=True, stop=True),
                            inc=('t', 1))
            self.mark(f"zmm_{qq}_{eg}")
            self.wm('v', 't', f"zmm_{qq}_{eg}")
            E('v', lambda eg=eg: h['v'].tensor_copy(lv2("tmpb", eg), self.psv(0)),
              inc=('v', 1))
            self.mark(f"zev_{qq}_{eg}")
        for eg in range(8):
            self.wm('t', 'v', f"zev_{qq}_{eg}")
            self.wm('t', 's', f"sg_{qq}")
            if eg >= 1:
                self.wm('t', 'v', f"mx_{qq}_{eg-1}")
            for c0, cw in self.CHK:
                E('t', lambda eg=eg, c0=c0, cw=cw: h['t'].matmul(
                    self.ps[:, 3072 + c0:3072 + c0 + cw],
                    self.ssel[:, 128 * eg:128 * (eg + 1)], self.sigb[:, c0:c0 + cw],
                    start=True, stop=True), inc=('t', 1))
            self.mark(f"smm_{qq}_{eg}")
            self.wm('v', 't', f"smm_{qq}_{eg}")
            E('v', lambda eg=eg: h['v'].tensor_sub(lv("tmpa", eg, 0), lv("tmpb", eg, 0),
                                                   lv("tmpb", eg, 1)), inc=('v', 1))
            E('v', lambda eg=eg: h['v'].tensor_mul(lv("tmpa", eg, 0),
                                                   self.ps[:, 3072:3072 + QB],
                                                   lv("tmpa", eg, 0)), inc=('v', 1))
            E('v', lambda eg=eg: h['v'].tensor_sub(lv("tmpc", eg, 1), lv("tmpb", eg, 0),
                                                   lv("tmpa", eg, 0)), inc=('v', 1))
            E('v', lambda eg=eg: h['v'].tensor_add(lv("tmpc", eg, 0), lv("tmpb", eg, 1),
                                                   lv("tmpa", eg, 0)), inc=('v', 1))
            self.mark(f"mx_{qq}_{eg}")
        for eg in range(8):
            self.wm('t', 'v', f"mx_{qq}_7")
            if eg >= 2:
                self.wm('t', 'v', f"r3_{qq}_{eg-2}")
            base = 2048 * (eg % 2)
            for i in range(8):
                for bb in range(2):
                    for c0, cw in self.CHK:
                        E('t', lambda eg=eg, i=i, bb=bb, c0=c0, cw=cw, base=base:
                            h['t'].matmul(
                                self.ps[:, base + bb * 1024 + c0:base + bb * 1024 + c0 + cw],
                                self.b("owb")[:, 1024 * i + 128 * eg:1024 * i + 128 * (eg + 1)],
                                lv("tmpc", i, bb)[:, c0:c0 + cw], start=(i == 0), stop=(i == 7)),
                            inc=('t', 1))
            self.mark(f"owmm_{qq}_{eg}")
            self.wm('v', 't', f"owmm_{qq}_{eg}")
            E('v', lambda eg=eg, base=base: h['v'].scalar_tensor_tensor(
                lv2("rres", eg), self.psv(base), self.outbc[:, eg:eg + 1],
                lv2("rres", eg), ALU.add, ALU.add), inc=('v', 1))
            self.mark(f"r3_{qq}_{eg}")

    def _finish(self):
        self.W('sp', 'out', self.cnt['out'])


def _prep_core_inputs(k, L, inp):
    JS = L // NC
    TB = JS + 2
    x = np.asarray(inp["L0_attn_a_x"], np.float32)
    xT = np.ascontiguousarray(x.transpose(2, 0, 1).reshape(D, 2 * L))
    xr = np.zeros((D, 2, TB), np.float32)
    j0 = JS * k
    xr[:, :, 2:] = x[:, j0:j0 + JS, :].transpose(2, 0, 1)
    if k > 0:
        xr[:, :, 0:2] = x[:, j0 - 2:j0, :].transpose(2, 0, 1)
    xri = np.ascontiguousarray(
        x[:, j0:j0 + JS, :].transpose(2, 1, 0).reshape(D, 2 * JS))
    W_in = np.asarray(inp["L0_attn_a_W_in"], np.float32)
    cols = list(range(128 * k, 128 * (k + 1))) + \
           list(range(D + 128 * k, D + 128 * (k + 1))) + \
           list(range(2 * D, 2 * D + 64))
    winT = np.ascontiguousarray(W_in.T[:, cols])
    alog = np.asarray(inp["L0_attn_a_A_log"], np.float32)
    alogp = np.zeros((DS, NST), np.float32)
    for ch in range(DS):
        n_ = ch % 32
        dg = ch // 32
        for t in range(NST):
            alogp[ch, t] = alog[n_, 128 * k + 4 * t + dg]
    d = {
        "xT": xT, "xres": np.ascontiguousarray(xr.reshape(D, 2 * TB)),
        "xresi": xri,
        "winT": winT,
        "woutT": np.ascontiguousarray(np.asarray(inp["L0_attn_a_W_out"], np.float32).T),
        "wrT": np.ascontiguousarray(np.asarray(inp["L0_ffn_W_r"], np.float32).T),
        "wkT": np.ascontiguousarray(np.asarray(inp["L0_ffn_W_k"], np.float32).T),
        "wvT": np.ascontiguousarray(np.asarray(inp["L0_ffn_W_v"], np.float32).T),
        "qw": np.ascontiguousarray(
            np.asarray(inp["L0_attn_b_q_weight"], np.float32).reshape(D, D)),
        "vw": np.ascontiguousarray(
            np.asarray(inp["L0_attn_b_v_weight"], np.float32).reshape(D, D)),
        "owT": np.ascontiguousarray(np.asarray(inp["L0_attn_b_out_w"], np.float32).T),
        "outb": np.ascontiguousarray(
            np.asarray(inp["L0_attn_b_out_b"], np.float32).reshape(8, 128).T),
        "mur": np.ascontiguousarray(np.asarray(inp["L0_ffn_mu_r"], np.float32).reshape(8, 128).T),
        "muk": np.ascontiguousarray(np.asarray(inp["L0_ffn_mu_k"], np.float32).reshape(8, 128).T),
        "alogp": alogp,
        "dcol": np.ascontiguousarray(
            np.asarray(inp["L0_attn_a_D_param"], np.float32)[128 * k:128 * (k + 1)].reshape(DS, 1)),
        "convw": np.ascontiguousarray(
            np.asarray(inp["L0_attn_a_conv_w"], np.float32)[128 * k:128 * (k + 1), 0, :]),
        "convb": np.ascontiguousarray(
            np.asarray(inp["L0_attn_a_conv_b"], np.float32)[128 * k:128 * (k + 1)].reshape(DS, 1)),
        "leadmask": np.full((DS, 1), 0.0 if k == 0 else 1.0, np.float32),
    }
    ones = np.ones((128, 128), BF)
    ysel = np.zeros((128, 32 * 128), BF)
    for t in range(32):
        for ch in range(128):
            ysel[ch, 128 * t + 4 * t + ch // 32] = 1
    hsel = np.zeros((128, 8 * 16), BF)
    for eg in range(8):
        for p in range(128):
            hsel[p, 16 * eg + 2 * eg + p // 64] = 1
    ssel = np.zeros((16, 8 * 128), BF)
    for eg in range(8):
        for p in range(128):
            ssel[2 * eg + p // 64, 128 * eg + p] = 1
    dsel = np.zeros((128, 32 * 128), np.float16)
    for t in range(32):
        for ch in range(128):
            dsel[4 * t + ch // 32, 128 * t + ch] = 1
    d.update({"ones128": ones, "ysel": ysel, "hsel": hsel, "ssel": ssel,
              "dsel": dsel})
    return d


def assemble_output(results, L):
    JS = L // NC
    QW = min(256, JS)
    NQ = JS // QW
    out = np.zeros((2, L, D), np.float32)
    for k in range(NC):
        o = np.asarray(results[k]["out"]).reshape(D, NQ, 2 * QW)
        for qq in range(NQ):
            blk = o[:, qq, :]
            if qq == 0:
                b = blk.reshape(D, 2, QW)
                out[:, JS * k:JS * k + QW, :] = b.transpose(1, 2, 0)
            else:
                b = blk.reshape(D, QW, 2)
                out[:, JS * k + qq * QW:JS * k + (qq + 1) * QW, :] = \
                    b.transpose(2, 1, 0)
    return out


def kernel(**inputs):
    L = inputs["L0_attn_a_x"].shape[1]
    kb = KB(L=L, SUPW=min(2048, L))
    nc = kb.build()
    in_maps = [_prep_core_inputs(k, L, inputs) for k in range(NC)]
    res = run_bass_kernel_spmd(nc, in_maps, core_ids=list(range(NC)))
    return assemble_output(res.results, L)


# version 18
# speedup vs baseline: 1.0348x; 1.0348x over previous
import contextlib
import numpy as np
import ml_dtypes

import concourse.bass as bass
import concourse.mybir as mybir
from concourse.bass_utils import run_bass_kernel_spmd
from concourse.aot_env import allow_ici_all_to_all

F32 = mybir.dt.float32
BF16 = mybir.dt.bfloat16
F16 = mybir.dt.float16
U8 = mybir.dt.uint8
AF = mybir.ActivationFunctionType
ALU = mybir.AluOpType

NC = 8
D = 1024
DS = 128
NST = 32
LN_EPS = 1e-5
BF = np.dtype(ml_dtypes.bfloat16)


def vap(view, roff, shape_steps):
    rs = view.ap[0][0]
    dims = [[rs if s == 'p' else s, c] for s, c in shape_steps]
    return bass.AP(view.tensor, view.offset + roff[0] * rs + roff[1], dims)


class KB:
    def __init__(self, L=8192, SUPW=2048, dbg=0):
        assert L % SUPW == 0
        self.L = L
        self.SUPW = SUPW
        self.nsb = L // SUPW
        self.nsup = 2 * self.nsb
        self.JS = L // NC
        self.C = min(256, self.JS)
        self.ncb = self.JS // self.C
        self.ncall = 2 * self.ncb
        self.QW = self.C
        self.NQ = self.JS // self.QW
        self.QB = self.QW + 2
        self.W2 = 2 * self.QB
        self.CHK = [(c, min(512, self.QB - c)) for c in range(0, self.QB, 512)]
        self.PB = (0, 1024, 2048, 3072)
        self.CW = min(512, SUPW)
        self.NCH = SUPW // self.CW
        self.CW2 = min(1024, SUPW)
        self.NCH2 = SUPW // self.CW2
        self.dbg = dbg
        self.marks = {}
        self._gc = {}

    def E(self, eng, fn, inc=None):
        if eng in ("v", "s", "g") and self.eng == eng and self.cnt[eng] > 0:
            self.h[eng].wait_ge(self.sem[eng], self.cnt[eng])
        ins = fn() if self.eng == eng else None
        if inc is not None:
            s, n = inc
            if ins is not None:
                ins.then_inc(self.sem[s], n)
            self.cnt[s] += n
        return ins

    def W(self, eng, s, val):
        if self.eng == eng and val > 0:
            self.h[self.eng].wait_ge(self.sem[s], val)

    def mark(self, key):
        if self.pass0:
            assert key not in self.marks, key
            self.marks[key] = dict(self.cnt)

    def wm(self, eng, s, key):
        self.W(eng, s, self.marks[key][s])

    def sw(self, eng):
        self.W(eng, eng, self.cnt[eng])

    def build(self):
        nc = bass.Bass("TRN2", target_bir_lowering=False, debug=False)
        self.nc = nc
        L = self.L
        dt = nc.dram_tensor
        TB = self.JS + 2
        self.i = {}
        for name, shape in [
            ("xT", [D, 2 * L]), ("xres", [D, 2 * TB]), ("xresi", [D, 2 * self.JS]),
            ("winT", [D, 320]),
            ("woutT", [D, D]), ("wrT", [D, D]), ("wkT", [D, D]), ("wvT", [D, D]),
            ("qw", [D, D]), ("vw", [D, D]), ("owT", [D, D]),
            ("outb", [128, 8]), ("mur", [128, 8]), ("muk", [128, 8]),
            ("alogp", [DS, NST]), ("dcol", [DS, 1]), ("convw", [DS, 4]),
            ("convb", [DS, 1]), ("leadmask", [DS, 1]),
        ]:
            self.i[name] = dt(name, shape, F32, kind="ExternalInput").ap()
        for name, shape in [
            ("ones128", [128, 128]), ("ysel", [128, 32 * 128]),
            ("hsel", [128, 8 * 16]), ("ssel", [16, 8 * 128]),
        ]:
            self.i[name] = dt(name, shape, BF16, kind="ExternalInput").ap()
        self.i["dsel"] = dt("dsel", [128, 32 * 128], F16, kind="ExternalInput").ap()
        self.o_out = dt("out", [D, 2 * self.JS], F32, kind="ExternalOutput").ap()
        if self.dbg:
            self.o_z = dt("dbg_z", [DS, 2 * L], F32, kind="ExternalOutput").ap()
        self.a2ai = [dt(f"a2ai{c}", [NC, DS, self.C], BF16, kind="Internal").ap()
                     for c in range(self.ncall)]
        self.a2ao = [dt(f"a2ao{c}", [NC, DS, self.C], BF16, kind="Internal").ap()
                     for c in range(self.ncall)]
        self.leadi = dt("leadi", [NC, DS, 4], BF16, kind="Internal").ap()
        self.leado = dt("leado", [NC, DS, 4], BF16, kind="Internal").ap()

        es = contextlib.ExitStack()
        with es:
            self._alloc(nc, es)
            self.sem = {}
            for s in ("sp", "sd", "g", "t", "v", "s", "sm", "p0", "p1",
                      "bc", "stg0", "stg1", "rc", "out", "rep0", "rep1"):
                self.sem[s] = es.enter_context(nc.semaphore(s))
            with nc.Block() as blk:
                self.h = {}
                self.pass0 = True
                me = self

                def reg(name):
                    def f(hdl):
                        me.h[name] = hdl
                        me.eng = name
                        me.cnt = {k: 0 for k in (
                            "sp", "sd", "g", "t", "v", "s", "sm", "p0", "p1",
                            "bc", "stg0", "stg1", "rc", "out", "rep0", "rep1")}
                        me._walk()
                        me.pass0 = False
                    return f
                blk.sync(reg('sp'))
                blk.scalar(reg('s'))
                blk.vector(reg('v'))
                blk.tensor(reg('t'))
                blk.gpsimd(reg('g'))
        return nc

    def _alloc(self, nc, es):
        SUPW, W2 = self.SUPW, self.W2

        def sb(name, shape, dtp):
            return es.enter_context(nc.sbuf_tensor(name, shape, dtp))

        self.ones = sb("ones", [128, 128], BF16)
        self.ysel = sb("yselt", [128, 32 * 128], BF16)
        self.hsel = sb("hselt", [128, 8 * 16], BF16)
        self.ssel = sb("sselt", [16, 8 * 128], BF16)
        self.alogp = sb("alogpt", [DS, NST], F32)
        self.adev = sb("adevt", [DS, NST], F32)
        self.dcol = sb("dcolt", [DS, 1], F32)
        self.convw = sb("convwt", [DS, 4], F32)
        self.convb = sb("convbt", [DS, 1], F32)
        self.maskc = sb("maskct", [DS, 1], F32)
        self.mur = sb("murt", [128, 8], F32)
        self.muk = sb("mukt", [128, 8], F32)
        self.outbc = sb("outbt", [128, 8], F32)
        self.winT = sb("winTt", [128, 8 * 320], BF16)
        self.wsumrow = sb("wsumrowt", [1, 320], BF16)
        self.hcarry = sb("hcarryt", [128, 32], F32)
        self.zcol0 = sb("zcol0", [DS, 4], BF16)
        self.dsel = sb("dselt", [128, 32 * 128], F16)
        self.mmat = sb("mmatt", [128, 8 * 64], BF16)
        self.sigg = sb("siggt", [16, self.QB], F32)
        self.sigb = sb("sigbt", [16, self.QB], BF16)
        self.qsq = sb("qsqt", [16, self.QB], F32)
        self.mrepB = sb("mrepBt", [128, W2], BF16)
        self.rrepB = sb("rrepBt", [128, W2], BF16)
        self.vtmpB = sb("vtmpBt", [128, W2], F32)
        self.save2 = sb("save2t", [128, 16], BF16)
        self.save4 = sb("save4t", [128, 16], BF16)
        self.wstage = [sb(f"wstg{i}", [128, D], F32) for i in range(2)]
        self.vwblk = sb("vwblk", [128, D], BF16)
        self.macc = sb("macc", [128, 512], F32)
        self.cpool = sb("cpool", [128, 8], F32)
        nc.const_aps.register(F32, LN_EPS, self.cpool[:, 0:1])
        nc.const_aps.register(F32, -0.5, self.cpool[:, 1:2])
        nc.const_aps.register(F32, 1.0 / D, self.cpool[:, 2:3])
        nc.const_aps.register(F32, -1.0 / D, self.cpool[:, 3:4])
        self.xresb = [sb(f"xresb{i}", [128, W2], F32) for i in range(2)]

        layA, layB = {}, {}

        def mk(lay, name, p, cols, dtp, off):
            nb = cols * mybir.dt.size(dtp)
            lay[name] = (off, p, cols, dtp)
            return off + (nb + 63) // 64 * 64

        off = 0
        CW = self.CW
        for spec in [("xstage0", 128, SUPW, F32), ("xstage1", 128, SUPW, F32)] + \
                    [(f"xbf{i}", 128, SUPW, BF16) for i in range(8)] + \
                    [("xsq0", 128, self.CW2, BF16), ("xsq1", 128, self.CW2, BF16),
                     ("vtmp", 128, SUPW, F32),
                     ("uraw", DS, SUPW + 4, F32),
                     ("uconv0", DS, SUPW, F16), ("uconv1", DS, SUPW, F16),
                     ("dsp0", DS, SUPW, F16), ("dsp1", DS, SUPW, F16),
                     ("dtmp", DS, SUPW, F16),
                     ("bcbuf", 64, SUPW, BF16),
                     ("brep0", 128, SUPW, BF16), ("crep0", 128, SUPW, BF16),
                     ("brep1", 128, SUPW, BF16), ("crep1", 128, SUPW, BF16),
                     ("urep0", 128, SUPW, F16), ("urep1", 128, SUPW, F16),
                     ("lam0", 128, SUPW, F16), ("lam1", 128, SUPW, F16),
                     ("xb", 128, SUPW, BF16), ("hbuf", 128, SUPW, BF16),
                     ("hc0", 128, SUPW, BF16), ("hc1", 128, SUPW, BF16),
                     ("zsup", DS, SUPW, BF16),
                     ("negm", 1, SUPW, BF16), ("rrep", 128, SUPW, BF16),
                     ("mwork", 128, self.CW2, F32)]:
            off = mk(layA, *spec, off)
        endA = off
        off = 0
        for spec in [(w, 128, 8 * 1024, BF16)
                     for w in ("wout", "wr", "wk", "wv", "qwb", "owb")] + \
                    [("zrecv", 128, 8 * W2, BF16), ("rres", 128, 8 * W2, F32),
                     ("xn", 128, 8 * W2, BF16), ("tmpa", 128, 8 * W2, BF16),
                     ("tmpb", 128, 8 * W2, BF16), ("tmpc", 128, 8 * W2, BF16)]:
            off = mk(layB, *spec, off)
        endB = off
        self.ARENA_B = max(endA, endB)
        self.arena = es.enter_context(nc.sbuf_tensor("arena", [128, self.ARENA_B], U8))
        self._layA, self._layB = layA, layB
        self._vcache = {}
        self.ps = es.enter_context(nc.psum_tensor("psar", [128, 4096], F32))

    def _view(self, lay, name):
        key = (id(lay), name)
        if key not in self._vcache:
            off, p, cols, dtp = lay[name]
            esz = mybir.dt.size(dtp)
            assert off % esz == 0
            v = bass.AP(self.arena, 0,
                        [[self.ARENA_B, p], [1, cols * esz]])
            v = bass.AP(self.arena, off, [[self.ARENA_B, p], [1, cols * esz]]).bitcast(dtp)
            self._vcache[key] = v
        return self._vcache[key]

    def a(self, name):
        return self._view(self._layA, name)

    def b(self, name):
        return self._view(self._layB, name)

    def _walk(self):
        self._uidx = 0
        self._qidx = 0
        self._qc = {}
        self._startup()
        for sup in range(self.nsup):
            self._super(sup)
        self._exchange_and_wprep()
        for qq in range(self.NQ):
            if qq == 0:
                self._quarter(qq)
            else:
                self._quarterI(qq)
        self._finish()

    def _startup(self):
        E, h = self.E, self.h
        loads = [
            (self.ones[:], self.i["ones128"]), (self.ysel[:], self.i["ysel"]),
            (self.hsel[:], self.i["hsel"]), (self.ssel[:], self.i["ssel"]),
            (self.alogp[:], self.i["alogp"]), (self.dcol[:], self.i["dcol"]),
            (self.convw[:], self.i["convw"]), (self.convb[:], self.i["convb"]),
            (self.maskc[:], self.i["leadmask"]),
            (self.dsel[:], self.i["dsel"]),
            (self.mur[:], self.i["mur"]),
            (self.muk[:], self.i["muk"]),
            (self.outbc[:], self.i["outb"]),
        ]
        for dst, src in loads:
            E('sp', lambda d=dst, s=src: h['sp'].dma_start(out=d, in_=s), inc=('sm', 16))
        self.mark("smalls")
        for i in range(8):
            if i >= 2:
                self.wm('sp', 's', f"wic{i-2}")
            E('sp', lambda i=i: h['sp'].dma_start(
                out=self.a(f"xstage{i%2}")[:, 0:320],
                in_=self.i["winT"][128 * i:128 * (i + 1), :]), inc=(f"p{i%2}", 16))
            self.mark(f"wil{i}")
            self.wm('s', f"p{i%2}", f"wil{i}")
            E('s', lambda i=i: h['s'].copy(
                self.winT[:, 320 * i:320 * (i + 1)], self.a(f"xstage{i%2}")[:, 0:320]),
                inc=('s', 1))
            self.mark(f"wic{i}")
        self.wm('s', 'sm', "smalls")
        E('s', lambda: h['s'].activation(self.adev[:], self.alogp[:], AF.Exp), inc=('s', 1))
        self.mark("adevexp")
        self.wm('v', 's', "adevexp")
        E('v', lambda: h['v'].tensor_scalar_mul(self.adev[:], self.adev[:], -1.0), inc=('v', 1))
        pa = self._ugate()
        self.wm('t', 's', "wic7")
        self.wm('t', 'sm', "smalls")
        for i in range(8):
            E('t', lambda i=i, pa=pa: h['t'].matmul(
                self.ps[0:1, pa:pa + 320], self.ones[:, 0:1],
                self.winT[:, 320 * i:320 * (i + 1)],
                start=(i == 0), stop=(i == 7)), inc=('t', 1))
        self.mark("wsmm")
        self.wm('s', 't', "wsmm")
        E('s', lambda pa=pa: h['s'].copy(self.wsumrow[0:1, :], self.ps[0:1, pa:pa + 320]),
          inc=('s', 1))
        self.mark("wsumrow")
        self._udone('s', "wsumrow")
        E('v', lambda: h['v'].memset(self.cpool[:, 0:1], LN_EPS), inc=('v', 1))
        E('v', lambda: h['v'].memset(self.cpool[:, 1:2], -0.5), inc=('v', 1))
        E('v', lambda: h['v'].memset(self.cpool[:, 2:3], 1.0 / D), inc=('v', 1))
        E('v', lambda: h['v'].memset(self.cpool[:, 3:4], -1.0 / D), inc=('v', 1))
        E('v', lambda: h['v'].memset(self.zcol0[:], 0.0), inc=('v', 1))
        E('v', lambda: h['v'].memset(self.a("uraw")[:, 0:4], 0.0), inc=('v', 1))
        E('v', lambda: h['v'].memset(self.hcarry[:], 0.0), inc=('v', 1))
        self.mark("zc0")
        self.wm('sp', 'v', "zc0")
        E('sp', lambda: h['sp'].dma_start(out=self.leadi[0], in_=self.zcol0[:]),
          inc=('stg0', 16))
        self.mark("startup")
        for u in self._prep_units(0):
            u()
        self.mark("preamble")

    def _ugate(self):
        k = self._uidx
        if k >= 2:
            sem, name = self._gc[k - 2]
            self.wm('t', sem, name)
        return 2048 + (k % 2) * 1024

    def _udone(self, sem, name):
        self._gc[self._uidx] = (sem, name)
        self._uidx += 1

    def _prep_units(self, n):
        E, h, a = self.E, self.h, self.a
        SUPW = self.SUPW
        CW2 = min(512, SUPW)
        NCH2 = SUPW // CW2
        par = n % 2
        nsub = CW2 // 512
        units = []

        def u_load(i):
            def f():
                if i >= 2:
                    self.wm('sp', 's', f"cast_{n}_{i-2}")
                elif n > 0:
                    self.wm('sp', 's', f"cast_{n-1}_{6+i}")
                else:
                    self.wm('sp', 's', f"wic{6+i}")
                E('sp', lambda i=i: h['sp'].dma_start(
                    out=a(f"xstage{i%2}")[:],
                    in_=self.i["xT"][128 * i:128 * (i + 1),
                                     n * SUPW:(n + 1) * SUPW]),
                    inc=(f"p{i%2}", 16))
                self.mark(f"xl_{n}_{i}")
                self.wm('s', f"p{i%2}", f"xl_{n}_{i}")
                if n > 0:
                    self.wm('s', 't', f"prj_{n-1}_b_{NCH2-1}")
                E('s', lambda i=i: h['s'].copy(a(f"xbf{i}")[:],
                                               a(f"xstage{i%2}")[:]), inc=('s', 1))
                self.mark(f"cast_{n}_{i}")
            return f
        for i in range(8):
            units.append(u_load(i))

        def u_sum(c):
            def f():
                pa = self._ugate()
                for i in range(8):
                    self.wm('t', 's', f"cast_{n}_{i}")
                    for sub in range(nsub):
                        E('t', lambda i=i, c=c, sub=sub, pa=pa: h['t'].matmul(
                            self.ps[:, pa + sub * 512:pa + sub * 512 + 512],
                            self.ones[:],
                            a(f"xbf{i}")[:, c * CW2 + sub * 512:
                                         c * CW2 + sub * 512 + 512],
                            start=(i == 0), stop=(i == 7)), inc=('t', 1))
                self.mark(f"summ_{n}_{c}")
                self.wm('s', 't', f"summ_{n}_{c}")
                if n == 0 and c == 0:
                    self.wm('s', 'v', "zc0")
                E('s', lambda c=c, pa=pa: h['s'].activation(
                    a("negm")[0:1, c * CW2:(c + 1) * CW2],
                    self.ps[0:1, pa:pa + CW2],
                    AF.Copy, scale=-1.0 / D), inc=('s', 1))
                self.mark(f"negm_{n}_{c}")
                E('s', lambda pa=pa: h['s'].activation(
                    a("mwork")[:, 0:CW2], self.ps[:, pa:pa + CW2],
                    AF.Square, scale=1.0 / D), inc=('s', 1))
                self.mark(f"msq_{n}_{c}")
                self._udone('s', f"msq_{n}_{c}")
            return f

        def u_sumsq(c):
            def f():
                pa = self._ugate()
                for i in range(8):
                    g = c * 8 + i
                    if g >= 2:
                        pc, pi = divmod(g - 2, 8)
                        self.wm('s', 't', f"smm_{n}_{pc}_{pi}")
                    elif n > 0:
                        self.wm('s', 't', f"smm_{n-1}_{NCH2-1}_{6+g}")
                    E('s', lambda c=c, i=i: h['s'].activation(
                        a(f"xsq{i%2}")[:, 0:CW2],
                        a(f"xbf{i}")[:, c * CW2:(c + 1) * CW2],
                        AF.Square), inc=('s', 1))
                    self.mark(f"sq_{n}_{c}_{i}")
                    self.wm('t', 's', f"sq_{n}_{c}_{i}")
                    for sub in range(nsub):
                        E('t', lambda i=i, sub=sub, pa=pa: h['t'].matmul(
                            self.ps[:, pa + sub * 512:pa + sub * 512 + 512],
                            self.ones[:],
                            a(f"xsq{i%2}")[:, sub * 512:sub * 512 + 512],
                            start=(i == 0), stop=(i == 7)), inc=('t', 1))
                    self.mark(f"smm_{n}_{c}_{i}")
                self.wm('v', 't', f"smm_{n}_{c}_7")
                self.wm('v', 's', f"msq_{n}_{c}")
                E('v', lambda pa=pa: h['v'].scalar_tensor_tensor(
                    a("mwork")[:, 0:CW2], self.ps[:, pa:pa + CW2], 1.0 / D,
                    a("mwork")[:, 0:CW2], ALU.mult, ALU.subtract), inc=('v', 1))
                self.mark(f"vrow_{n}_{c}")
                self.wm('s', 'v', f"vrow_{n}_{c}")
                E('s', lambda: h['s'].activation(
                    a("mwork")[:, 0:CW2], a("mwork")[:, 0:CW2],
                    AF.Ln, bias=LN_EPS), inc=('s', 1))
                E('s', lambda c=c: h['s'].activation(
                    a("rrep")[:, c * CW2:(c + 1) * CW2], a("mwork")[:, 0:CW2],
                    AF.Exp, scale=-0.5), inc=('s', 1))
                self.mark(f"rstd_{n}_{c}")
                self._udone('v', f"vrow_{n}_{c}")
            return f

        for c in range(NCH2):
            units.append(u_sum(c))
            units.append(u_sumsq(c))

        def u_proj(g, c0w, gw, c):
            def f():
                pa = self._ugate()
                for i in range(8):
                    self.wm('t', 's', f"cast_{n}_{i}")
                    for sub in range(nsub):
                        E('t', lambda i=i, c=c, sub=sub, pa=pa: h['t'].matmul(
                            self.ps[0:gw, pa + sub * 512:pa + sub * 512 + 512],
                            self.winT[:, 320 * i + c0w:320 * i + c0w + gw],
                            a(f"xbf{i}")[:, c * CW2 + sub * 512:
                                         c * CW2 + sub * 512 + 512],
                            start=(i == 0), stop=False), inc=('t', 1))
                self.wm('t', 's', f"negm_{n}_{c}")
                for sub in range(nsub):
                    E('t', lambda c=c, sub=sub, pa=pa: h['t'].matmul(
                        self.ps[0:gw, pa + sub * 512:pa + sub * 512 + 512],
                        self.wsumrow[0:1, c0w:c0w + gw],
                        a("negm")[0:1, c * CW2 + sub * 512:
                                  c * CW2 + sub * 512 + 512],
                        start=False, stop=True), inc=('t', 1))
                self.mark(f"prj_{n}_{g}_{c}")
                self.wm('v', 't', f"prj_{n}_{g}_{c}")
                self.wm('v', 's', f"rstd_{n}_{c}")
                if g == 'b':
                    if c == 0 and n > 0:
                        self.wm('v', 'bc', f"bcrep_{n-1}")
                    dst = a("bcbuf")[0:64, c * CW2:(c + 1) * CW2]
                elif g == 'u':
                    dst = a("uraw")[:, 4 + c * CW2:4 + (c + 1) * CW2]
                else:
                    dst = a("dtmp")[:, c * CW2:(c + 1) * CW2]
                E('v', lambda dst=dst, c=c, pa=pa: h['v'].tensor_mul(
                    dst, self.ps[0:gw, pa:pa + CW2],
                    a("rrep")[0:gw, c * CW2:(c + 1) * CW2]), inc=('v', 1))
                self.mark(f"dst_{n}_{g}_{c}")
                self._udone('v', f"dst_{n}_{g}_{c}")
                if g == 'd':
                    self.wm('s', 'v', f"dst_{n}_d_{c}")
                    E('s', lambda c=c: h['s'].activation(
                        a("vtmp")[:, c * CW2:(c + 1) * CW2],
                        a("dtmp")[:, c * CW2:(c + 1) * CW2], AF.Exp), inc=('s', 1))
                    if n >= 2:
                        self.wm('s', 't', f"Amm_{n-2}_31_d{self.NCH2-1}")
                    E('s', lambda c=c: h['s'].activation(
                        a(f"dsp{par}")[:, c * CW2:(c + 1) * CW2],
                        a("vtmp")[:, c * CW2:(c + 1) * CW2],
                        AF.Ln, bias=1.0), inc=('s', 1))
                    self.mark(f"sp_{n}_{c}")
            return f

        for (g, c0w, gw) in (('u', 0, 128), ('d', 128, 128), ('b', 256, 64)):
            for c in range(NCH2):
                units.append(u_proj(g, c0w, gw, c))

        def u_convA():
            def f():
                if n % self.nsb == 0:
                    E('v', lambda: h['v'].memset(a("uraw")[:, 1:4], 0.0),
                      inc=('v', 1))
                self.wm('v', 's', f"sp_{n}_{NCH2-1}")
                if n >= 2:
                    self.wm('v', 't', f"Amm_{n-2}_31_u{self.NCH2-1}")
                E('v', lambda: h['v'].tensor_scalar(
                    a("vtmp")[:, 0:SUPW], a("uraw")[:, 1:1 + SUPW],
                    self.convw[:, 0:1], self.convb[:, 0:1], ALU.mult, ALU.add),
                    inc=('v', 1))
                E('v', lambda: h['v'].scalar_tensor_tensor(
                    a("vtmp")[:, 0:SUPW], a("uraw")[:, 2:2 + SUPW],
                    self.convw[:, 1:2], a("vtmp")[:, 0:SUPW],
                    ALU.mult, ALU.add), inc=('v', 1))
            return f

        def u_convB():
            def f():
                E('v', lambda: h['v'].scalar_tensor_tensor(
                    a("vtmp")[:, 0:SUPW], a("uraw")[:, 3:3 + SUPW],
                    self.convw[:, 2:3], a("vtmp")[:, 0:SUPW],
                    ALU.mult, ALU.add), inc=('v', 1))
                E('v', lambda: h['v'].scalar_tensor_tensor(
                    a(f"uconv{par}")[:], a("uraw")[:, 4:4 + SUPW],
                    self.convw[:, 3:4], a("vtmp")[:, 0:SUPW],
                    ALU.mult, ALU.add), inc=('v', 1))
                E('v', lambda: h['v'].tensor_copy(
                    a("uraw")[:, 1:4], a("uraw")[:, 1 + SUPW:4 + SUPW]),
                    inc=('v', 1))
                self.mark(f"conv_{n}")
            return f

        def u_bcrep():
            def f():
                self.wm('sp', 'v', f"dst_{n}_b_{NCH2-1}")
                if n >= 2:
                    self.wm('sp', 'v', f"hC_{n-2}_31")
                for gi in range(4):
                    E('sp', lambda gi=gi: h['sp'].dma_start(
                        out=a(f"brep{par}")[32 * gi:32 * (gi + 1), :],
                        in_=a("bcbuf")[0:32, :]), inc=('bc', 16))
                    E('sp', lambda gi=gi: h['sp'].dma_start(
                        out=a(f"crep{par}")[32 * gi:32 * (gi + 1), :],
                        in_=a("bcbuf")[32:64, :]), inc=('bc', 16))
                self.mark(f"bcrep_{n}")
            return f

        units.append(u_convA())
        units.append(u_convB())
        units.append(u_bcrep())
        return units

    def _ymm(self, sup, t):
        E, h, a = self.E, self.h, self.a
        CW, NCH = self.CW, self.NCH
        self.wm('t', 'v', f"hC_{sup}_{t}")
        if t == 0 and sup > 0:
            self.wm('t', 'v', f"zev_{sup-1}")
        for c in range(NCH):
            E('t', lambda t=t, c=c: h['t'].matmul(
                self.ps[:, c * CW:(c + 1) * CW],
                self.ysel[:, 128 * t:128 * (t + 1)],
                a(f"hc{t%2}")[:, c * CW:(c + 1) * CW],
                start=(t == 0), stop=(t == 31)), inc=('t', 1))
        self.mark(f"ymm_{sup}_{t}")

    def _super(self, sup):
        E, h, a = self.E, self.h, self.a
        SUPW, CW2, NCH2 = self.SUPW, self.CW2, self.NCH2
        nsub = CW2 // 512
        par = sup % 2
        batch_start = (sup % self.nsb) == 0
        nt = 32
        sched = [[] for _ in range(nt)]
        if sup + 1 < self.nsup:
            units = self._prep_units(sup + 1)
            nu = len(units)
            for ui, u in enumerate(units):
                sched[min(nt - 4, 1 + (ui * 27) // max(1, nu - 1))].append(u)
        ucv, dspv = a(f"uconv{par}"), a(f"dsp{par}")
        for t in range(nt):
            buf = t % 2
            for kind, src_v in (('u', ucv), ('d', dspv)):
                for hh in range(NCH2):
                    pa = self._ugate()
                    if t == 0 and kind == 'u' and hh == 0:
                        self.wm('t', 'v', f"conv_{sup}")
                    if t == 0 and kind == 'd' and hh == 0:
                        self.wm('t', 's', f"sp_{sup}_{SUPW // min(512, SUPW) - 1}")
                    for sub in range(nsub):
                        hs = hh * CW2 + sub * 512
                        src = vap(src_v, (0, hs), [['p', 128], [1, 512]])
                        E('t', lambda pa=pa, sub=sub, src=src, t=t: h['t'].matmul(
                            self.ps[:, pa + sub * 512:pa + sub * 512 + 512],
                            self.dsel[:, 128 * t:128 * (t + 1)],
                            src, start=True, stop=True), inc=('t', 1))
                    self.mark(f"Amm_{sup}_{t}_{kind}{hh}")
                    self.wm('s', 't', f"Amm_{sup}_{t}_{kind}{hh}")
                    if kind == 'u':
                        if t >= 2:
                            self.wm('s', 'v', f"xb_{sup}_{t-2}")
                        elif sup > 0:
                            self.wm('s', 'v', f"xb_{sup-1}_{30+t}")
                        E('s', lambda buf=buf, hh=hh, pa=pa: h['s'].copy(
                            a(f"urep{buf}")[:, hh * CW2:(hh + 1) * CW2],
                            self.ps[:, pa:pa + CW2]), inc=('s', 1))
                    else:
                        if t >= 2:
                            self.wm('s', 'v', f"scan_{sup}_{t-2}")
                        elif sup > 0:
                            self.wm('s', 'v', f"scan_{sup-1}_{30+t}")
                        E('s', lambda buf=buf, hh=hh, pa=pa, t=t: h['s'].activation(
                            a(f"lam{buf}")[:, hh * CW2:(hh + 1) * CW2],
                            self.ps[:, pa:pa + CW2],
                            AF.Exp, scale=self.adev[:, t:t + 1]), inc=('s', 1))
                    self.mark(f"Acons_{sup}_{t}_{kind}{hh}")
                    self._udone('s', f"Acons_{sup}_{t}_{kind}{hh}")
            self.wm('v', 's', f"Acons_{sup}_{t}_u{NCH2-1}")
            if t == 0:
                self.wm('v', 'bc', f"bcrep_{sup}")
            E('v', lambda buf=buf: h['v'].tensor_mul(
                a("xb")[:], a(f"urep{buf}")[:], a(f"brep{par}")[:]), inc=('v', 1))
            self.mark(f"xb_{sup}_{t}")
            if t >= 1:
                pb = (t - 1) % 2
                if t >= 3:
                    self.wm('v', 't', f"ymm_{sup}_{t-3}")
                elif sup > 0:
                    self.wm('v', 't', f"ymm_{sup-1}_{29+t}")
                E('v', lambda pb=pb: h['v'].tensor_mul(
                    a(f"hc{pb}")[:], a("hbuf")[:], a(f"crep{par}")[:]),
                    inc=('v', 1))
                self.mark(f"hC_{sup}_{t-1}")
            self.wm('v', 's', f"Acons_{sup}_{t}_d{NCH2-1}")
            init = 0.0 if batch_start else self.hcarry[:, t:t + 1]
            E('v', lambda buf=buf, init=init: h['v'].tensor_tensor_scan(
                a("hbuf")[:], a(f"lam{buf}")[:], a("xb")[:], init,
                ALU.mult, ALU.add), inc=('v', 1))
            self.mark(f"scan_{sup}_{t}")
            E('v', lambda t=t: h['v'].tensor_copy(
                self.hcarry[:, t:t + 1], a("hbuf")[:, SUPW - 1:SUPW]),
                inc=('v', 1))
            if t >= 2:
                self._ymm(sup, t - 2)
            for cl in sched[t]:
                cl()
        self.wm('v', 't', f"ymm_{sup}_29")
        E('v', lambda: h['v'].tensor_mul(
            a("hc1")[:], a("hbuf")[:], a(f"crep{par}")[:]), inc=('v', 1))
        self.mark(f"hC_{sup}_31")
        self._ymm(sup, 30)
        self._ymm(sup, 31)

        self.wm('v', 't', f"ymm_{sup}_31")
        if sup > 0:
            self.wm('v', f"stg{(sup-1) // self.nsb}", f"staged_{sup-1}")
        E('v', lambda: h['v'].scalar_tensor_tensor(
            a("zsup")[:], a(f"uconv{par}")[:], self.dcol[:, 0:1],
            self.ps[0:DS, 0:SUPW], ALU.mult, ALU.add), inc=('v', 1))
        self.mark(f"zev_{sup}")
        self.wm('sp', 'v', f"zev_{sup}")
        if self.dbg:
            E('sp', lambda: h['sp'].dma_start(
                out=self.o_z[:, sup * SUPW:(sup + 1) * SUPW], in_=a("zsup")[:]),
                inc=('out', 16))
        bidx = sup // self.nsb
        q = sup % self.nsb
        nslot2 = max(1, SUPW // self.JS)
        spl = SUPW // nslot2
        for mi in range(nslot2):
            m = nslot2 * q + mi
            for c in range(self.ncb):
                E('sp', lambda mi=mi, m=m, c=c: h['sp'].dma_start(
                    out=self.a2ai[bidx * self.ncb + c][m],
                    in_=a("zsup")[:, mi * spl + c * self.C: mi * spl + (c + 1) * self.C]),
                    inc=(f"stg{bidx}", 16))
        for mi in range(nslot2):
            m = nslot2 * q + mi + 1
            if m <= 7:
                E('sp', lambda mi=mi, m=m: h['sp'].dma_start(
                    out=self.leadi[m][:, 2 * bidx:2 * bidx + 2],
                    in_=a("zsup")[:, (mi + 1) * spl - 2:(mi + 1) * spl]),
                    inc=(f"stg{bidx}", 16))
        self.mark(f"staged_{sup}")

    def _exchange_and_wprep(self):
        E, h = self.E, self.h
        for c in range(self.ncall):
            bidx = c // self.ncb
            self.wm('g', f"stg{bidx}", f"staged_{(bidx + 1) * self.nsb - 1}")
            with allow_ici_all_to_all():
                E('g', lambda c=c: h['g'].collective_compute(
                    "AllToAll", ALU.bypass, replica_groups=[list(range(NC))],
                    ins=[self.a2ai[c]], outs=[self.a2ao[c]]), inc=('g', 1))
            self.mark(f"a2a_{c}")
        self.wm('g', 'stg0', f"staged_{self.nsup - 1}")
        self.wm('g', 'stg1', f"staged_{self.nsup - 1}")
        with allow_ici_all_to_all():
            E('g', lambda: h['g'].collective_compute(
                "AllToAll", ALU.bypass, replica_groups=[list(range(NC))],
                ins=[self.leadi], outs=[self.leado]), inc=('g', 1))
        self.mark("a2a_lead")

        wlist = [("wout", "woutT"), ("wr", "wrT"), ("wk", "wkT"), ("wv", "wvT"),
                 ("qwb", "qw"), ("owb", "owT")]
        n = 0
        for wname, iname in wlist:
            for i in range(8):
                if n >= 2:
                    self.wm('sp', 's', f"wc_{n-2}")
                E('sp', lambda iname=iname, i=i, n=n: h['sp'].dma_start(
                    out=self.wstage[n % 2][:],
                    in_=self.i[iname][128 * i:128 * (i + 1), :]), inc=(f"p{n%2}", 16))
                self.mark(f"wl_{n}")
                self.wm('s', f"p{n%2}", f"wl_{n}")
                if n == 0:
                    self.wm('s', 'stg0', f"staged_{self.nsup - 1}")
                    self.wm('s', 'stg1', f"staged_{self.nsup - 1}")
                    self.wm('s', 'v', f"zev_{self.nsup - 1}")
                    self.wm('s', 't', f"ymm_{self.nsup - 1}_31")
                E('s', lambda wname=wname, i=i, n=n: h['s'].copy(
                    self.b(wname)[:, 1024 * i:1024 * (i + 1)], self.wstage[n % 2][:]),
                    inc=('s', 1))
                self.mark(f"wc_{n}")
                n += 1
        self.mark("wprep")
        for i in range(8):
            if n >= 2:
                self.wm('sp', 's', f"wc_{n-2}")
            E('sp', lambda i=i, n=n: h['sp'].dma_start(
                out=self.wstage[n % 2][:],
                in_=self.i["vw"][128 * i:128 * (i + 1), :]), inc=(f"p{n%2}", 16))
            self.mark(f"wl_{n}")
            self.wm('s', f"p{n%2}", f"wl_{n}")
            if i > 0:
                self.wm('s', 't', f"mmmb_{i-1}")
            E('s', lambda n=n: h['s'].copy(self.vwblk[:], self.wstage[n % 2][:]),
              inc=('s', 1))
            self.mark(f"wc_{n}")
            self.wm('t', 's', f"wc_{n}")
            if i > 0:
                self.wm('t', 'v', f"mac_{i-1}")
            for hh in range(16):
                po, pc = 64 * (hh % 2), 64 * (hh // 2)
                E('t', lambda hh=hh, i=i, po=po, pc=pc: h['t'].matmul(
                    self.ps[po:po + 64, pc:pc + 64],
                    self.b("qwb")[:, 1024 * i + 64 * hh:1024 * i + 64 * (hh + 1)],
                    self.vwblk[:, 64 * hh:64 * (hh + 1)],
                    start=True, stop=True),
                    inc=('t', 1))
            self.mark(f"mmmb_{i}")
            self.wm('v', 't', f"mmmb_{i}")
            if i == 0:
                E('v', lambda: h['v'].tensor_copy(self.macc[:], self.ps[:, 0:512]),
                  inc=('v', 1))
            else:
                E('v', lambda: h['v'].tensor_add(self.macc[:], self.macc[:],
                                                 self.ps[:, 0:512]), inc=('v', 1))
            self.mark(f"mac_{i}")
            n += 1
        self.mark("mmm")
        self.wm('v', 't', "mmm")
        E('v', lambda: h['v'].tensor_scalar_mul(self.mmat[:], self.macc[:],
                                                0.125), inc=('v', 1))
        for tname in ("tmpa", "tmpb", "tmpc"):
            E('v', lambda tname=tname: h['v'].memset(self.b(tname)[:], 0.0),
              inc=('v', 1))
        self.mark("mmat")

    def lv(self, t, eg, bb=None):
        base = self.b(t)
        W2, QB = self.W2, self.QB
        if bb is None:
            return base[:, eg * W2:(eg + 1) * W2]
        return base[:, eg * W2 + bb * QB:eg * W2 + bb * QB + QB]

    def lv2(self, t, eg):
        v = self.b(t)
        return vap(v, (0, eg * self.W2), [['p', 128], [self.QB, 2], [1, self.QB]])

    def psv(self, col0):
        return bass.AP(self.ps, col0, [[4096, 128], [1024, 2], [1, self.QB]])

    def _quarter(self, qq):
        E, h = self.E, self.h
        QB, QW, W2 = self.QB, self.QW, self.W2
        lv, lv2 = self.lv, self.lv2
        zr = self.b("zrecv")

        ncq = self.QW // self.C
        for bb in range(2):
            for u in range(ncq):
                c = bb * self.ncb + qq * ncq + u
                self.wm('sp', 'g', f"a2a_{c}")
                if qq == 0 and u == 0:
                    self.wm('sp', 'g', "a2a_lead")
                if qq > 0 and u == 0:
                    self.wm('sp', 't', f"opmm_{qq-1}_7")
                dst = vap(zr, (0, bb * QB + 2 + u * self.C),
                          [['p', 128], [W2, 8], [1, self.C]])
                E('sp', lambda c=c, dst=dst: h['sp'].dma_start(
                    out=dst, in_=self.a2ao[c].rearrange("s p t -> p s t")), inc=('rc', 16))
            dl = vap(zr, (0, bb * QB), [['p', 128], [W2, 8], [1, 2]])
            if qq == 0:
                E('sp', lambda dl=dl, bb=bb: h['sp'].dma_start(
                    out=dl,
                    in_=self.leado.rearrange("s p t -> p s t")[:, :, 2 * bb:2 * bb + 2]),
                    inc=('rc', 16))
            else:
                cprev = bb * self.ncb + qq * ncq - 1
                E('sp', lambda cprev=cprev, dl=dl: h['sp'].dma_start(
                    out=dl,
                    in_=self.a2ao[cprev].rearrange("s p t -> p s t")[:, :, self.C - 2:self.C]),
                    inc=('rc', 16))
        self.mark(f"zr_{qq}")

        for eg in range(8):
            if eg >= 2:
                self.wm('sp', 'v', f"r1_{qq}_{eg-2}")
            elif qq > 0:
                self.wm('sp', 'v', f"r1_{qq-1}_{6+eg}")
            else:
                self.wm('sp', 's', "wc_55")
            src = bass.AP(self.i["xres"].tensor,
                          128 * eg * 2 * (self.JS + 2) + qq * QW,
                          [[2 * (self.JS + 2), 128], [(self.JS + 2), 2], [1, QB]])
            E('sp', lambda src=src, eg=eg: h['sp'].dma_start(
                out=self.xresb[eg % 2][:].rearrange("p (b t) -> p b t", b=2), in_=src),
                inc=(f"p{eg%2}", 16))
            self.mark(f"xr_{qq}_{eg}")
            self.wm('t', 'rc', f"zr_{qq}")
            if qq == 0 and eg == 0:
                self.wm('t', 'v', "mmat")
            if eg >= 1:
                self.wm('t', 'v', f"r1_{qq}_{eg-1}")
            elif qq > 0:
                self.wm('t', 'v', f"r1_{qq-1}_7")
            for r in range(8):
                for bb in range(2):
                    for c0, cw in self.CHK:
                        E('t', lambda eg=eg, r=r, bb=bb, c0=c0, cw=cw: h['t'].matmul(
                            self.ps[:, bb * 1024 + c0:bb * 1024 + c0 + cw],
                            self.b("wout")[:, 1024 * r + 128 * eg:1024 * r + 128 * (eg + 1)],
                            lv("zrecv", r, bb)[:, c0:c0 + cw],
                            start=(r == 0), stop=(r == 7)),
                            inc=('t', 1))
            self.mark(f"opmm_{qq}_{eg}")
            self.wm('v', 't', f"opmm_{qq}_{eg}")
            self.wm('v', f"p{eg%2}", f"xr_{qq}_{eg}")
            if qq == 0 and eg == 0:
                self.wm('v', 's', "wprep")
            if qq > 0 and eg == 0:
                self.wm('v', 'out', f"out_{qq-1}")
            E('v', lambda eg=eg: h['v'].tensor_add(
                lv2("rres", eg), self.xresb[eg % 2][:].rearrange("p (b t) -> p b t", b=2),
                self.psv(0)), inc=('v', 1))
            self.mark(f"r1_{qq}_{eg}")

        self._layer_norm(qq, "l2", f"r1_{qq}_7")
        self._save_old(qq, self.save2)
        self._ffn(qq, "f1", "l2")
        self._layer_norm(qq, "l3", f"fo_{qq}_f1_7")
        self._attnb(qq)
        self._layer_norm(qq, "l4", f"r3_{qq}_7")
        self._save_old(qq, self.save4)
        self._ffn(qq, "f2", "l4")

        for eg in range(8):
            self.wm('sp', 'v', f"fo_{qq}_f2_{eg}")
            rr = self.b("rres")
            src = vap(rr, (0, eg * W2 + 2), [['p', 128], [QB, 2], [1, QW]])
            dst = bass.AP(self.o_out.tensor, 128 * eg * 2 * self.JS + qq * 2 * QW,
                          [[2 * self.JS, 128], [QW, 2], [1, QW]])
            E('sp', lambda src=src, dst=dst: h['sp'].dma_start(out=dst, in_=src),
              inc=('out', 16))
        self.mark(f"out_{qq}")

    def _save_old(self, qq, dstbuf):
        if self.NQ <= 1:
            return
        E, h = self.E, self.h
        xnv = self.b("xn")
        src = vap(xnv, (0, self.QB - 1), [['p', 128], [self.W2, 8], [self.QB, 2]])
        E('v', lambda src=src, dstbuf=dstbuf: h['v'].tensor_copy(
            dstbuf[:].rearrange("p (a b) -> p a b", a=8), src), inc=('v', 1))

    def rl(self, t, eg):
        return self.b(t)[:, eg * self.W2 + 4:eg * self.W2 + 4 + 2 * self.QW]

    def rl2(self, t, eg, b):
        return vap(self.b(t), (0, eg * self.W2 + 4 + b),
                   [['p', 128], [2, self.QW]])

    def shl(self, t, eg):
        return self.b(t)[:, eg * self.W2 + 2:eg * self.W2 + 2 + 2 * self.QW]

    def _stgate(self, extra=None):
        k = self._qidx
        if k >= 4:
            sem, name = self._qc[k - 4]
            self.wm('t', sem, name)
        if extra is not None:
            sem, name = extra
            self.wm('t', sem, name)
        return 512 * (k % 4)

    def _stdone(self, sem, name):
        self._qc[self._qidx] = (sem, name)
        self._qidx += 1

    def _quarterI(self, qq):
        E, h = self.E, self.h
        QB, QW, W2 = self.QB, self.QW, self.W2
        Q2 = 2 * QW
        zr = self.b("zrecv")
        if self._qidx == 0:
            for k in range(4):
                self._qc[k] = ('v', f"fo_{qq-1}_f2_6")
            self._qidx = 4

        ncq = QW // self.C
        for bb in range(2):
            for u in range(ncq):
                c = bb * self.ncb + qq * ncq + u
                self.wm('sp', 'g', f"a2a_{c}")
                if u == 0 and bb == 0:
                    self.wm('sp', 't', f"opmm_{qq-1}_7")
                dst = vap(zr, (0, 4 + bb * QW + u * self.C),
                          [['p', 128], [W2, 8], [1, self.C]])
                E('sp', lambda c=c, dst=dst: h['sp'].dma_start(
                    out=dst, in_=self.a2ao[c].rearrange("s p t -> p s t")),
                    inc=('rc', 16))
        self.mark(f"zr_{qq}")

        for eg in range(8):
            if eg >= 2:
                self.wm('sp', 'v', f"r1_{qq}_{eg-2}")
            else:
                self.wm('sp', 'v', f"r1_{qq-1}_{6+eg}")
            src = bass.AP(self.i["xresi"].tensor,
                          128 * eg * 2 * self.JS + qq * Q2,
                          [[2 * self.JS, 128], [1, Q2]])
            E('sp', lambda src=src, eg=eg: h['sp'].dma_start(
                out=self.xresb[eg % 2][:, 0:Q2], in_=src), inc=(f"p{eg%2}", 16))
            self.mark(f"xr_{qq}_{eg}")
            base = self._stgate(extra=('rc', f"zr_{qq}"))
            for r in range(8):
                E('t', lambda eg=eg, r=r, base=base: h['t'].matmul(
                    self.ps[:, base:base + Q2],
                    self.b("wout")[:, 1024 * r + 128 * eg:1024 * r + 128 * (eg + 1)],
                    self.rl("zrecv", r), start=(r == 0), stop=(r == 7)),
                    inc=('t', 1))
            self.mark(f"opmm_{qq}_{eg}")
            self.wm('v', 't', f"opmm_{qq}_{eg}")
            self.wm('v', f"p{eg%2}", f"xr_{qq}_{eg}")
            if eg == 0:
                self.wm('v', 'out', f"out_{qq-1}")
            rr = self.b("rres")
            out3 = vap(rr, (0, eg * W2 + 4), [['p', 128], [1, 2], [2, QW]])
            xb3 = bass.AP(self.xresb[eg % 2], 0, [[W2, 128], [1, 2], [2, QW]])
            ps3 = bass.AP(self.ps, base, [[4096, 128], [QW, 2], [1, QW]])
            E('v', lambda out3=out3, xb3=xb3, ps3=ps3: h['v'].tensor_add(
                out3, xb3, ps3), inc=('v', 1))
            self.mark(f"r1_{qq}_{eg}")
            self._stdone('v', f"r1_{qq}_{eg}")

        self._lnI(qq, "l2", lambda eg: f"r1_{qq}_{eg}",
                  pb4=('s', f"lnm_l4_{qq-1}"), pb5=('v', f"lnv_l4_{qq-1}"),
                  srbuf=self.save2)
        self._ffnI(qq, "f1", "l2")
        self._lnI(qq, "l3", lambda eg: f"fo_{qq}_f1_{eg}",
                  pb4=('s', f"lnm_l2_{qq}"), pb5=('v', f"lnv_l2_{qq}"))
        self._attnbI(qq)
        self._lnI(qq, "l4", lambda eg: f"r3_{qq}_{eg}",
                  pb4=('v', f"zev_{qq}_6"), pb5=('v', f"zev_{qq}_7"),
                  srbuf=self.save4)
        self._ffnI(qq, "f2", "l4")

        for eg in range(8):
            self.wm('sp', 'v', f"fo_{qq}_f2_{eg}")
            src = self.rl("rres", eg)
            dst = bass.AP(self.o_out.tensor, 128 * eg * 2 * self.JS + qq * Q2,
                          [[2 * self.JS, 128], [1, Q2]])
            E('sp', lambda src=src, dst=dst: h['sp'].dma_start(out=dst, in_=src),
              inc=('out', 16))
        self.mark(f"out_{qq}")

    def _lnI(self, qq, tag, vdep, pb4, pb5, srbuf=None):
        E, h = self.E, self.h
        QW, W2 = self.QW, self.W2
        Q2 = 2 * QW
        rl = self.rl
        tg = f"{tag}_{qq}"
        for eg in range(8):
            self.wm('s', 'v', vdep(eg))
            E('s', lambda eg=eg: h['s'].copy(rl("tmpa", eg), rl("rres", eg)),
              inc=('s', 1))
            E('s', lambda eg=eg: h['s'].activation(rl("tmpb", eg), rl("tmpa", eg),
                                                   AF.Square), inc=('s', 1))
            self.mark(f"lncg_{tg}_{eg}")
            self.wm('t', 's', f"lncg_{tg}_{eg}")
            if eg == 0:
                self.wm('t', pb4[0], pb4[1])
                self.wm('t', pb5[0], pb5[1])
            E('t', lambda eg=eg: h['t'].matmul(
                self.ps[:, 2048:2048 + Q2], self.ones[:], rl("tmpa", eg),
                start=(eg == 0), stop=(eg == 7)), inc=('t', 1))
            E('t', lambda eg=eg: h['t'].matmul(
                self.ps[:, 2560:2560 + Q2], self.ones[:], rl("tmpb", eg),
                start=(eg == 0), stop=(eg == 7)), inc=('t', 1))
        self.mark(f"lnmm_{tg}")
        self.wm('s', 't', f"lnmm_{tg}")
        E('s', lambda: h['s'].activation(
            self.mrepB[:, 0:Q2], self.ps[:, 2048:2048 + Q2],
            AF.Copy, scale=1.0 / D), inc=('s', 1))
        E('s', lambda: h['s'].activation(self.vtmpB[:, 0:Q2], self.mrepB[:, 0:Q2],
                                         AF.Square), inc=('s', 1))
        self.mark(f"lnm_{tg}")
        self.wm('v', 's', f"lnm_{tg}")
        self.wm('v', 't', f"lnmm_{tg}")
        E('v', lambda: h['v'].scalar_tensor_tensor(
            self.vtmpB[:, 0:Q2], self.ps[:, 2560:2560 + Q2], 1.0 / D,
            self.vtmpB[:, 0:Q2], ALU.mult, ALU.subtract), inc=('v', 1))
        self.mark(f"lnv_{tg}")
        self.wm('s', 'v', f"lnv_{tg}")
        E('s', lambda: h['s'].activation(self.vtmpB[:, 0:Q2], self.vtmpB[:, 0:Q2],
                                         AF.Ln, bias=LN_EPS), inc=('s', 1))
        E('s', lambda: h['s'].activation(self.rrepB[:, 0:Q2], self.vtmpB[:, 0:Q2],
                                         AF.Exp, scale=-0.5), inc=('s', 1))
        self.mark(f"lnr_{tg}")
        self.wm('v', 's', f"lnr_{tg}")
        for eg in range(8):
            E('v', lambda eg=eg: h['v'].tensor_sub(rl("tmpa", eg), rl("rres", eg),
                                                   self.mrepB[:, 0:Q2]), inc=('v', 1))
            E('v', lambda eg=eg: h['v'].tensor_mul(rl("xn", eg), rl("tmpa", eg),
                                                   self.rrepB[:, 0:Q2]), inc=('v', 1))
            self.mark(f"lnxn_{tg}_{eg}")
        if srbuf is not None:
            xnv = self.b("xn")
            dstl = vap(xnv, (0, 2), [['p', 128], [W2, 8], [1, 2]])
            E('v', lambda dstl=dstl, srbuf=srbuf: h['v'].tensor_copy(
                dstl, srbuf[:].rearrange("p (a b) -> p a b", a=8)), inc=('v', 1))
            srcl = vap(xnv, (0, 2 + Q2 - 2), [['p', 128], [W2, 8], [1, 2]])
            E('v', lambda srcl=srcl, srbuf=srbuf: h['v'].tensor_copy(
                srbuf[:].rearrange("p (a b) -> p a b", a=8), srcl), inc=('v', 1))
        self.mark(f"ln_{tg}")

    def _ffnI(self, qq, tag, lntag):
        E, h = self.E, self.h
        QW, W2 = self.QW, self.W2
        Q2 = 2 * QW
        rl, shl = self.rl, self.shl
        tg = f"{qq}_{tag}"
        self.wm('v', 't', f"lnmm_{lntag}_{qq}")
        for eg in range(8):
            E('v', lambda eg=eg: h['v'].tensor_sub(
                rl("tmpa", eg), rl("xn", eg), shl("xn", eg)), inc=('v', 1))
            E('v', lambda eg=eg: h['v'].scalar_tensor_tensor(
                rl("tmpb", eg), rl("tmpa", eg),
                self.mur[:, eg:eg + 1], shl("xn", eg), ALU.mult, ALU.add),
                inc=('v', 1))
            E('v', lambda eg=eg: h['v'].scalar_tensor_tensor(
                rl("tmpc", eg), rl("tmpa", eg),
                self.muk[:, eg:eg + 1], shl("xn", eg), ALU.mult, ALU.add),
                inc=('v', 1))
            self.mark(f"mixe_{tg}_{eg}")
        self.mark(f"mix_{tg}")
        for eg in range(8):
            base = self._stgate()
            for i in range(8):
                self.wm('t', 'v', f"mixe_{tg}_{i}")
                E('t', lambda eg=eg, i=i, base=base: h['t'].matmul(
                    self.ps[:, base:base + Q2],
                    self.b("wr")[:, 1024 * i + 128 * eg:1024 * i + 128 * (eg + 1)],
                    rl("tmpb", i), start=(i == 0), stop=(i == 7)), inc=('t', 1))
            self.mark(f"rmm_{tg}_{eg}")
            self.wm('s', 't', f"rmm_{tg}_{eg}")
            E('s', lambda eg=eg, base=base: h['s'].activation(
                rl("tmpa", eg), self.ps[:, base:base + Q2],
                AF.Sigmoid), inc=('s', 1))
            self.mark(f"rg_{tg}_{eg}")
            self._stdone('s', f"rg_{tg}_{eg}")
        for eg in range(8):
            base = self._stgate()
            for i in range(8):
                self.wm('t', 'v', f"mixe_{tg}_{i}")
                E('t', lambda eg=eg, i=i, base=base: h['t'].matmul(
                    self.ps[:, base:base + Q2],
                    self.b("wk")[:, 1024 * i + 128 * eg:1024 * i + 128 * (eg + 1)],
                    rl("tmpc", i), start=(i == 0), stop=(i == 7)), inc=('t', 1))
            self.mark(f"kmm_{tg}_{eg}")
            self.wm('v', 't', f"kmm_{tg}_{eg}")
            if eg == 0:
                self.wm('v', 't', f"rmm_{tg}_7")
            E('v', lambda eg=eg, base=base: h['v'].tensor_scalar_max(
                rl("tmpb", eg), self.ps[:, base:base + Q2], 0.0), inc=('v', 1))
            self.mark(f"kr_{tg}_{eg}")
            self.wm('s', 'v', f"kr_{tg}_{eg}")
            E('s', lambda eg=eg: h['s'].activation(rl("xn", eg), rl("tmpb", eg),
                                                   AF.Square), inc=('s', 1))
            self.mark(f"k2_{tg}_{eg}")
            self._stdone('v', f"kr_{tg}_{eg}")
        for eg in range(8):
            base = self._stgate()
            for i in range(8):
                self.wm('t', 's', f"k2_{tg}_{i}")
                E('t', lambda eg=eg, i=i, base=base: h['t'].matmul(
                    self.ps[:, base:base + Q2],
                    self.b("wv")[:, 1024 * i + 128 * eg:1024 * i + 128 * (eg + 1)],
                    rl("xn", i), start=(i == 0), stop=(i == 7)), inc=('t', 1))
            self.mark(f"vmm_{tg}_{eg}")
            self.wm('v', 't', f"vmm_{tg}_{eg}")
            if eg == 0:
                self.wm('v', 't', f"kmm_{tg}_7")
            E('v', lambda eg=eg, base=base: h['v'].tensor_mul(
                rl("tmpc", eg), rl("tmpa", eg),
                self.ps[:, base:base + Q2]), inc=('v', 1))
            E('v', lambda eg=eg: h['v'].tensor_add(rl("rres", eg), rl("rres", eg),
                                                   rl("tmpc", eg)), inc=('v', 1))
            self.mark(f"fo_{qq}_{tag}_{eg}")
            self._stdone('v', f"fo_{qq}_{tag}_{eg}")

    def _attnbI(self, qq):
        E, h = self.E, self.h
        QW, W2 = self.QW, self.W2
        Q2 = 2 * QW
        rl, rl2 = self.rl, self.rl2
        for eg in range(8):
            base = self._stgate()
            for i in range(8):
                self.wm('t', 'v', f"lnxn_l3_{qq}_{i}")
                E('t', lambda eg=eg, i=i, base=base: h['t'].matmul(
                    self.ps[:, base:base + Q2],
                    self.b("qwb")[:, 1024 * i + 128 * eg:1024 * i + 128 * (eg + 1)],
                    rl("xn", i), start=(i == 0), stop=(i == 7)), inc=('t', 1))
            self.mark(f"qmm_{qq}_{eg}")
            self.wm('v', 't', f"qmm_{qq}_{eg}")
            E('v', lambda eg=eg, base=base: h['v'].tensor_copy(
                rl("tmpa", eg), self.ps[:, base:base + Q2]), inc=('v', 1))
            self.mark(f"qev_{qq}_{eg}")
            self.wm('s', 'v', f"qev_{qq}_{eg}")
            E('s', lambda eg=eg: h['s'].activation(rl("tmpb", eg), rl("tmpa", eg),
                                                   AF.Square), inc=('s', 1))
            self.mark(f"q2_{qq}_{eg}")
            self._stdone('v', f"qev_{qq}_{eg}")
        for eg in range(8):
            self.wm('t', 's', f"q2_{qq}_{eg}")
            if eg == 0:
                self.wm('t', 'v', f"mx_{qq-1}_7")
            E('t', lambda eg=eg: h['t'].matmul(
                self.ps[0:16, 3072:3072 + Q2],
                self.hsel[:, 16 * eg:16 * (eg + 1)], rl("tmpb", eg),
                start=(eg == 0), stop=(eg == 7)), inc=('t', 1))
        self.mark(f"qsmm_{qq}")
        self.wm('v', 't', f"qsmm_{qq}")
        for eg in range(8):
            E('v', lambda eg=eg: h['v'].tensor_mul(
                rl2("tmpb", eg, 0), rl2("tmpa", eg, 0), rl2("tmpa", eg, 1)),
                inc=('v', 1))
        E('v', lambda: h['v'].tensor_copy(
            self.qsq[:, 0:QW], bass.AP(self.ps, 3072, [[4096, 16], [2, QW]])),
            inc=('v', 1))
        E('v', lambda: h['v'].scalar_tensor_tensor(
            self.qsq[:, 0:QW], bass.AP(self.ps, 3073, [[4096, 16], [2, QW]]),
            1.0, self.qsq[:, 0:QW], ALU.mult, ALU.add), inc=('v', 1))
        E('v', lambda: h['v'].tensor_scalar_mul(self.qsq[:, 0:QW],
                                                self.qsq[:, 0:QW], 0.125),
          inc=('v', 1))
        self.mark(f"qd_{qq}")
        self.wm('t', 'v', f"qd_{qq}")
        self.wm('t', 'v', f"gv_{qq-1}")
        for eg in range(8):
            E('t', lambda eg=eg: h['t'].matmul(
                self.ps[0:16, 3584:3584 + QW],
                self.hsel[:, 16 * eg:16 * (eg + 1)], rl2("tmpb", eg, 0),
                start=(eg == 0), stop=(eg == 7)), inc=('t', 1))
        self.mark(f"qdmm_{qq}")
        self.wm('v', 't', f"qdmm_{qq}")
        E('v', lambda: h['v'].scalar_tensor_tensor(
            self.sigg[:, 0:QW], self.ps[0:16, 3584:3584 + QW], -0.25,
            self.qsq[:, 0:QW], ALU.mult, ALU.add), inc=('v', 1))
        self.mark(f"gv_{qq}")
        self.wm('s', 'v', f"gv_{qq}")
        E('s', lambda: h['s'].activation(self.sigb[:, 0:QW], self.sigg[:, 0:QW],
                                         AF.Sigmoid), inc=('s', 1))
        self.mark(f"sg_{qq}")
        for eg in range(8):
            if eg == 0:
                self.wm('t', 'v', f"qd_{qq}")
                self.wm('t', 's', f"lnm_l3_{qq}")
                self.wm('t', 'v', f"lnv_l3_{qq}")
            else:
                self.wm('t', 'v', f"zev_{qq}_{eg-1}")
            zb = 2048 + 512 * (eg % 2)
            for hh in (2 * eg, 2 * eg + 1):
                po, pc = 64 * (hh % 2), 64 * (hh // 2)
                E('t', lambda hh=hh, po=po, pc=pc, eg=eg, zb=zb: h['t'].matmul(
                    self.ps[po:po + 64, zb:zb + Q2],
                    self.mmat[po:po + 64, pc:pc + 64],
                    self.rl("tmpa", eg)[po:po + 64, :],
                    start=True, stop=True), inc=('t', 1))
            self.mark(f"zmm_{qq}_{eg}")
            self.wm('v', 't', f"zmm_{qq}_{eg}")
            if eg == 0:
                self.wm('v', 't', f"qdmm_{qq}")
            E('v', lambda eg=eg, zb=zb: h['v'].tensor_copy(
                rl("tmpb", eg), self.ps[:, zb:zb + Q2]), inc=('v', 1))
            self.mark(f"zev_{qq}_{eg}")
        for eg in range(8):
            self.wm('t', 'v', f"zev_{qq}_{eg}")
            self.wm('t', 's', f"sg_{qq}")
            if eg >= 1:
                self.wm('t', 'v', f"mx_{qq}_{eg-1}")
            elif qq >= 2:
                self.wm('t', 'v', f"mx_{qq-1}_7")
            E('t', lambda eg=eg: h['t'].matmul(
                self.ps[:, 3840:3840 + QW],
                self.ssel[:, 128 * eg:128 * (eg + 1)], self.sigb[:, 0:QW],
                start=True, stop=True), inc=('t', 1))
            self.mark(f"smm_{qq}_{eg}")
            self.wm('v', 't', f"smm_{qq}_{eg}")
            E('v', lambda eg=eg: h['v'].tensor_sub(
                rl2("tmpa", eg, 0), rl2("tmpb", eg, 0), rl2("tmpb", eg, 1)),
                inc=('v', 1))
            E('v', lambda eg=eg: h['v'].tensor_mul(
                rl2("tmpa", eg, 0), self.ps[:, 3840:3840 + QW],
                rl2("tmpa", eg, 0)), inc=('v', 1))
            E('v', lambda eg=eg: h['v'].tensor_sub(
                rl2("tmpc", eg, 1), rl2("tmpb", eg, 0), rl2("tmpa", eg, 0)),
                inc=('v', 1))
            E('v', lambda eg=eg: h['v'].tensor_add(
                rl2("tmpc", eg, 0), rl2("tmpb", eg, 1), rl2("tmpa", eg, 0)),
                inc=('v', 1))
            self.mark(f"mx_{qq}_{eg}")
        for eg in range(8):
            base = self._stgate()
            for i in range(8):
                self.wm('t', 'v', f"mx_{qq}_{i}")
                E('t', lambda eg=eg, i=i, base=base: h['t'].matmul(
                    self.ps[:, base:base + Q2],
                    self.b("owb")[:, 1024 * i + 128 * eg:1024 * i + 128 * (eg + 1)],
                    rl("tmpc", i), start=(i == 0), stop=(i == 7)), inc=('t', 1))
            self.mark(f"owmm_{qq}_{eg}")
            self.wm('v', 't', f"owmm_{qq}_{eg}")
            E('v', lambda eg=eg, base=base: h['v'].scalar_tensor_tensor(
                self.rl("rres", eg), self.ps[:, base:base + Q2],
                self.outbc[:, eg:eg + 1],
                self.rl("rres", eg), ALU.add, ALU.add), inc=('v', 1))
            self.mark(f"r3_{qq}_{eg}")
            self._stdone('v', f"r3_{qq}_{eg}")

    def _layer_norm(self, qq, tag, vdep):
        E, h = self.E, self.h
        QB, W2 = self.QB, self.W2
        lv, lv2 = self.lv, self.lv2
        vap_ = vap
        tg = f"{tag}_{qq}"
        self.wm('s', 'v', vdep)
        for eg in range(8):
            E('s', lambda eg=eg: h['s'].copy(lv("tmpa", eg), lv("rres", eg)), inc=('s', 1))
            self.sw('s')
            E('s', lambda eg=eg: h['s'].activation(lv("tmpb", eg), lv("tmpa", eg),
                                                   AF.Square), inc=('s', 1))
        self.mark(f"lnc_{tg}")
        self.wm('t', 's', f"lnc_{tg}")
        for eg in range(8):
            for bb in range(2):
                for c0, cw in self.CHK:
                    E('t', lambda eg=eg, bb=bb, c0=c0, cw=cw: h['t'].matmul(
                        self.ps[:, 2048 + bb * 1024 + c0:2048 + bb * 1024 + c0 + cw],
                        self.ones[:], lv("tmpa", eg, bb)[:, c0:c0 + cw],
                        start=(eg == 0), stop=(eg == 7)),
                        inc=('t', 1))
        self.mark(f"lnmmx_{tg}")
        self.wm('s', 't', f"lnmmx_{tg}")
        E('s', lambda: h['s'].activation(
            vap_(self.mrepB[:], (0, 0), [['p', 128], [QB, 2], [1, QB]]),
            self.psv(2048), AF.Copy, scale=1.0 / D), inc=('s', 1))
        self.sw('s')
        E('s', lambda: h['s'].activation(self.vtmpB[:], self.mrepB[:], AF.Square),
          inc=('s', 1))
        self.mark(f"lnm_{tg}")
        self.wm('t', 's', f"lnm_{tg}")
        for eg in range(8):
            for bb in range(2):
                for c0, cw in self.CHK:
                    E('t', lambda eg=eg, bb=bb, c0=c0, cw=cw: h['t'].matmul(
                        self.ps[:, 2048 + bb * 1024 + c0:2048 + bb * 1024 + c0 + cw],
                        self.ones[:], lv("tmpb", eg, bb)[:, c0:c0 + cw],
                        start=(eg == 0), stop=(eg == 7)),
                        inc=('t', 1))
        self.mark(f"lnmm_{tg}")
        self.wm('v', 's', f"lnm_{tg}")
        self.wm('v', 't', f"lnmm_{tg}")
        E('v', lambda: h['v'].scalar_tensor_tensor(
            vap_(self.vtmpB[:], (0, 0), [['p', 128], [QB, 2], [1, QB]]),
            self.psv(2048), 1.0 / D,
            vap_(self.vtmpB[:], (0, 0), [['p', 128], [QB, 2], [1, QB]]),
            ALU.mult, ALU.subtract), inc=('v', 1))
        self.mark(f"lnv_{tg}")
        self.wm('s', 'v', f"lnv_{tg}")
        E('s', lambda: h['s'].activation(self.vtmpB[:], self.vtmpB[:], AF.Ln,
                                         bias=LN_EPS), inc=('s', 1))
        self.sw('s')
        E('s', lambda: h['s'].activation(self.rrepB[:], self.vtmpB[:], AF.Exp,
                                         scale=-0.5), inc=('s', 1))
        self.mark(f"lnr_{tg}")
        self.wm('v', 's', f"lnr_{tg}")
        for eg in range(8):
            E('v', lambda eg=eg: h['v'].tensor_sub(lv("tmpa", eg), lv("rres", eg),
                                                   self.mrepB[:]), inc=('v', 1))
            E('v', lambda eg=eg: h['v'].tensor_mul(lv("xn", eg), lv("tmpa", eg),
                                                   self.rrepB[:]), inc=('v', 1))
        xnv = self.b("xn")
        leadv = vap_(xnv, (0, 0), [['p', 128], [W2, 8], [QB, 2], [1, 2]])
        E('v', lambda leadv=leadv: h['v'].tensor_scalar_mul(leadv, leadv,
                                                            self.maskc[:, 0:1]),
          inc=('v', 1))
        self.mark(f"ln_{tg}")

    def _ffn(self, qq, tag, lntag):
        E, h = self.E, self.h
        QB, W2 = self.QB, self.W2
        lv, lv2 = self.lv, self.lv2
        tg = f"{qq}_{tag}"
        self.wm('v', 't', f"lnmm_{lntag}_{qq}")
        for eg in range(8):
            for bb in range(2):
                xn = lv("xn", eg, bb)
                E('v', lambda xn=xn, eg=eg, bb=bb: h['v'].tensor_sub(
                    self.lv("tmpa", eg, bb)[:, 1:QB], xn[:, 1:QB], xn[:, 0:QB - 1]),
                    inc=('v', 1))
                E('v', lambda xn=xn, eg=eg, bb=bb: h['v'].scalar_tensor_tensor(
                    self.lv("tmpb", eg, bb)[:, 1:QB], self.lv("tmpa", eg, bb)[:, 1:QB],
                    self.mur[:, eg:eg + 1], xn[:, 0:QB - 1], ALU.mult, ALU.add),
                    inc=('v', 1))
                E('v', lambda xn=xn, eg=eg, bb=bb: h['v'].scalar_tensor_tensor(
                    self.lv("tmpc", eg, bb)[:, 1:QB], self.lv("tmpa", eg, bb)[:, 1:QB],
                    self.muk[:, eg:eg + 1], xn[:, 0:QB - 1], ALU.mult, ALU.add),
                    inc=('v', 1))
        self.mark(f"mix_{tg}")
        for eg in range(8):
            self.wm('t', 'v', f"mix_{tg}")
            if eg >= 2:
                self.wm('t', 's', f"rg_{tg}_{eg-2}")
            base = 2048 * (eg % 2)
            for i in range(8):
                for bb in range(2):
                    for c0, cw in self.CHK:
                        E('t', lambda eg=eg, i=i, bb=bb, c0=c0, cw=cw, base=base:
                            h['t'].matmul(
                                self.ps[:, base + bb * 1024 + c0:base + bb * 1024 + c0 + cw],
                                self.b("wr")[:, 1024 * i + 128 * eg:1024 * i + 128 * (eg + 1)],
                                lv("tmpb", i, bb)[:, c0:c0 + cw], start=(i == 0), stop=(i == 7)),
                            inc=('t', 1))
            self.mark(f"rmm_{tg}_{eg}")
            self.wm('s', 't', f"rmm_{tg}_{eg}")
            E('s', lambda eg=eg, base=base: h['s'].activation(lv2("tmpa", eg),
                                                              self.psv(base),
                                                              AF.Sigmoid), inc=('s', 1))
            self.mark(f"rg_{tg}_{eg}")
        for eg in range(8):
            self.wm('t', 's', f"rg_{tg}_7")
            if eg >= 2:
                self.wm('t', 'v', f"kr_{tg}_{eg-2}")
            base = 2048 * (eg % 2)
            for i in range(8):
                for bb in range(2):
                    for c0, cw in self.CHK:
                        E('t', lambda eg=eg, i=i, bb=bb, c0=c0, cw=cw, base=base:
                            h['t'].matmul(
                                self.ps[:, base + bb * 1024 + c0:base + bb * 1024 + c0 + cw],
                                self.b("wk")[:, 1024 * i + 128 * eg:1024 * i + 128 * (eg + 1)],
                                lv("tmpc", i, bb)[:, c0:c0 + cw], start=(i == 0), stop=(i == 7)),
                            inc=('t', 1))
            self.mark(f"kmm_{tg}_{eg}")
            self.wm('v', 't', f"kmm_{tg}_{eg}")
            E('v', lambda eg=eg, base=base: h['v'].tensor_scalar_max(lv2("tmpb", eg),
                                                                     self.psv(base), 0.0),
              inc=('v', 1))
            self.mark(f"kr_{tg}_{eg}")
            self.wm('s', 'v', f"kr_{tg}_{eg}")
            E('s', lambda eg=eg: h['s'].activation(lv("xn", eg), lv("tmpb", eg),
                                                   AF.Square), inc=('s', 1))
            self.mark(f"k2_{tg}_{eg}")
        for eg in range(8):
            self.wm('t', 's', f"k2_{tg}_7")
            if eg >= 2:
                self.wm('t', 'v', f"fo_{qq}_{tag}_{eg-2}")
            base = 2048 * (eg % 2)
            for i in range(8):
                for bb in range(2):
                    for c0, cw in self.CHK:
                        E('t', lambda eg=eg, i=i, bb=bb, c0=c0, cw=cw, base=base:
                            h['t'].matmul(
                                self.ps[:, base + bb * 1024 + c0:base + bb * 1024 + c0 + cw],
                                self.b("wv")[:, 1024 * i + 128 * eg:1024 * i + 128 * (eg + 1)],
                                lv("xn", i, bb)[:, c0:c0 + cw], start=(i == 0), stop=(i == 7)),
                            inc=('t', 1))
            self.mark(f"vmm_{tg}_{eg}")
            self.wm('v', 't', f"vmm_{tg}_{eg}")
            E('v', lambda eg=eg, base=base: h['v'].tensor_mul(lv2("tmpc", eg),
                                                              lv2("tmpa", eg),
                                                              self.psv(base)), inc=('v', 1))
            self.sw('v')
            E('v', lambda eg=eg: h['v'].tensor_add(lv("rres", eg), lv("rres", eg),
                                                   lv("tmpc", eg)), inc=('v', 1))
            self.mark(f"fo_{qq}_{tag}_{eg}")

    def _attnb(self, qq):
        E, h = self.E, self.h
        QB, W2 = self.QB, self.W2
        lv, lv2 = self.lv, self.lv2
        for eg in range(8):
            self.wm('t', 'v', f"ln_l3_{qq}")
            if eg >= 2:
                self.wm('t', 'v', f"qev_{qq}_{eg-2}")
            base = 2048 * (eg % 2)
            for i in range(8):
                for bb in range(2):
                    for c0, cw in self.CHK:
                        E('t', lambda eg=eg, i=i, bb=bb, c0=c0, cw=cw, base=base:
                            h['t'].matmul(
                                self.ps[:, base + bb * 1024 + c0:base + bb * 1024 + c0 + cw],
                                self.b("qwb")[:, 1024 * i + 128 * eg:1024 * i + 128 * (eg + 1)],
                                lv("xn", i, bb)[:, c0:c0 + cw], start=(i == 0), stop=(i == 7)),
                            inc=('t', 1))
            self.mark(f"qmm_{qq}_{eg}")
            self.wm('v', 't', f"qmm_{qq}_{eg}")
            E('v', lambda eg=eg, base=base: h['v'].tensor_copy(lv2("tmpa", eg),
                                                               self.psv(base)),
              inc=('v', 1))
            self.mark(f"qev_{qq}_{eg}")
            self.wm('s', 'v', f"qev_{qq}_{eg}")
            E('s', lambda eg=eg: h['s'].activation(lv("tmpb", eg), lv("tmpa", eg),
                                                   AF.Square), inc=('s', 1))
            self.mark(f"q2_{qq}_{eg}")
        self.wm('t', 's', f"q2_{qq}_7")
        for eg in range(8):
            for bb in range(2):
                for c0, cw in self.CHK:
                    E('t', lambda eg=eg, bb=bb, c0=c0, cw=cw: h['t'].matmul(
                        self.ps[0:16, 2048 + bb * 1024 + c0:2048 + bb * 1024 + c0 + cw],
                        self.hsel[:, 16 * eg:16 * (eg + 1)], lv("tmpb", eg, bb)[:, c0:c0 + cw],
                        start=(eg == 0), stop=(eg == 7)),
                        inc=('t', 1))
        self.mark(f"qsmm_{qq}")
        self.wm('v', 't', f"qsmm_{qq}")
        for eg in range(8):
            E('v', lambda eg=eg: h['v'].tensor_mul(lv("tmpb", eg, 0), lv("tmpa", eg, 0),
                                                   lv("tmpa", eg, 1)), inc=('v', 1))
        E('v', lambda: h['v'].tensor_copy(self.qsq[:], self.ps[0:16, 2048:2048 + QB]),
          inc=('v', 1))
        E('v', lambda: h['v'].scalar_tensor_tensor(
            self.qsq[:], self.ps[0:16, 3072:3072 + QB], 1.0, self.qsq[:],
            ALU.mult, ALU.add), inc=('v', 1))
        E('v', lambda: h['v'].tensor_scalar_mul(self.qsq[:], self.qsq[:], 0.125),
          inc=('v', 1))
        self.mark(f"qd_{qq}")
        self.wm('t', 'v', f"qd_{qq}")
        for eg in range(8):
            for c0, cw in self.CHK:
                E('t', lambda eg=eg, c0=c0, cw=cw: h['t'].matmul(
                    self.ps[0:16, 2048 + c0:2048 + c0 + cw],
                    self.hsel[:, 16 * eg:16 * (eg + 1)], lv("tmpb", eg, 0)[:, c0:c0 + cw],
                    start=(eg == 0), stop=(eg == 7)),
                    inc=('t', 1))
        self.mark(f"qdmm_{qq}")
        self.wm('v', 't', f"qdmm_{qq}")
        E('v', lambda: h['v'].scalar_tensor_tensor(
            self.sigg[:], self.ps[0:16, 2048:2048 + QB], -0.25,
            self.qsq[:], ALU.mult, ALU.add), inc=('v', 1))
        self.mark(f"gv_{qq}")
        self.wm('s', 'v', f"gv_{qq}")
        E('s', lambda: h['s'].activation(self.sigb[:], self.sigg[:], AF.Sigmoid),
          inc=('s', 1))
        self.mark(f"sg_{qq}")
        for eg in range(8):
            if eg == 0:
                self.wm('t', 'v', f"qd_{qq}")
            else:
                self.wm('t', 'v', f"zev_{qq}_{eg-1}")
            for hh in (2 * eg, 2 * eg + 1):
                po, pc = 64 * (hh % 2), 64 * (hh // 2)
                for bb in range(2):
                    for c0, cw in self.CHK:
                        E('t', lambda eg=eg, hh=hh, po=po, pc=pc, bb=bb, c0=c0, cw=cw:
                            h['t'].matmul(
                                self.ps[po:po + 64, bb * 1024 + c0:bb * 1024 + c0 + cw],
                                self.mmat[po:po + 64, pc:pc + 64],
                                lv("tmpa", eg, bb)[po:po + 64, c0:c0 + cw],
                                start=True, stop=True),
                            inc=('t', 1))
            self.mark(f"zmm_{qq}_{eg}")
            self.wm('v', 't', f"zmm_{qq}_{eg}")
            E('v', lambda eg=eg: h['v'].tensor_copy(lv2("tmpb", eg), self.psv(0)),
              inc=('v', 1))
            self.mark(f"zev_{qq}_{eg}")
        for eg in range(8):
            self.wm('t', 'v', f"zev_{qq}_{eg}")
            self.wm('t', 's', f"sg_{qq}")
            if eg >= 1:
                self.wm('t', 'v', f"mx_{qq}_{eg-1}")
            for c0, cw in self.CHK:
                E('t', lambda eg=eg, c0=c0, cw=cw: h['t'].matmul(
                    self.ps[:, 3072 + c0:3072 + c0 + cw],
                    self.ssel[:, 128 * eg:128 * (eg + 1)], self.sigb[:, c0:c0 + cw],
                    start=True, stop=True), inc=('t', 1))
            self.mark(f"smm_{qq}_{eg}")
            self.wm('v', 't', f"smm_{qq}_{eg}")
            E('v', lambda eg=eg: h['v'].tensor_sub(lv("tmpa", eg, 0), lv("tmpb", eg, 0),
                                                   lv("tmpb", eg, 1)), inc=('v', 1))
            E('v', lambda eg=eg: h['v'].tensor_mul(lv("tmpa", eg, 0),
                                                   self.ps[:, 3072:3072 + QB],
                                                   lv("tmpa", eg, 0)), inc=('v', 1))
            E('v', lambda eg=eg: h['v'].tensor_sub(lv("tmpc", eg, 1), lv("tmpb", eg, 0),
                                                   lv("tmpa", eg, 0)), inc=('v', 1))
            E('v', lambda eg=eg: h['v'].tensor_add(lv("tmpc", eg, 0), lv("tmpb", eg, 1),
                                                   lv("tmpa", eg, 0)), inc=('v', 1))
            self.mark(f"mx_{qq}_{eg}")
        for eg in range(8):
            self.wm('t', 'v', f"mx_{qq}_7")
            if eg >= 2:
                self.wm('t', 'v', f"r3_{qq}_{eg-2}")
            base = 2048 * (eg % 2)
            for i in range(8):
                for bb in range(2):
                    for c0, cw in self.CHK:
                        E('t', lambda eg=eg, i=i, bb=bb, c0=c0, cw=cw, base=base:
                            h['t'].matmul(
                                self.ps[:, base + bb * 1024 + c0:base + bb * 1024 + c0 + cw],
                                self.b("owb")[:, 1024 * i + 128 * eg:1024 * i + 128 * (eg + 1)],
                                lv("tmpc", i, bb)[:, c0:c0 + cw], start=(i == 0), stop=(i == 7)),
                            inc=('t', 1))
            self.mark(f"owmm_{qq}_{eg}")
            self.wm('v', 't', f"owmm_{qq}_{eg}")
            E('v', lambda eg=eg, base=base: h['v'].scalar_tensor_tensor(
                lv2("rres", eg), self.psv(base), self.outbc[:, eg:eg + 1],
                lv2("rres", eg), ALU.add, ALU.add), inc=('v', 1))
            self.mark(f"r3_{qq}_{eg}")

    def _finish(self):
        self.W('sp', 'out', self.cnt['out'])


def _prep_core_inputs(k, L, inp):
    JS = L // NC
    TB = JS + 2
    x = np.asarray(inp["L0_attn_a_x"], np.float32)
    xT = np.ascontiguousarray(x.transpose(2, 0, 1).reshape(D, 2 * L))
    xr = np.zeros((D, 2, TB), np.float32)
    j0 = JS * k
    xr[:, :, 2:] = x[:, j0:j0 + JS, :].transpose(2, 0, 1)
    if k > 0:
        xr[:, :, 0:2] = x[:, j0 - 2:j0, :].transpose(2, 0, 1)
    xri = np.ascontiguousarray(
        x[:, j0:j0 + JS, :].transpose(2, 1, 0).reshape(D, 2 * JS))
    W_in = np.asarray(inp["L0_attn_a_W_in"], np.float32)
    cols = list(range(128 * k, 128 * (k + 1))) + \
           list(range(D + 128 * k, D + 128 * (k + 1))) + \
           list(range(2 * D, 2 * D + 64))
    winT = np.ascontiguousarray(W_in.T[:, cols])
    alog = np.asarray(inp["L0_attn_a_A_log"], np.float32)
    alogp = np.zeros((DS, NST), np.float32)
    for ch in range(DS):
        n_ = ch % 32
        dg = ch // 32
        for t in range(NST):
            alogp[ch, t] = alog[n_, 128 * k + 4 * t + dg]
    d = {
        "xT": xT, "xres": np.ascontiguousarray(xr.reshape(D, 2 * TB)),
        "xresi": xri,
        "winT": winT,
        "woutT": np.ascontiguousarray(np.asarray(inp["L0_attn_a_W_out"], np.float32).T),
        "wrT": np.ascontiguousarray(np.asarray(inp["L0_ffn_W_r"], np.float32).T),
        "wkT": np.ascontiguousarray(np.asarray(inp["L0_ffn_W_k"], np.float32).T),
        "wvT": np.ascontiguousarray(np.asarray(inp["L0_ffn_W_v"], np.float32).T),
        "qw": np.ascontiguousarray(
            np.asarray(inp["L0_attn_b_q_weight"], np.float32).reshape(D, D)),
        "vw": np.ascontiguousarray(
            np.asarray(inp["L0_attn_b_v_weight"], np.float32).reshape(D, D)),
        "owT": np.ascontiguousarray(np.asarray(inp["L0_attn_b_out_w"], np.float32).T),
        "outb": np.ascontiguousarray(
            np.asarray(inp["L0_attn_b_out_b"], np.float32).reshape(8, 128).T),
        "mur": np.ascontiguousarray(np.asarray(inp["L0_ffn_mu_r"], np.float32).reshape(8, 128).T),
        "muk": np.ascontiguousarray(np.asarray(inp["L0_ffn_mu_k"], np.float32).reshape(8, 128).T),
        "alogp": alogp,
        "dcol": np.ascontiguousarray(
            np.asarray(inp["L0_attn_a_D_param"], np.float32)[128 * k:128 * (k + 1)].reshape(DS, 1)),
        "convw": np.ascontiguousarray(
            np.asarray(inp["L0_attn_a_conv_w"], np.float32)[128 * k:128 * (k + 1), 0, :]),
        "convb": np.ascontiguousarray(
            np.asarray(inp["L0_attn_a_conv_b"], np.float32)[128 * k:128 * (k + 1)].reshape(DS, 1)),
        "leadmask": np.full((DS, 1), 0.0 if k == 0 else 1.0, np.float32),
    }
    ones = np.ones((128, 128), BF)
    ysel = np.zeros((128, 32 * 128), BF)
    for t in range(32):
        for ch in range(128):
            ysel[ch, 128 * t + 4 * t + ch // 32] = 1
    hsel = np.zeros((128, 8 * 16), BF)
    for eg in range(8):
        for p in range(128):
            hsel[p, 16 * eg + 2 * eg + p // 64] = 1
    ssel = np.zeros((16, 8 * 128), BF)
    for eg in range(8):
        for p in range(128):
            ssel[2 * eg + p // 64, 128 * eg + p] = 1
    dsel = np.zeros((128, 32 * 128), np.float16)
    for t in range(32):
        for ch in range(128):
            dsel[4 * t + ch // 32, 128 * t + ch] = 1
    d.update({"ones128": ones, "ysel": ysel, "hsel": hsel, "ssel": ssel,
              "dsel": dsel})
    return d


def assemble_output(results, L):
    JS = L // NC
    QW = min(256, JS)
    NQ = JS // QW
    out = np.zeros((2, L, D), np.float32)
    for k in range(NC):
        o = np.asarray(results[k]["out"]).reshape(D, NQ, 2 * QW)
        for qq in range(NQ):
            blk = o[:, qq, :]
            if qq == 0:
                b = blk.reshape(D, 2, QW)
                out[:, JS * k:JS * k + QW, :] = b.transpose(1, 2, 0)
            else:
                b = blk.reshape(D, QW, 2)
                out[:, JS * k + qq * QW:JS * k + (qq + 1) * QW, :] = \
                    b.transpose(2, 1, 0)
    return out


def kernel(**inputs):
    L = inputs["L0_attn_a_x"].shape[1]
    kb = KB(L=L, SUPW=min(2048, L))
    nc = kb.build()
    in_maps = [_prep_core_inputs(k, L, inputs) for k in range(NC)]
    res = run_bass_kernel_spmd(nc, in_maps, core_ids=list(range(NC)))
    return assemble_output(res.results, L)


# version 19 (final)
# speedup vs baseline: 1.0893x; 1.0527x over previous
import contextlib
import numpy as np
import ml_dtypes

import concourse.bass as bass
import concourse.mybir as mybir
from concourse.bass_utils import run_bass_kernel_spmd
from concourse.aot_env import allow_ici_all_to_all

F32 = mybir.dt.float32
BF16 = mybir.dt.bfloat16
F16 = mybir.dt.float16
U8 = mybir.dt.uint8
AF = mybir.ActivationFunctionType
ALU = mybir.AluOpType

NC = 8
D = 1024
DS = 128
NST = 32
LN_EPS = 1e-5
BF = np.dtype(ml_dtypes.bfloat16)


def vap(view, roff, shape_steps):
    rs = view.ap[0][0]
    dims = [[rs if s == 'p' else s, c] for s, c in shape_steps]
    return bass.AP(view.tensor, view.offset + roff[0] * rs + roff[1], dims)


class KB:
    def __init__(self, L=8192, SUPW=2048, dbg=0):
        assert L % SUPW == 0
        self.L = L
        self.SUPW = SUPW
        self.nsb = L // SUPW
        self.nsup = 2 * self.nsb
        self.JS = L // NC
        self.C = min(256, self.JS)
        self.ncb = self.JS // self.C
        self.ncall = 2 * self.ncb
        self.QW = self.C
        self.NQ = self.JS // self.QW
        self.QB = self.QW + 2
        self.W2 = 2 * self.QB
        self.CHK = [(c, min(512, self.QB - c)) for c in range(0, self.QB, 512)]
        self.PB = (0, 1024, 2048, 3072)
        self.CW = min(512, SUPW)
        self.NCH = SUPW // self.CW
        self.CW2 = min(1024, SUPW)
        self.NCH2 = SUPW // self.CW2
        self.dbg = dbg
        self.marks = {}
        self._gc = {}

    def E(self, eng, fn, inc=None):
        if eng in ("v", "s", "g") and self.eng == eng and self.cnt[eng] > 0:
            self.h[eng].wait_ge(self.sem[eng], self.cnt[eng])
        ins = fn() if self.eng == eng else None
        if inc is not None:
            s, n = inc
            if ins is not None:
                ins.then_inc(self.sem[s], n)
            self.cnt[s] += n
        return ins

    def W(self, eng, s, val):
        if self.eng == eng and val > 0:
            self.h[self.eng].wait_ge(self.sem[s], val)

    def mark(self, key):
        if self.pass0:
            assert key not in self.marks, key
            self.marks[key] = dict(self.cnt)

    def wm(self, eng, s, key):
        self.W(eng, s, self.marks[key][s])

    def sw(self, eng):
        self.W(eng, eng, self.cnt[eng])

    def build(self):
        nc = bass.Bass("TRN2", target_bir_lowering=False, debug=False)
        self.nc = nc
        L = self.L
        dt = nc.dram_tensor
        TB = self.JS + 2
        self.i = {}
        for name, shape in [
            ("xT", [D, 2 * L]), ("xres", [D, 2 * TB]), ("xresi", [D, 2 * self.JS]),
            ("winT", [D, 320]),
            ("woutT", [D, D]), ("wrT", [D, D]), ("wkT", [D, D]), ("wvT", [D, D]),
            ("qw", [D, D]), ("vw", [D, D]), ("owT", [D, D]),
            ("outb", [128, 8]), ("mur", [128, 8]), ("muk", [128, 8]),
            ("alogp", [DS, NST]), ("dcol", [DS, 1]), ("convw", [DS, 4]),
            ("convb", [DS, 1]), ("leadmask", [DS, 1]),
        ]:
            self.i[name] = dt(name, shape, F32, kind="ExternalInput").ap()
        for name, shape in [
            ("ones128", [128, 128]), ("ysel", [128, 32 * 128]),
            ("hsel", [128, 8 * 16]), ("ssel", [16, 8 * 128]),
        ]:
            self.i[name] = dt(name, shape, BF16, kind="ExternalInput").ap()
        self.i["dsel"] = dt("dsel", [128, 32 * 128], F16, kind="ExternalInput").ap()
        self.o_out = dt("out", [D, 2 * self.JS], F32, kind="ExternalOutput").ap()
        if self.dbg:
            self.o_z = dt("dbg_z", [DS, 2 * L], F32, kind="ExternalOutput").ap()
        self.a2ai = [dt(f"a2ai{c}", [NC, DS, self.C], BF16, kind="Internal").ap()
                     for c in range(self.ncall)]
        self.a2ao = [dt(f"a2ao{c}", [NC, DS, self.C], BF16, kind="Internal").ap()
                     for c in range(self.ncall)]
        self.leadi = dt("leadi", [NC, DS, 4], BF16, kind="Internal").ap()
        self.leado = dt("leado", [NC, DS, 4], BF16, kind="Internal").ap()

        es = contextlib.ExitStack()
        with es:
            self._alloc(nc, es)
            self.sem = {}
            for s in ("sp", "sd", "g", "t", "v", "s", "sm", "p0", "p1",
                      "bc", "stg0", "stg1", "rc", "out", "rep0", "rep1"):
                self.sem[s] = es.enter_context(nc.semaphore(s))
            with nc.Block() as blk:
                self.h = {}
                self.pass0 = True
                me = self

                def reg(name):
                    def f(hdl):
                        me.h[name] = hdl
                        me.eng = name
                        me.cnt = {k: 0 for k in (
                            "sp", "sd", "g", "t", "v", "s", "sm", "p0", "p1",
                            "bc", "stg0", "stg1", "rc", "out", "rep0", "rep1")}
                        me._walk()
                        me.pass0 = False
                    return f
                blk.sync(reg('sp'))
                blk.scalar(reg('s'))
                blk.vector(reg('v'))
                blk.tensor(reg('t'))
                blk.gpsimd(reg('g'))
        return nc

    def _alloc(self, nc, es):
        SUPW, W2 = self.SUPW, self.W2

        def sb(name, shape, dtp):
            return es.enter_context(nc.sbuf_tensor(name, shape, dtp))

        self.ones = sb("ones", [128, 128], BF16)
        self.ysel = sb("yselt", [128, 32 * 128], BF16)
        self.hsel = sb("hselt", [128, 8 * 16], BF16)
        self.ssel = sb("sselt", [16, 8 * 128], BF16)
        self.alogp = sb("alogpt", [DS, NST], F32)
        self.adev = sb("adevt", [DS, NST], F32)
        self.dcol = sb("dcolt", [DS, 1], F32)
        self.convw = sb("convwt", [DS, 4], F32)
        self.convb = sb("convbt", [DS, 1], F32)
        self.maskc = sb("maskct", [DS, 1], F32)
        self.mur = sb("murt", [128, 8], F32)
        self.muk = sb("mukt", [128, 8], F32)
        self.outbc = sb("outbt", [128, 8], F32)
        self.winT = sb("winTt", [128, 8 * 320], BF16)
        self.wsumrow = sb("wsumrowt", [1, 320], BF16)
        self.hcarry = sb("hcarryt", [128, 32], F32)
        self.zcol0 = sb("zcol0", [DS, 4], BF16)
        self.dsel = sb("dselt", [128, 32 * 128], F16)
        self.mmat = sb("mmatt", [128, 8 * 64], BF16)
        self.sigg = sb("siggt", [16, self.QB], F32)
        self.sigb = sb("sigbt", [16, self.QB], BF16)
        self.qsq = sb("qsqt", [16, self.QB], F32)
        self.mrepB = sb("mrepBt", [128, W2], BF16)
        self.rrepB = sb("rrepBt", [128, W2], BF16)
        self.vtmpB = sb("vtmpBt", [128, W2], F32)
        self.save2 = sb("save2t", [128, 16], BF16)
        self.save4 = sb("save4t", [128, 16], BF16)
        self.wstage = [sb(f"wstg{i}", [128, D], F32) for i in range(2)]
        self.vwblk = sb("vwblk", [128, D], BF16)
        self.macc = sb("macc", [128, 512], F32)
        self.cpool = sb("cpool", [128, 8], F32)
        nc.const_aps.register(F32, LN_EPS, self.cpool[:, 0:1])
        nc.const_aps.register(F32, -0.5, self.cpool[:, 1:2])
        nc.const_aps.register(F32, 1.0 / D, self.cpool[:, 2:3])
        nc.const_aps.register(F32, -1.0 / D, self.cpool[:, 3:4])
        self.xresb = [sb(f"xresb{i}", [128, W2], F32) for i in range(2)]

        layA, layB = {}, {}

        def mk(lay, name, p, cols, dtp, off):
            nb = cols * mybir.dt.size(dtp)
            lay[name] = (off, p, cols, dtp)
            return off + (nb + 63) // 64 * 64

        off = 0
        CW = self.CW
        for spec in [("xstage0", 128, SUPW, F32), ("xstage1", 128, SUPW, F32)] + \
                    [(f"xbf{i}", 128, SUPW, BF16) for i in range(8)] + \
                    [("xsq0", 128, self.CW2, BF16), ("xsq1", 128, self.CW2, BF16),
                     ("vtmp", 128, SUPW, F32),
                     ("uraw", DS, SUPW + 4, F32),
                     ("uconv0", DS, SUPW, F16), ("uconv1", DS, SUPW, F16),
                     ("dsp0", DS, SUPW, F16), ("dsp1", DS, SUPW, F16),
                     ("dtmp", DS, SUPW, F16),
                     ("bcbuf", 64, SUPW, BF16),
                     ("brep0", 128, SUPW, BF16), ("crep0", 128, SUPW, BF16),
                     ("brep1", 128, SUPW, BF16), ("crep1", 128, SUPW, BF16),
                     ("urep0", 128, SUPW, F16), ("urep1", 128, SUPW, F16),
                     ("lam0", 128, SUPW, F16), ("lam1", 128, SUPW, F16),
                     ("xb", 128, SUPW, BF16), ("hbuf", 128, SUPW, BF16),
                     ("hc0", 128, SUPW, BF16), ("hc1", 128, SUPW, BF16),
                     ("zsup", DS, SUPW, BF16),
                     ("negm", 1, SUPW, BF16), ("rrep", 128, SUPW, BF16),
                     ("mwork", 128, self.CW2, F32)]:
            off = mk(layA, *spec, off)
        endA = off
        off = 0
        for spec in [(w, 128, 8 * 1024, BF16)
                     for w in ("wout", "wr", "wk", "wv", "qwb", "owb")] + \
                    [("zrecv", 128, 8 * W2, BF16), ("rres", 128, 8 * W2, F32),
                     ("xn", 128, 8 * W2, BF16), ("tmpa", 128, 8 * W2, BF16),
                     ("tmpb", 128, 8 * W2, BF16), ("tmpc", 128, 8 * W2, BF16)]:
            off = mk(layB, *spec, off)
        endB = off
        self.ARENA_B = max(endA, endB)
        self.arena = es.enter_context(nc.sbuf_tensor("arena", [128, self.ARENA_B], U8))
        self._layA, self._layB = layA, layB
        self._vcache = {}
        self.ps = es.enter_context(nc.psum_tensor("psar", [128, 4096], F32))

    def _view(self, lay, name):
        key = (id(lay), name)
        if key not in self._vcache:
            off, p, cols, dtp = lay[name]
            esz = mybir.dt.size(dtp)
            assert off % esz == 0
            v = bass.AP(self.arena, 0,
                        [[self.ARENA_B, p], [1, cols * esz]])
            v = bass.AP(self.arena, off, [[self.ARENA_B, p], [1, cols * esz]]).bitcast(dtp)
            self._vcache[key] = v
        return self._vcache[key]

    def a(self, name):
        return self._view(self._layA, name)

    def b(self, name):
        return self._view(self._layB, name)

    def _walk(self):
        self._uidx = 0
        self._qidx = 0
        self._qc = {}
        self._startup()
        for sup in range(self.nsup):
            self._super(sup)
        self._exchange_and_wprep()
        for qq in range(self.NQ):
            if qq == 0:
                self._quarter(qq)
            else:
                self._quarterI(qq)
        self._finish()

    def _startup(self):
        E, h = self.E, self.h
        loads = [
            (self.ones[:], self.i["ones128"]), (self.ysel[:], self.i["ysel"]),
            (self.hsel[:], self.i["hsel"]), (self.ssel[:], self.i["ssel"]),
            (self.alogp[:], self.i["alogp"]), (self.dcol[:], self.i["dcol"]),
            (self.convw[:], self.i["convw"]), (self.convb[:], self.i["convb"]),
            (self.maskc[:], self.i["leadmask"]),
            (self.dsel[:], self.i["dsel"]),
            (self.mur[:], self.i["mur"]),
            (self.muk[:], self.i["muk"]),
            (self.outbc[:], self.i["outb"]),
        ]
        for dst, src in loads:
            E('sp', lambda d=dst, s=src: h['sp'].dma_start(out=d, in_=s), inc=('sm', 16))
        self.mark("smalls")
        for i in range(8):
            if i >= 2:
                self.wm('sp', 's', f"wic{i-2}")
            E('sp', lambda i=i: h['sp'].dma_start(
                out=self.a(f"xstage{i%2}")[:, 0:320],
                in_=self.i["winT"][128 * i:128 * (i + 1), :]), inc=(f"p{i%2}", 16))
            self.mark(f"wil{i}")
            self.wm('s', f"p{i%2}", f"wil{i}")
            E('s', lambda i=i: h['s'].copy(
                self.winT[:, 320 * i:320 * (i + 1)], self.a(f"xstage{i%2}")[:, 0:320]),
                inc=('s', 1))
            self.mark(f"wic{i}")
        self.wm('s', 'sm', "smalls")
        E('s', lambda: h['s'].activation(self.adev[:], self.alogp[:], AF.Exp), inc=('s', 1))
        self.mark("adevexp")
        self.wm('v', 's', "adevexp")
        E('v', lambda: h['v'].tensor_scalar_mul(self.adev[:], self.adev[:], -1.0), inc=('v', 1))
        pa = self._ugate()
        self.wm('t', 's', "wic7")
        self.wm('t', 'sm', "smalls")
        for i in range(8):
            E('t', lambda i=i, pa=pa: h['t'].matmul(
                self.ps[0:1, pa:pa + 320], self.ones[:, 0:1],
                self.winT[:, 320 * i:320 * (i + 1)],
                start=(i == 0), stop=(i == 7)), inc=('t', 1))
        self.mark("wsmm")
        self.wm('s', 't', "wsmm")
        E('s', lambda pa=pa: h['s'].copy(self.wsumrow[0:1, :], self.ps[0:1, pa:pa + 320]),
          inc=('s', 1))
        self.mark("wsumrow")
        self._udone('s', "wsumrow")
        E('v', lambda: h['v'].memset(self.cpool[:, 0:1], LN_EPS), inc=('v', 1))
        E('v', lambda: h['v'].memset(self.cpool[:, 1:2], -0.5), inc=('v', 1))
        E('v', lambda: h['v'].memset(self.cpool[:, 2:3], 1.0 / D), inc=('v', 1))
        E('v', lambda: h['v'].memset(self.cpool[:, 3:4], -1.0 / D), inc=('v', 1))
        E('v', lambda: h['v'].memset(self.zcol0[:], 0.0), inc=('v', 1))
        E('v', lambda: h['v'].memset(self.a("uraw")[:, 0:4], 0.0), inc=('v', 1))
        E('v', lambda: h['v'].memset(self.hcarry[:], 0.0), inc=('v', 1))
        self.mark("zc0")
        self.wm('sp', 'v', "zc0")
        E('sp', lambda: h['sp'].dma_start(out=self.leadi[0], in_=self.zcol0[:]),
          inc=('stg0', 16))
        self.mark("startup")
        for u in self._prep_units(0):
            u()
        self.mark("preamble")

    def _ugate(self):
        k = self._uidx
        if k >= 2:
            sem, name = self._gc[k - 2]
            self.wm('t', sem, name)
        return 2048 + (k % 2) * 1024

    def _udone(self, sem, name):
        self._gc[self._uidx] = (sem, name)
        self._uidx += 1

    def _prep_units(self, n):
        E, h, a = self.E, self.h, self.a
        SUPW, CW2, NCH2 = self.SUPW, self.CW2, self.NCH2
        par = n % 2
        nsub = CW2 // 512
        units = []

        def u_load(i):
            def f():
                if i >= 2:
                    self.wm('sp', 's', f"cast_{n}_{i-2}")
                elif n > 0:
                    self.wm('sp', 's', f"cast_{n-1}_{6+i}")
                else:
                    self.wm('sp', 's', f"wic{6+i}")
                E('sp', lambda i=i: h['sp'].dma_start(
                    out=a(f"xstage{i%2}")[:],
                    in_=self.i["xT"][128 * i:128 * (i + 1),
                                     n * SUPW:(n + 1) * SUPW]),
                    inc=(f"p{i%2}", 16))
                self.mark(f"xl_{n}_{i}")
                self.wm('s', f"p{i%2}", f"xl_{n}_{i}")
                if n > 0:
                    self.wm('s', 't', f"prj_{n-1}_b_{NCH2-1}")
                E('s', lambda i=i: h['s'].copy(a(f"xbf{i}")[:],
                                               a(f"xstage{i%2}")[:]), inc=('s', 1))
                self.mark(f"cast_{n}_{i}")
            return f
        for i in range(8):
            units.append(u_load(i))

        def u_sum(c):
            def f():
                pa = self._ugate()
                for i in range(8):
                    self.wm('t', 's', f"cast_{n}_{i}")
                    for sub in range(nsub):
                        E('t', lambda i=i, c=c, sub=sub, pa=pa: h['t'].matmul(
                            self.ps[:, pa + sub * 512:pa + sub * 512 + 512],
                            self.ones[:],
                            a(f"xbf{i}")[:, c * CW2 + sub * 512:
                                         c * CW2 + sub * 512 + 512],
                            start=(i == 0), stop=(i == 7)), inc=('t', 1))
                self.mark(f"summ_{n}_{c}")
                self.wm('s', 't', f"summ_{n}_{c}")
                if n == 0 and c == 0:
                    self.wm('s', 'v', "zc0")
                E('s', lambda c=c, pa=pa: h['s'].activation(
                    a("negm")[0:1, c * CW2:(c + 1) * CW2],
                    self.ps[0:1, pa:pa + CW2],
                    AF.Copy, scale=-1.0 / D), inc=('s', 1))
                self.mark(f"negm_{n}_{c}")
                E('s', lambda pa=pa: h['s'].activation(
                    a("mwork")[:], self.ps[:, pa:pa + CW2],
                    AF.Square, scale=1.0 / D), inc=('s', 1))
                self.mark(f"msq_{n}_{c}")
                self._udone('s', f"msq_{n}_{c}")
            return f

        def u_sumsq(c):
            def f():
                pa = self._ugate()
                for i in range(8):
                    g = c * 8 + i
                    if g >= 2:
                        pc, pi = divmod(g - 2, 8)
                        self.wm('s', 't', f"smm_{n}_{pc}_{pi}")
                    elif n > 0:
                        self.wm('s', 't', f"smm_{n-1}_{NCH2-1}_{6+g}")
                    E('s', lambda c=c, i=i: h['s'].activation(
                        a(f"xsq{i%2}")[:], a(f"xbf{i}")[:, c * CW2:(c + 1) * CW2],
                        AF.Square), inc=('s', 1))
                    self.mark(f"sq_{n}_{c}_{i}")
                    self.wm('t', 's', f"sq_{n}_{c}_{i}")
                    for sub in range(nsub):
                        E('t', lambda i=i, sub=sub, pa=pa: h['t'].matmul(
                            self.ps[:, pa + sub * 512:pa + sub * 512 + 512],
                            self.ones[:],
                            a(f"xsq{i%2}")[:, sub * 512:sub * 512 + 512],
                            start=(i == 0), stop=(i == 7)), inc=('t', 1))
                    self.mark(f"smm_{n}_{c}_{i}")
                self.wm('v', 't', f"smm_{n}_{c}_7")
                self.wm('v', 's', f"msq_{n}_{c}")
                E('v', lambda pa=pa: h['v'].scalar_tensor_tensor(
                    a("mwork")[:], self.ps[:, pa:pa + CW2], 1.0 / D,
                    a("mwork")[:], ALU.mult, ALU.subtract), inc=('v', 1))
                self.mark(f"vrow_{n}_{c}")
                self.wm('s', 'v', f"vrow_{n}_{c}")
                E('s', lambda: h['s'].activation(
                    a("mwork")[:], a("mwork")[:],
                    AF.Ln, bias=LN_EPS), inc=('s', 1))
                E('s', lambda c=c: h['s'].activation(
                    a("rrep")[:, c * CW2:(c + 1) * CW2], a("mwork")[:],
                    AF.Exp, scale=-0.5), inc=('s', 1))
                self.mark(f"rstd_{n}_{c}")
                self._udone('v', f"vrow_{n}_{c}")
            return f

        for c in range(NCH2):
            units.append(u_sum(c))
            units.append(u_sumsq(c))

        def u_proj(g, c0w, gw, c):
            def f():
                pa = self._ugate()
                for i in range(8):
                    self.wm('t', 's', f"cast_{n}_{i}")
                    for sub in range(nsub):
                        E('t', lambda i=i, c=c, sub=sub, pa=pa: h['t'].matmul(
                            self.ps[0:gw, pa + sub * 512:pa + sub * 512 + 512],
                            self.winT[:, 320 * i + c0w:320 * i + c0w + gw],
                            a(f"xbf{i}")[:, c * CW2 + sub * 512:
                                         c * CW2 + sub * 512 + 512],
                            start=(i == 0), stop=False), inc=('t', 1))
                self.wm('t', 's', f"negm_{n}_{c}")
                for sub in range(nsub):
                    E('t', lambda c=c, sub=sub, pa=pa: h['t'].matmul(
                        self.ps[0:gw, pa + sub * 512:pa + sub * 512 + 512],
                        self.wsumrow[0:1, c0w:c0w + gw],
                        a("negm")[0:1, c * CW2 + sub * 512:
                                  c * CW2 + sub * 512 + 512],
                        start=False, stop=True), inc=('t', 1))
                self.mark(f"prj_{n}_{g}_{c}")
                self.wm('v', 't', f"prj_{n}_{g}_{c}")
                self.wm('v', 's', f"rstd_{n}_{c}")
                if g == 'b':
                    if c == 0 and n > 0:
                        self.wm('v', 'bc', f"bcrep_{n-1}")
                    dst = a("bcbuf")[0:64, c * CW2:(c + 1) * CW2]
                elif g == 'u':
                    dst = a("uraw")[:, 4 + c * CW2:4 + (c + 1) * CW2]
                else:
                    dst = a("dtmp")[:, c * CW2:(c + 1) * CW2]
                E('v', lambda dst=dst, c=c, pa=pa: h['v'].tensor_mul(
                    dst, self.ps[0:gw, pa:pa + CW2],
                    a("rrep")[0:gw, c * CW2:(c + 1) * CW2]), inc=('v', 1))
                self.mark(f"dst_{n}_{g}_{c}")
                self._udone('v', f"dst_{n}_{g}_{c}")
                if g == 'd':
                    self.wm('s', 'v', f"dst_{n}_d_{c}")
                    E('s', lambda c=c: h['s'].activation(
                        a("vtmp")[:, c * CW2:(c + 1) * CW2],
                        a("dtmp")[:, c * CW2:(c + 1) * CW2], AF.Exp), inc=('s', 1))
                    if n >= 2:
                        self.wm('s', 't', f"Amm_{n-2}_31_d{NCH2-1}")
                    E('s', lambda c=c: h['s'].activation(
                        a(f"dsp{par}")[:, c * CW2:(c + 1) * CW2],
                        a("vtmp")[:, c * CW2:(c + 1) * CW2],
                        AF.Ln, bias=1.0), inc=('s', 1))
                    self.mark(f"sp_{n}_{c}")
            return f

        for (g, c0w, gw) in (('u', 0, 128), ('d', 128, 128), ('b', 256, 64)):
            for c in range(NCH2):
                units.append(u_proj(g, c0w, gw, c))

        def u_convA():
            def f():
                if n % self.nsb == 0:
                    E('v', lambda: h['v'].memset(a("uraw")[:, 1:4], 0.0),
                      inc=('v', 1))
                self.wm('v', 's', f"sp_{n}_{NCH2-1}")
                if n >= 2:
                    self.wm('v', 't', f"Amm_{n-2}_31_u{NCH2-1}")
                E('v', lambda: h['v'].tensor_scalar(
                    a("vtmp")[:, 0:SUPW], a("uraw")[:, 1:1 + SUPW],
                    self.convw[:, 0:1], self.convb[:, 0:1], ALU.mult, ALU.add),
                    inc=('v', 1))
                E('v', lambda: h['v'].scalar_tensor_tensor(
                    a("vtmp")[:, 0:SUPW], a("uraw")[:, 2:2 + SUPW],
                    self.convw[:, 1:2], a("vtmp")[:, 0:SUPW],
                    ALU.mult, ALU.add), inc=('v', 1))
            return f

        def u_convB():
            def f():
                E('v', lambda: h['v'].scalar_tensor_tensor(
                    a("vtmp")[:, 0:SUPW], a("uraw")[:, 3:3 + SUPW],
                    self.convw[:, 2:3], a("vtmp")[:, 0:SUPW],
                    ALU.mult, ALU.add), inc=('v', 1))
                E('v', lambda: h['v'].scalar_tensor_tensor(
                    a(f"uconv{par}")[:], a("uraw")[:, 4:4 + SUPW],
                    self.convw[:, 3:4], a("vtmp")[:, 0:SUPW],
                    ALU.mult, ALU.add), inc=('v', 1))
                E('v', lambda: h['v'].tensor_copy(
                    a("uraw")[:, 1:4], a("uraw")[:, 1 + SUPW:4 + SUPW]),
                    inc=('v', 1))
                self.mark(f"conv_{n}")
            return f

        def u_bcrep():
            def f():
                self.wm('sp', 'v', f"dst_{n}_b_{NCH2-1}")
                if n >= 2:
                    self.wm('sp', 'v', f"hC_{n-2}_31")
                for gi in range(4):
                    E('sp', lambda gi=gi: h['sp'].dma_start(
                        out=a(f"brep{par}")[32 * gi:32 * (gi + 1), :],
                        in_=a("bcbuf")[0:32, :]), inc=('bc', 16))
                    E('sp', lambda gi=gi: h['sp'].dma_start(
                        out=a(f"crep{par}")[32 * gi:32 * (gi + 1), :],
                        in_=a("bcbuf")[32:64, :]), inc=('bc', 16))
                self.mark(f"bcrep_{n}")
            return f

        units.append(u_convA())
        units.append(u_convB())
        units.append(u_bcrep())
        return units

    def _ymm(self, sup, t):
        E, h, a = self.E, self.h, self.a
        CW, NCH = self.CW, self.NCH
        self.wm('t', 'v', f"hC_{sup}_{t}")
        if t == 0 and sup > 0:
            self.wm('t', 'v', f"zev_{sup-1}")
        for c in range(NCH):
            E('t', lambda t=t, c=c: h['t'].matmul(
                self.ps[:, c * CW:(c + 1) * CW],
                self.ysel[:, 128 * t:128 * (t + 1)],
                a(f"hc{t%2}")[:, c * CW:(c + 1) * CW],
                start=(t == 0), stop=(t == 31)), inc=('t', 1))
        self.mark(f"ymm_{sup}_{t}")

    def _super(self, sup):
        E, h, a = self.E, self.h, self.a
        SUPW, CW2, NCH2 = self.SUPW, self.CW2, self.NCH2
        nsub = CW2 // 512
        par = sup % 2
        batch_start = (sup % self.nsb) == 0
        nt = 32
        sched = [[] for _ in range(nt)]
        if sup + 1 < self.nsup:
            units = self._prep_units(sup + 1)
            nu = len(units)
            for ui, u in enumerate(units):
                sched[min(nt - 4, 1 + (ui * 27) // max(1, nu - 1))].append(u)
        ucv, dspv = a(f"uconv{par}"), a(f"dsp{par}")
        for t in range(nt):
            buf = t % 2
            for kind, src_v in (('u', ucv), ('d', dspv)):
                for hh in range(NCH2):
                    pa = self._ugate()
                    if t == 0 and kind == 'u' and hh == 0:
                        self.wm('t', 'v', f"conv_{sup}")
                    if t == 0 and kind == 'd' and hh == 0:
                        self.wm('t', 's', f"sp_{sup}_{NCH2-1}")
                    for sub in range(nsub):
                        hs = hh * CW2 + sub * 512
                        src = vap(src_v, (0, hs), [['p', 128], [1, 512]])
                        E('t', lambda pa=pa, sub=sub, src=src, t=t: h['t'].matmul(
                            self.ps[:, pa + sub * 512:pa + sub * 512 + 512],
                            self.dsel[:, 128 * t:128 * (t + 1)],
                            src, start=True, stop=True), inc=('t', 1))
                    self.mark(f"Amm_{sup}_{t}_{kind}{hh}")
                    self.wm('s', 't', f"Amm_{sup}_{t}_{kind}{hh}")
                    if kind == 'u':
                        if t >= 2:
                            self.wm('s', 'v', f"xb_{sup}_{t-2}")
                        elif sup > 0:
                            self.wm('s', 'v', f"xb_{sup-1}_{30+t}")
                        E('s', lambda buf=buf, hh=hh, pa=pa: h['s'].copy(
                            a(f"urep{buf}")[:, hh * CW2:(hh + 1) * CW2],
                            self.ps[:, pa:pa + CW2]), inc=('s', 1))
                    else:
                        if t >= 2:
                            self.wm('s', 'v', f"scan_{sup}_{t-2}")
                        elif sup > 0:
                            self.wm('s', 'v', f"scan_{sup-1}_{30+t}")
                        E('s', lambda buf=buf, hh=hh, pa=pa, t=t: h['s'].activation(
                            a(f"lam{buf}")[:, hh * CW2:(hh + 1) * CW2],
                            self.ps[:, pa:pa + CW2],
                            AF.Exp, scale=self.adev[:, t:t + 1]), inc=('s', 1))
                    self.mark(f"Acons_{sup}_{t}_{kind}{hh}")
                    self._udone('s', f"Acons_{sup}_{t}_{kind}{hh}")
            self.wm('v', 's', f"Acons_{sup}_{t}_u{NCH2-1}")
            if t == 0:
                self.wm('v', 'bc', f"bcrep_{sup}")
            E('v', lambda buf=buf: h['v'].tensor_mul(
                a("xb")[:], a(f"urep{buf}")[:], a(f"brep{par}")[:]), inc=('v', 1))
            self.mark(f"xb_{sup}_{t}")
            if t >= 1:
                pb = (t - 1) % 2
                if t >= 3:
                    self.wm('v', 't', f"ymm_{sup}_{t-3}")
                elif sup > 0:
                    self.wm('v', 't', f"ymm_{sup-1}_{29+t}")
                E('v', lambda pb=pb: h['v'].tensor_mul(
                    a(f"hc{pb}")[:], a("hbuf")[:], a(f"crep{par}")[:]),
                    inc=('v', 1))
                self.mark(f"hC_{sup}_{t-1}")
            self.wm('v', 's', f"Acons_{sup}_{t}_d{NCH2-1}")
            init = 0.0 if batch_start else self.hcarry[:, t:t + 1]
            E('v', lambda buf=buf, init=init: h['v'].tensor_tensor_scan(
                a("hbuf")[:], a(f"lam{buf}")[:], a("xb")[:], init,
                ALU.mult, ALU.add), inc=('v', 1))
            self.mark(f"scan_{sup}_{t}")
            E('v', lambda t=t: h['v'].tensor_copy(
                self.hcarry[:, t:t + 1], a("hbuf")[:, SUPW - 1:SUPW]),
                inc=('v', 1))
            if t >= 2:
                self._ymm(sup, t - 2)
            for cl in sched[t]:
                cl()
        self.wm('v', 't', f"ymm_{sup}_29")
        E('v', lambda: h['v'].tensor_mul(
            a("hc1")[:], a("hbuf")[:], a(f"crep{par}")[:]), inc=('v', 1))
        self.mark(f"hC_{sup}_31")
        self._ymm(sup, 30)
        self._ymm(sup, 31)

        self.wm('v', 't', f"ymm_{sup}_31")
        if sup > 0:
            self.wm('v', f"stg{(sup-1) // self.nsb}", f"staged_{sup-1}")
        E('v', lambda: h['v'].scalar_tensor_tensor(
            a("zsup")[:], a(f"uconv{par}")[:], self.dcol[:, 0:1],
            self.ps[0:DS, 0:SUPW], ALU.mult, ALU.add), inc=('v', 1))
        self.mark(f"zev_{sup}")
        self.wm('sp', 'v', f"zev_{sup}")
        if self.dbg:
            E('sp', lambda: h['sp'].dma_start(
                out=self.o_z[:, sup * SUPW:(sup + 1) * SUPW], in_=a("zsup")[:]),
                inc=('out', 16))
        bidx = sup // self.nsb
        q = sup % self.nsb
        nslot2 = max(1, SUPW // self.JS)
        spl = SUPW // nslot2
        for mi in range(nslot2):
            m = nslot2 * q + mi
            for c in range(self.ncb):
                E('sp', lambda mi=mi, m=m, c=c: h['sp'].dma_start(
                    out=self.a2ai[bidx * self.ncb + c][m],
                    in_=a("zsup")[:, mi * spl + c * self.C: mi * spl + (c + 1) * self.C]),
                    inc=(f"stg{bidx}", 16))
        for mi in range(nslot2):
            m = nslot2 * q + mi + 1
            if m <= 7:
                E('sp', lambda mi=mi, m=m: h['sp'].dma_start(
                    out=self.leadi[m][:, 2 * bidx:2 * bidx + 2],
                    in_=a("zsup")[:, (mi + 1) * spl - 2:(mi + 1) * spl]),
                    inc=(f"stg{bidx}", 16))
        self.mark(f"staged_{sup}")

    def _exchange_and_wprep(self):
        E, h = self.E, self.h
        for c in range(self.ncall):
            bidx = c // self.ncb
            self.wm('g', f"stg{bidx}", f"staged_{(bidx + 1) * self.nsb - 1}")
            with allow_ici_all_to_all():
                E('g', lambda c=c: h['g'].collective_compute(
                    "AllToAll", ALU.bypass, replica_groups=[list(range(NC))],
                    ins=[self.a2ai[c]], outs=[self.a2ao[c]]), inc=('g', 1))
            self.mark(f"a2a_{c}")
        self.wm('g', 'stg0', f"staged_{self.nsup - 1}")
        self.wm('g', 'stg1', f"staged_{self.nsup - 1}")
        with allow_ici_all_to_all():
            E('g', lambda: h['g'].collective_compute(
                "AllToAll", ALU.bypass, replica_groups=[list(range(NC))],
                ins=[self.leadi], outs=[self.leado]), inc=('g', 1))
        self.mark("a2a_lead")

        wlist = [("wout", "woutT"), ("wr", "wrT"), ("wk", "wkT"), ("wv", "wvT"),
                 ("qwb", "qw"), ("owb", "owT")]
        n = 0
        for wname, iname in wlist:
            for i in range(8):
                if n >= 2:
                    self.wm('sp', 's', f"wc_{n-2}")
                E('sp', lambda iname=iname, i=i, n=n: h['sp'].dma_start(
                    out=self.wstage[n % 2][:],
                    in_=self.i[iname][128 * i:128 * (i + 1), :]), inc=(f"p{n%2}", 16))
                self.mark(f"wl_{n}")
                self.wm('s', f"p{n%2}", f"wl_{n}")
                if n == 0:
                    self.wm('s', 'stg0', f"staged_{self.nsup - 1}")
                    self.wm('s', 'stg1', f"staged_{self.nsup - 1}")
                    self.wm('s', 'v', f"zev_{self.nsup - 1}")
                    self.wm('s', 't', f"ymm_{self.nsup - 1}_31")
                E('s', lambda wname=wname, i=i, n=n: h['s'].copy(
                    self.b(wname)[:, 1024 * i:1024 * (i + 1)], self.wstage[n % 2][:]),
                    inc=('s', 1))
                self.mark(f"wc_{n}")
                n += 1
        self.mark("wprep")
        for i in range(8):
            if n >= 2:
                self.wm('sp', 's', f"wc_{n-2}")
            E('sp', lambda i=i, n=n: h['sp'].dma_start(
                out=self.wstage[n % 2][:],
                in_=self.i["vw"][128 * i:128 * (i + 1), :]), inc=(f"p{n%2}", 16))
            self.mark(f"wl_{n}")
            self.wm('s', f"p{n%2}", f"wl_{n}")
            if i > 0:
                self.wm('s', 't', f"mmmb_{i-1}")
            E('s', lambda n=n: h['s'].copy(self.vwblk[:], self.wstage[n % 2][:]),
              inc=('s', 1))
            self.mark(f"wc_{n}")
            self.wm('t', 's', f"wc_{n}")
            if i > 0:
                self.wm('t', 'v', f"mac_{i-1}")
            for hh in range(16):
                po, pc = 64 * (hh % 2), 64 * (hh // 2)
                E('t', lambda hh=hh, i=i, po=po, pc=pc: h['t'].matmul(
                    self.ps[po:po + 64, 3584 + pc:3584 + pc + 64],
                    self.b("qwb")[:, 1024 * i + 64 * hh:1024 * i + 64 * (hh + 1)],
                    self.vwblk[:, 64 * hh:64 * (hh + 1)],
                    start=True, stop=True),
                    inc=('t', 1))
            self.mark(f"mmmb_{i}")
            self.wm('v', 't', f"mmmb_{i}")
            if i == 0:
                E('v', lambda: h['v'].tensor_copy(self.macc[:], self.ps[:, 3584:4096]),
                  inc=('v', 1))
            else:
                E('v', lambda: h['v'].tensor_add(self.macc[:], self.macc[:],
                                                 self.ps[:, 3584:4096]), inc=('v', 1))
            self.mark(f"mac_{i}")
            n += 1
        self.mark("mmm")
        self.wm('v', 't', "mmm")
        E('v', lambda: h['v'].tensor_scalar_mul(self.mmat[:], self.macc[:],
                                                0.125), inc=('v', 1))
        for tname in ("tmpa", "tmpb", "tmpc"):
            E('v', lambda tname=tname: h['v'].memset(self.b(tname)[:], 0.0),
              inc=('v', 1))
        self.mark("mmat")

    def lv(self, t, eg, bb=None):
        base = self.b(t)
        W2, QB = self.W2, self.QB
        if bb is None:
            return base[:, eg * W2:(eg + 1) * W2]
        return base[:, eg * W2 + bb * QB:eg * W2 + bb * QB + QB]

    def lv2(self, t, eg):
        v = self.b(t)
        return vap(v, (0, eg * self.W2), [['p', 128], [self.QB, 2], [1, self.QB]])

    def psv(self, col0):
        return bass.AP(self.ps, col0, [[4096, 128], [1024, 2], [1, self.QB]])

    def _quarter(self, qq):
        E, h = self.E, self.h
        QB, QW, W2 = self.QB, self.QW, self.W2
        lv, lv2 = self.lv, self.lv2
        zr = self.b("zrecv")

        ncq = self.QW // self.C
        for bb in range(2):
            for u in range(ncq):
                c = bb * self.ncb + qq * ncq + u
                self.wm('sp', 'g', f"a2a_{c}")
                if qq == 0 and u == 0:
                    self.wm('sp', 'g', "a2a_lead")
                if qq > 0 and u == 0:
                    self.wm('sp', 't', f"opmm_{qq-1}_7")
                dst = vap(zr, (0, bb * QB + 2 + u * self.C),
                          [['p', 128], [W2, 8], [1, self.C]])
                E('sp', lambda c=c, dst=dst: h['sp'].dma_start(
                    out=dst, in_=self.a2ao[c].rearrange("s p t -> p s t")), inc=('rc', 16))
            dl = vap(zr, (0, bb * QB), [['p', 128], [W2, 8], [1, 2]])
            if qq == 0:
                E('sp', lambda dl=dl, bb=bb: h['sp'].dma_start(
                    out=dl,
                    in_=self.leado.rearrange("s p t -> p s t")[:, :, 2 * bb:2 * bb + 2]),
                    inc=('rc', 16))
            else:
                cprev = bb * self.ncb + qq * ncq - 1
                E('sp', lambda cprev=cprev, dl=dl: h['sp'].dma_start(
                    out=dl,
                    in_=self.a2ao[cprev].rearrange("s p t -> p s t")[:, :, self.C - 2:self.C]),
                    inc=('rc', 16))
        self.mark(f"zr_{qq}")

        for eg in range(8):
            if eg >= 2:
                self.wm('sp', 'v', f"r1_{qq}_{eg-2}")
            elif qq > 0:
                self.wm('sp', 'v', f"r1_{qq-1}_{6+eg}")
            else:
                self.wm('sp', 's', "wc_55")
            src = bass.AP(self.i["xres"].tensor,
                          128 * eg * 2 * (self.JS + 2) + qq * QW,
                          [[2 * (self.JS + 2), 128], [(self.JS + 2), 2], [1, QB]])
            E('sp', lambda src=src, eg=eg: h['sp'].dma_start(
                out=self.xresb[eg % 2][:].rearrange("p (b t) -> p b t", b=2), in_=src),
                inc=(f"p{eg%2}", 16))
            self.mark(f"xr_{qq}_{eg}")
            self.wm('t', 'rc', f"zr_{qq}")
            if qq == 0 and eg == 0:
                self.wm('t', 's', "wprep")
            if eg >= 1:
                self.wm('t', 'v', f"r1_{qq}_{eg-1}")
            elif qq > 0:
                self.wm('t', 'v', f"r1_{qq-1}_7")
            for r in range(8):
                for bb in range(2):
                    for c0, cw in self.CHK:
                        E('t', lambda eg=eg, r=r, bb=bb, c0=c0, cw=cw: h['t'].matmul(
                            self.ps[:, bb * 1024 + c0:bb * 1024 + c0 + cw],
                            self.b("wout")[:, 1024 * r + 128 * eg:1024 * r + 128 * (eg + 1)],
                            lv("zrecv", r, bb)[:, c0:c0 + cw],
                            start=(r == 0), stop=(r == 7)),
                            inc=('t', 1))
            self.mark(f"opmm_{qq}_{eg}")
            self.wm('v', 't', f"opmm_{qq}_{eg}")
            self.wm('v', f"p{eg%2}", f"xr_{qq}_{eg}")
            if qq == 0 and eg == 0:
                self.wm('v', 's', "wprep")
            if qq > 0 and eg == 0:
                self.wm('v', 'out', f"out_{qq-1}")
            E('v', lambda eg=eg: h['v'].tensor_add(
                lv2("rres", eg), self.xresb[eg % 2][:].rearrange("p (b t) -> p b t", b=2),
                self.psv(0)), inc=('v', 1))
            self.mark(f"r1_{qq}_{eg}")

        self._layer_norm(qq, "l2", f"r1_{qq}_7")
        self._save_old(qq, self.save2)
        self._ffn(qq, "f1", "l2")
        self._layer_norm(qq, "l3", f"fo_{qq}_f1_7")
        self._attnb(qq)
        self._layer_norm(qq, "l4", f"r3_{qq}_7")
        self._save_old(qq, self.save4)
        self._ffn(qq, "f2", "l4")

        for eg in range(8):
            self.wm('sp', 'v', f"fo_{qq}_f2_{eg}")
            rr = self.b("rres")
            src = vap(rr, (0, eg * W2 + 2), [['p', 128], [QB, 2], [1, QW]])
            dst = bass.AP(self.o_out.tensor, 128 * eg * 2 * self.JS + qq * 2 * QW,
                          [[2 * self.JS, 128], [QW, 2], [1, QW]])
            E('sp', lambda src=src, dst=dst: h['sp'].dma_start(out=dst, in_=src),
              inc=('out', 16))
        self.mark(f"out_{qq}")

    def _save_old(self, qq, dstbuf):
        if self.NQ <= 1:
            return
        E, h = self.E, self.h
        xnv = self.b("xn")
        src = vap(xnv, (0, self.QB - 1), [['p', 128], [self.W2, 8], [self.QB, 2]])
        E('v', lambda src=src, dstbuf=dstbuf: h['v'].tensor_copy(
            dstbuf[:].rearrange("p (a b) -> p a b", a=8), src), inc=('v', 1))

    def rl(self, t, eg):
        return self.b(t)[:, eg * self.W2 + 4:eg * self.W2 + 4 + 2 * self.QW]

    def rl2(self, t, eg, b):
        return vap(self.b(t), (0, eg * self.W2 + 4 + b),
                   [['p', 128], [2, self.QW]])

    def shl(self, t, eg):
        return self.b(t)[:, eg * self.W2 + 2:eg * self.W2 + 2 + 2 * self.QW]

    def _stgate(self, extra=None):
        k = self._qidx
        if k >= 4:
            sem, name = self._qc[k - 4]
            self.wm('t', sem, name)
        if extra is not None:
            sem, name = extra
            self.wm('t', sem, name)
        return 512 * (k % 4)

    def _stdone(self, sem, name):
        self._qc[self._qidx] = (sem, name)
        self._qidx += 1

    def _quarterI(self, qq):
        E, h = self.E, self.h
        QB, QW, W2 = self.QB, self.QW, self.W2
        Q2 = 2 * QW
        zr = self.b("zrecv")
        if self._qidx == 0:
            for k in range(4):
                self._qc[k] = ('v', f"fo_{qq-1}_f2_6")
            self._qidx = 4

        ncq = QW // self.C
        for bb in range(2):
            for u in range(ncq):
                c = bb * self.ncb + qq * ncq + u
                self.wm('sp', 'g', f"a2a_{c}")
                if u == 0 and bb == 0:
                    self.wm('sp', 't', f"opmm_{qq-1}_7")
                dst = vap(zr, (0, 4 + bb * QW + u * self.C),
                          [['p', 128], [W2, 8], [1, self.C]])
                E('sp', lambda c=c, dst=dst: h['sp'].dma_start(
                    out=dst, in_=self.a2ao[c].rearrange("s p t -> p s t")),
                    inc=('rc', 16))
        self.mark(f"zr_{qq}")

        for eg in range(8):
            if eg >= 2:
                self.wm('sp', 'v', f"r1_{qq}_{eg-2}")
            else:
                self.wm('sp', 'v', f"r1_{qq-1}_{6+eg}")
            src = bass.AP(self.i["xresi"].tensor,
                          128 * eg * 2 * self.JS + qq * Q2,
                          [[2 * self.JS, 128], [1, Q2]])
            E('sp', lambda src=src, eg=eg: h['sp'].dma_start(
                out=self.xresb[eg % 2][:, 0:Q2], in_=src), inc=(f"p{eg%2}", 16))
            self.mark(f"xr_{qq}_{eg}")
            base = self._stgate(extra=('rc', f"zr_{qq}"))
            for r in range(8):
                E('t', lambda eg=eg, r=r, base=base: h['t'].matmul(
                    self.ps[:, base:base + Q2],
                    self.b("wout")[:, 1024 * r + 128 * eg:1024 * r + 128 * (eg + 1)],
                    self.rl("zrecv", r), start=(r == 0), stop=(r == 7)),
                    inc=('t', 1))
            self.mark(f"opmm_{qq}_{eg}")
            self.wm('v', 't', f"opmm_{qq}_{eg}")
            self.wm('v', f"p{eg%2}", f"xr_{qq}_{eg}")
            if eg == 0:
                self.wm('v', 'out', f"out_{qq-1}")
            rr = self.b("rres")
            out3 = vap(rr, (0, eg * W2 + 4), [['p', 128], [1, 2], [2, QW]])
            xb3 = bass.AP(self.xresb[eg % 2], 0, [[W2, 128], [1, 2], [2, QW]])
            ps3 = bass.AP(self.ps, base, [[4096, 128], [QW, 2], [1, QW]])
            E('v', lambda out3=out3, xb3=xb3, ps3=ps3: h['v'].tensor_add(
                out3, xb3, ps3), inc=('v', 1))
            self.mark(f"r1_{qq}_{eg}")
            self._stdone('v', f"r1_{qq}_{eg}")

        self._lnI(qq, "l2", lambda eg: f"r1_{qq}_{eg}",
                  pb4=('s', f"lnm_l4_{qq-1}"), pb5=('v', f"lnv_l4_{qq-1}"),
                  srbuf=self.save2)
        self._ffnI(qq, "f1", "l2")
        self._lnI(qq, "l3", lambda eg: f"fo_{qq}_f1_{eg}",
                  pb4=('s', f"lnm_l2_{qq}"), pb5=('v', f"lnv_l2_{qq}"))
        self._attnbI(qq)
        self._lnI(qq, "l4", lambda eg: f"r3_{qq}_{eg}",
                  pb4=('v', f"zev_{qq}_6"), pb5=('v', f"zev_{qq}_7"),
                  srbuf=self.save4)
        self._ffnI(qq, "f2", "l4")

        for eg in range(8):
            self.wm('sp', 'v', f"fo_{qq}_f2_{eg}")
            src = self.rl("rres", eg)
            dst = bass.AP(self.o_out.tensor, 128 * eg * 2 * self.JS + qq * Q2,
                          [[2 * self.JS, 128], [1, Q2]])
            E('sp', lambda src=src, dst=dst: h['sp'].dma_start(out=dst, in_=src),
              inc=('out', 16))
        self.mark(f"out_{qq}")

    def _lnI(self, qq, tag, vdep, pb4, pb5, srbuf=None):
        E, h = self.E, self.h
        QW, W2 = self.QW, self.W2
        Q2 = 2 * QW
        rl = self.rl
        tg = f"{tag}_{qq}"
        for eg in range(8):
            self.wm('s', 'v', vdep(eg))
            E('s', lambda eg=eg: h['s'].copy(rl("tmpa", eg), rl("rres", eg)),
              inc=('s', 1))
            E('s', lambda eg=eg: h['s'].activation(rl("tmpb", eg), rl("tmpa", eg),
                                                   AF.Square), inc=('s', 1))
            self.mark(f"lncg_{tg}_{eg}")
            self.wm('t', 's', f"lncg_{tg}_{eg}")
            if eg == 0:
                self.wm('t', pb4[0], pb4[1])
                self.wm('t', pb5[0], pb5[1])
            E('t', lambda eg=eg: h['t'].matmul(
                self.ps[:, 2048:2048 + Q2], self.ones[:], rl("tmpa", eg),
                start=(eg == 0), stop=(eg == 7)), inc=('t', 1))
            E('t', lambda eg=eg: h['t'].matmul(
                self.ps[:, 2560:2560 + Q2], self.ones[:], rl("tmpb", eg),
                start=(eg == 0), stop=(eg == 7)), inc=('t', 1))
        self.mark(f"lnmm_{tg}")
        self.wm('s', 't', f"lnmm_{tg}")
        E('s', lambda: h['s'].activation(
            self.mrepB[:, 0:Q2], self.ps[:, 2048:2048 + Q2],
            AF.Copy, scale=1.0 / D), inc=('s', 1))
        E('s', lambda: h['s'].activation(self.vtmpB[:, 0:Q2], self.mrepB[:, 0:Q2],
                                         AF.Square), inc=('s', 1))
        self.mark(f"lnm_{tg}")
        self.wm('v', 's', f"lnm_{tg}")
        self.wm('v', 't', f"lnmm_{tg}")
        E('v', lambda: h['v'].scalar_tensor_tensor(
            self.vtmpB[:, 0:Q2], self.ps[:, 2560:2560 + Q2], 1.0 / D,
            self.vtmpB[:, 0:Q2], ALU.mult, ALU.subtract), inc=('v', 1))
        self.mark(f"lnv_{tg}")
        self.wm('s', 'v', f"lnv_{tg}")
        E('s', lambda: h['s'].activation(self.vtmpB[:, 0:Q2], self.vtmpB[:, 0:Q2],
                                         AF.Ln, bias=LN_EPS), inc=('s', 1))
        E('s', lambda: h['s'].activation(self.rrepB[:, 0:Q2], self.vtmpB[:, 0:Q2],
                                         AF.Exp, scale=-0.5), inc=('s', 1))
        self.mark(f"lnr_{tg}")
        self.wm('v', 's', f"lnr_{tg}")
        for eg in range(8):
            E('v', lambda eg=eg: h['v'].tensor_sub(rl("tmpa", eg), rl("rres", eg),
                                                   self.mrepB[:, 0:Q2]), inc=('v', 1))
            E('v', lambda eg=eg: h['v'].tensor_mul(rl("xn", eg), rl("tmpa", eg),
                                                   self.rrepB[:, 0:Q2]), inc=('v', 1))
            self.mark(f"lnxn_{tg}_{eg}")
        if srbuf is not None:
            xnv = self.b("xn")
            dstl = vap(xnv, (0, 2), [['p', 128], [W2, 8], [1, 2]])
            E('v', lambda dstl=dstl, srbuf=srbuf: h['v'].tensor_copy(
                dstl, srbuf[:].rearrange("p (a b) -> p a b", a=8)), inc=('v', 1))
            srcl = vap(xnv, (0, 2 + Q2 - 2), [['p', 128], [W2, 8], [1, 2]])
            E('v', lambda srcl=srcl, srbuf=srbuf: h['v'].tensor_copy(
                srbuf[:].rearrange("p (a b) -> p a b", a=8), srcl), inc=('v', 1))
        self.mark(f"ln_{tg}")

    def _ffnI(self, qq, tag, lntag):
        E, h = self.E, self.h
        QW, W2 = self.QW, self.W2
        Q2 = 2 * QW
        rl, shl = self.rl, self.shl
        tg = f"{qq}_{tag}"
        self.wm('v', 't', f"lnmm_{lntag}_{qq}")
        for eg in range(8):
            E('v', lambda eg=eg: h['v'].tensor_sub(
                rl("tmpa", eg), rl("xn", eg), shl("xn", eg)), inc=('v', 1))
            E('v', lambda eg=eg: h['v'].scalar_tensor_tensor(
                rl("tmpb", eg), rl("tmpa", eg),
                self.mur[:, eg:eg + 1], shl("xn", eg), ALU.mult, ALU.add),
                inc=('v', 1))
            E('v', lambda eg=eg: h['v'].scalar_tensor_tensor(
                rl("tmpc", eg), rl("tmpa", eg),
                self.muk[:, eg:eg + 1], shl("xn", eg), ALU.mult, ALU.add),
                inc=('v', 1))
            self.mark(f"mixe_{tg}_{eg}")
        self.mark(f"mix_{tg}")
        for eg in range(8):
            base = self._stgate()
            for i in range(8):
                self.wm('t', 'v', f"mixe_{tg}_{i}")
                E('t', lambda eg=eg, i=i, base=base: h['t'].matmul(
                    self.ps[:, base:base + Q2],
                    self.b("wr")[:, 1024 * i + 128 * eg:1024 * i + 128 * (eg + 1)],
                    rl("tmpb", i), start=(i == 0), stop=(i == 7)), inc=('t', 1))
            self.mark(f"rmm_{tg}_{eg}")
            self.wm('s', 't', f"rmm_{tg}_{eg}")
            E('s', lambda eg=eg, base=base: h['s'].activation(
                rl("tmpa", eg), self.ps[:, base:base + Q2],
                AF.Sigmoid), inc=('s', 1))
            self.mark(f"rg_{tg}_{eg}")
            self._stdone('s', f"rg_{tg}_{eg}")
        for eg in range(8):
            base = self._stgate()
            for i in range(8):
                self.wm('t', 'v', f"mixe_{tg}_{i}")
                E('t', lambda eg=eg, i=i, base=base: h['t'].matmul(
                    self.ps[:, base:base + Q2],
                    self.b("wk")[:, 1024 * i + 128 * eg:1024 * i + 128 * (eg + 1)],
                    rl("tmpc", i), start=(i == 0), stop=(i == 7)), inc=('t', 1))
            self.mark(f"kmm_{tg}_{eg}")
            self.wm('v', 't', f"kmm_{tg}_{eg}")
            if eg == 0:
                self.wm('v', 't', f"rmm_{tg}_7")
            E('v', lambda eg=eg, base=base: h['v'].tensor_scalar_max(
                rl("tmpb", eg), self.ps[:, base:base + Q2], 0.0), inc=('v', 1))
            self.mark(f"kr_{tg}_{eg}")
            self.wm('s', 'v', f"kr_{tg}_{eg}")
            E('s', lambda eg=eg: h['s'].activation(rl("xn", eg), rl("tmpb", eg),
                                                   AF.Square), inc=('s', 1))
            self.mark(f"k2_{tg}_{eg}")
            self._stdone('v', f"kr_{tg}_{eg}")
        for eg in range(8):
            base = self._stgate()
            for i in range(8):
                self.wm('t', 's', f"k2_{tg}_{i}")
                E('t', lambda eg=eg, i=i, base=base: h['t'].matmul(
                    self.ps[:, base:base + Q2],
                    self.b("wv")[:, 1024 * i + 128 * eg:1024 * i + 128 * (eg + 1)],
                    rl("xn", i), start=(i == 0), stop=(i == 7)), inc=('t', 1))
            self.mark(f"vmm_{tg}_{eg}")
            self.wm('v', 't', f"vmm_{tg}_{eg}")
            if eg == 0:
                self.wm('v', 't', f"kmm_{tg}_7")
            E('v', lambda eg=eg, base=base: h['v'].tensor_mul(
                rl("tmpc", eg), rl("tmpa", eg),
                self.ps[:, base:base + Q2]), inc=('v', 1))
            E('v', lambda eg=eg: h['v'].tensor_add(rl("rres", eg), rl("rres", eg),
                                                   rl("tmpc", eg)), inc=('v', 1))
            self.mark(f"fo_{qq}_{tag}_{eg}")
            self._stdone('v', f"fo_{qq}_{tag}_{eg}")

    def _attnbI(self, qq):
        E, h = self.E, self.h
        QW, W2 = self.QW, self.W2
        Q2 = 2 * QW
        rl, rl2 = self.rl, self.rl2
        for eg in range(8):
            base = self._stgate()
            for i in range(8):
                self.wm('t', 'v', f"lnxn_l3_{qq}_{i}")
                E('t', lambda eg=eg, i=i, base=base: h['t'].matmul(
                    self.ps[:, base:base + Q2],
                    self.b("qwb")[:, 1024 * i + 128 * eg:1024 * i + 128 * (eg + 1)],
                    rl("xn", i), start=(i == 0), stop=(i == 7)), inc=('t', 1))
            self.mark(f"qmm_{qq}_{eg}")
            self.wm('v', 't', f"qmm_{qq}_{eg}")
            E('v', lambda eg=eg, base=base: h['v'].tensor_copy(
                rl("tmpa", eg), self.ps[:, base:base + Q2]), inc=('v', 1))
            self.mark(f"qev_{qq}_{eg}")
            self.wm('s', 'v', f"qev_{qq}_{eg}")
            E('s', lambda eg=eg: h['s'].activation(rl("tmpb", eg), rl("tmpa", eg),
                                                   AF.Square), inc=('s', 1))
            self.mark(f"q2_{qq}_{eg}")
            self._stdone('v', f"qev_{qq}_{eg}")
        for eg in range(8):
            self.wm('t', 's', f"q2_{qq}_{eg}")
            if eg == 0:
                self.wm('t', 'v', f"mx_{qq-1}_7")
            E('t', lambda eg=eg: h['t'].matmul(
                self.ps[0:16, 3072:3072 + Q2],
                self.hsel[:, 16 * eg:16 * (eg + 1)], rl("tmpb", eg),
                start=(eg == 0), stop=(eg == 7)), inc=('t', 1))
        self.mark(f"qsmm_{qq}")
        self.wm('v', 't', f"qsmm_{qq}")
        for eg in range(8):
            E('v', lambda eg=eg: h['v'].tensor_mul(
                rl2("tmpb", eg, 0), rl2("tmpa", eg, 0), rl2("tmpa", eg, 1)),
                inc=('v', 1))
        E('v', lambda: h['v'].tensor_copy(
            self.qsq[:, 0:QW], bass.AP(self.ps, 3072, [[4096, 16], [2, QW]])),
            inc=('v', 1))
        E('v', lambda: h['v'].scalar_tensor_tensor(
            self.qsq[:, 0:QW], bass.AP(self.ps, 3073, [[4096, 16], [2, QW]]),
            1.0, self.qsq[:, 0:QW], ALU.mult, ALU.add), inc=('v', 1))
        E('v', lambda: h['v'].tensor_scalar_mul(self.qsq[:, 0:QW],
                                                self.qsq[:, 0:QW], 0.125),
          inc=('v', 1))
        self.mark(f"qd_{qq}")
        self.wm('t', 'v', f"qd_{qq}")
        self.wm('t', 'v', f"gv_{qq-1}")
        for eg in range(8):
            E('t', lambda eg=eg: h['t'].matmul(
                self.ps[0:16, 3584:3584 + QW],
                self.hsel[:, 16 * eg:16 * (eg + 1)], rl2("tmpb", eg, 0),
                start=(eg == 0), stop=(eg == 7)), inc=('t', 1))
        self.mark(f"qdmm_{qq}")
        self.wm('v', 't', f"qdmm_{qq}")
        E('v', lambda: h['v'].scalar_tensor_tensor(
            self.sigg[:, 0:QW], self.ps[0:16, 3584:3584 + QW], -0.25,
            self.qsq[:, 0:QW], ALU.mult, ALU.add), inc=('v', 1))
        self.mark(f"gv_{qq}")
        self.wm('s', 'v', f"gv_{qq}")
        E('s', lambda: h['s'].activation(self.sigb[:, 0:QW], self.sigg[:, 0:QW],
                                         AF.Sigmoid), inc=('s', 1))
        self.mark(f"sg_{qq}")
        for eg in range(8):
            if eg == 0:
                self.wm('t', 'v', f"qd_{qq}")
                self.wm('t', 'v', "mmat")
                self.wm('t', 's', f"lnm_l3_{qq}")
                self.wm('t', 'v', f"lnv_l3_{qq}")
            else:
                self.wm('t', 'v', f"zev_{qq}_{eg-1}")
            zb = 2048 + 512 * (eg % 2)
            for hh in (2 * eg, 2 * eg + 1):
                po, pc = 64 * (hh % 2), 64 * (hh // 2)
                E('t', lambda hh=hh, po=po, pc=pc, eg=eg, zb=zb: h['t'].matmul(
                    self.ps[po:po + 64, zb:zb + Q2],
                    self.mmat[po:po + 64, pc:pc + 64],
                    self.rl("tmpa", eg)[po:po + 64, :],
                    start=True, stop=True), inc=('t', 1))
            self.mark(f"zmm_{qq}_{eg}")
            self.wm('v', 't', f"zmm_{qq}_{eg}")
            if eg == 0:
                self.wm('v', 't', f"qdmm_{qq}")
            E('v', lambda eg=eg, zb=zb: h['v'].tensor_copy(
                rl("tmpb", eg), self.ps[:, zb:zb + Q2]), inc=('v', 1))
            self.mark(f"zev_{qq}_{eg}")
        for eg in range(8):
            self.wm('t', 'v', f"zev_{qq}_{eg}")
            self.wm('t', 's', f"sg_{qq}")
            if eg >= 1:
                self.wm('t', 'v', f"mx_{qq}_{eg-1}")
            elif qq >= 2:
                self.wm('t', 'v', f"mx_{qq-1}_7")
            E('t', lambda eg=eg: h['t'].matmul(
                self.ps[:, 3840:3840 + QW],
                self.ssel[:, 128 * eg:128 * (eg + 1)], self.sigb[:, 0:QW],
                start=True, stop=True), inc=('t', 1))
            self.mark(f"smm_{qq}_{eg}")
            self.wm('v', 't', f"smm_{qq}_{eg}")
            E('v', lambda eg=eg: h['v'].tensor_sub(
                rl2("tmpa", eg, 0), rl2("tmpb", eg, 0), rl2("tmpb", eg, 1)),
                inc=('v', 1))
            E('v', lambda eg=eg: h['v'].tensor_mul(
                rl2("tmpa", eg, 0), self.ps[:, 3840:3840 + QW],
                rl2("tmpa", eg, 0)), inc=('v', 1))
            E('v', lambda eg=eg: h['v'].tensor_sub(
                rl2("tmpc", eg, 1), rl2("tmpb", eg, 0), rl2("tmpa", eg, 0)),
                inc=('v', 1))
            E('v', lambda eg=eg: h['v'].tensor_add(
                rl2("tmpc", eg, 0), rl2("tmpb", eg, 1), rl2("tmpa", eg, 0)),
                inc=('v', 1))
            self.mark(f"mx_{qq}_{eg}")
        for eg in range(8):
            base = self._stgate()
            for i in range(8):
                self.wm('t', 'v', f"mx_{qq}_{i}")
                E('t', lambda eg=eg, i=i, base=base: h['t'].matmul(
                    self.ps[:, base:base + Q2],
                    self.b("owb")[:, 1024 * i + 128 * eg:1024 * i + 128 * (eg + 1)],
                    rl("tmpc", i), start=(i == 0), stop=(i == 7)), inc=('t', 1))
            self.mark(f"owmm_{qq}_{eg}")
            self.wm('v', 't', f"owmm_{qq}_{eg}")
            E('v', lambda eg=eg, base=base: h['v'].scalar_tensor_tensor(
                self.rl("rres", eg), self.ps[:, base:base + Q2],
                self.outbc[:, eg:eg + 1],
                self.rl("rres", eg), ALU.add, ALU.add), inc=('v', 1))
            self.mark(f"r3_{qq}_{eg}")
            self._stdone('v', f"r3_{qq}_{eg}")

    def _layer_norm(self, qq, tag, vdep):
        E, h = self.E, self.h
        QB, W2 = self.QB, self.W2
        lv, lv2 = self.lv, self.lv2
        vap_ = vap
        tg = f"{tag}_{qq}"
        self.wm('s', 'v', vdep)
        for eg in range(8):
            E('s', lambda eg=eg: h['s'].copy(lv("tmpa", eg), lv("rres", eg)), inc=('s', 1))
            self.sw('s')
            E('s', lambda eg=eg: h['s'].activation(lv("tmpb", eg), lv("tmpa", eg),
                                                   AF.Square), inc=('s', 1))
        self.mark(f"lnc_{tg}")
        self.wm('t', 's', f"lnc_{tg}")
        for eg in range(8):
            for bb in range(2):
                for c0, cw in self.CHK:
                    E('t', lambda eg=eg, bb=bb, c0=c0, cw=cw: h['t'].matmul(
                        self.ps[:, 2048 + bb * 1024 + c0:2048 + bb * 1024 + c0 + cw],
                        self.ones[:], lv("tmpa", eg, bb)[:, c0:c0 + cw],
                        start=(eg == 0), stop=(eg == 7)),
                        inc=('t', 1))
        self.mark(f"lnmmx_{tg}")
        self.wm('s', 't', f"lnmmx_{tg}")
        E('s', lambda: h['s'].activation(
            vap_(self.mrepB[:], (0, 0), [['p', 128], [QB, 2], [1, QB]]),
            self.psv(2048), AF.Copy, scale=1.0 / D), inc=('s', 1))
        self.sw('s')
        E('s', lambda: h['s'].activation(self.vtmpB[:], self.mrepB[:], AF.Square),
          inc=('s', 1))
        self.mark(f"lnm_{tg}")
        self.wm('t', 's', f"lnm_{tg}")
        for eg in range(8):
            for bb in range(2):
                for c0, cw in self.CHK:
                    E('t', lambda eg=eg, bb=bb, c0=c0, cw=cw: h['t'].matmul(
                        self.ps[:, 2048 + bb * 1024 + c0:2048 + bb * 1024 + c0 + cw],
                        self.ones[:], lv("tmpb", eg, bb)[:, c0:c0 + cw],
                        start=(eg == 0), stop=(eg == 7)),
                        inc=('t', 1))
        self.mark(f"lnmm_{tg}")
        self.wm('v', 's', f"lnm_{tg}")
        self.wm('v', 't', f"lnmm_{tg}")
        E('v', lambda: h['v'].scalar_tensor_tensor(
            vap_(self.vtmpB[:], (0, 0), [['p', 128], [QB, 2], [1, QB]]),
            self.psv(2048), 1.0 / D,
            vap_(self.vtmpB[:], (0, 0), [['p', 128], [QB, 2], [1, QB]]),
            ALU.mult, ALU.subtract), inc=('v', 1))
        self.mark(f"lnv_{tg}")
        self.wm('s', 'v', f"lnv_{tg}")
        E('s', lambda: h['s'].activation(self.vtmpB[:], self.vtmpB[:], AF.Ln,
                                         bias=LN_EPS), inc=('s', 1))
        self.sw('s')
        E('s', lambda: h['s'].activation(self.rrepB[:], self.vtmpB[:], AF.Exp,
                                         scale=-0.5), inc=('s', 1))
        self.mark(f"lnr_{tg}")
        self.wm('v', 's', f"lnr_{tg}")
        for eg in range(8):
            E('v', lambda eg=eg: h['v'].tensor_sub(lv("tmpa", eg), lv("rres", eg),
                                                   self.mrepB[:]), inc=('v', 1))
            E('v', lambda eg=eg: h['v'].tensor_mul(lv("xn", eg), lv("tmpa", eg),
                                                   self.rrepB[:]), inc=('v', 1))
        xnv = self.b("xn")
        leadv = vap_(xnv, (0, 0), [['p', 128], [W2, 8], [QB, 2], [1, 2]])
        E('v', lambda leadv=leadv: h['v'].tensor_scalar_mul(leadv, leadv,
                                                            self.maskc[:, 0:1]),
          inc=('v', 1))
        self.mark(f"ln_{tg}")

    def _ffn(self, qq, tag, lntag):
        E, h = self.E, self.h
        QB, W2 = self.QB, self.W2
        lv, lv2 = self.lv, self.lv2
        tg = f"{qq}_{tag}"
        self.wm('v', 't', f"lnmm_{lntag}_{qq}")
        for eg in range(8):
            for bb in range(2):
                xn = lv("xn", eg, bb)
                E('v', lambda xn=xn, eg=eg, bb=bb: h['v'].tensor_sub(
                    self.lv("tmpa", eg, bb)[:, 1:QB], xn[:, 1:QB], xn[:, 0:QB - 1]),
                    inc=('v', 1))
                E('v', lambda xn=xn, eg=eg, bb=bb: h['v'].scalar_tensor_tensor(
                    self.lv("tmpb", eg, bb)[:, 1:QB], self.lv("tmpa", eg, bb)[:, 1:QB],
                    self.mur[:, eg:eg + 1], xn[:, 0:QB - 1], ALU.mult, ALU.add),
                    inc=('v', 1))
                E('v', lambda xn=xn, eg=eg, bb=bb: h['v'].scalar_tensor_tensor(
                    self.lv("tmpc", eg, bb)[:, 1:QB], self.lv("tmpa", eg, bb)[:, 1:QB],
                    self.muk[:, eg:eg + 1], xn[:, 0:QB - 1], ALU.mult, ALU.add),
                    inc=('v', 1))
        self.mark(f"mix_{tg}")
        for eg in range(8):
            self.wm('t', 'v', f"mix_{tg}")
            if eg >= 2:
                self.wm('t', 's', f"rg_{tg}_{eg-2}")
            base = 2048 * (eg % 2)
            for i in range(8):
                for bb in range(2):
                    for c0, cw in self.CHK:
                        E('t', lambda eg=eg, i=i, bb=bb, c0=c0, cw=cw, base=base:
                            h['t'].matmul(
                                self.ps[:, base + bb * 1024 + c0:base + bb * 1024 + c0 + cw],
                                self.b("wr")[:, 1024 * i + 128 * eg:1024 * i + 128 * (eg + 1)],
                                lv("tmpb", i, bb)[:, c0:c0 + cw], start=(i == 0), stop=(i == 7)),
                            inc=('t', 1))
            self.mark(f"rmm_{tg}_{eg}")
            self.wm('s', 't', f"rmm_{tg}_{eg}")
            E('s', lambda eg=eg, base=base: h['s'].activation(lv2("tmpa", eg),
                                                              self.psv(base),
                                                              AF.Sigmoid), inc=('s', 1))
            self.mark(f"rg_{tg}_{eg}")
        for eg in range(8):
            self.wm('t', 's', f"rg_{tg}_7")
            if eg >= 2:
                self.wm('t', 'v', f"kr_{tg}_{eg-2}")
            base = 2048 * (eg % 2)
            for i in range(8):
                for bb in range(2):
                    for c0, cw in self.CHK:
                        E('t', lambda eg=eg, i=i, bb=bb, c0=c0, cw=cw, base=base:
                            h['t'].matmul(
                                self.ps[:, base + bb * 1024 + c0:base + bb * 1024 + c0 + cw],
                                self.b("wk")[:, 1024 * i + 128 * eg:1024 * i + 128 * (eg + 1)],
                                lv("tmpc", i, bb)[:, c0:c0 + cw], start=(i == 0), stop=(i == 7)),
                            inc=('t', 1))
            self.mark(f"kmm_{tg}_{eg}")
            self.wm('v', 't', f"kmm_{tg}_{eg}")
            E('v', lambda eg=eg, base=base: h['v'].tensor_scalar_max(lv2("tmpb", eg),
                                                                     self.psv(base), 0.0),
              inc=('v', 1))
            self.mark(f"kr_{tg}_{eg}")
            self.wm('s', 'v', f"kr_{tg}_{eg}")
            E('s', lambda eg=eg: h['s'].activation(lv("xn", eg), lv("tmpb", eg),
                                                   AF.Square), inc=('s', 1))
            self.mark(f"k2_{tg}_{eg}")
        for eg in range(8):
            self.wm('t', 's', f"k2_{tg}_7")
            if eg >= 2:
                self.wm('t', 'v', f"fo_{qq}_{tag}_{eg-2}")
            base = 2048 * (eg % 2)
            for i in range(8):
                for bb in range(2):
                    for c0, cw in self.CHK:
                        E('t', lambda eg=eg, i=i, bb=bb, c0=c0, cw=cw, base=base:
                            h['t'].matmul(
                                self.ps[:, base + bb * 1024 + c0:base + bb * 1024 + c0 + cw],
                                self.b("wv")[:, 1024 * i + 128 * eg:1024 * i + 128 * (eg + 1)],
                                lv("xn", i, bb)[:, c0:c0 + cw], start=(i == 0), stop=(i == 7)),
                            inc=('t', 1))
            self.mark(f"vmm_{tg}_{eg}")
            self.wm('v', 't', f"vmm_{tg}_{eg}")
            E('v', lambda eg=eg, base=base: h['v'].tensor_mul(lv2("tmpc", eg),
                                                              lv2("tmpa", eg),
                                                              self.psv(base)), inc=('v', 1))
            self.sw('v')
            E('v', lambda eg=eg: h['v'].tensor_add(lv("rres", eg), lv("rres", eg),
                                                   lv("tmpc", eg)), inc=('v', 1))
            self.mark(f"fo_{qq}_{tag}_{eg}")

    def _attnb(self, qq):
        E, h = self.E, self.h
        QB, W2 = self.QB, self.W2
        lv, lv2 = self.lv, self.lv2
        for eg in range(8):
            self.wm('t', 'v', f"ln_l3_{qq}")
            if eg >= 2:
                self.wm('t', 'v', f"qev_{qq}_{eg-2}")
            base = 2048 * (eg % 2)
            for i in range(8):
                for bb in range(2):
                    for c0, cw in self.CHK:
                        E('t', lambda eg=eg, i=i, bb=bb, c0=c0, cw=cw, base=base:
                            h['t'].matmul(
                                self.ps[:, base + bb * 1024 + c0:base + bb * 1024 + c0 + cw],
                                self.b("qwb")[:, 1024 * i + 128 * eg:1024 * i + 128 * (eg + 1)],
                                lv("xn", i, bb)[:, c0:c0 + cw], start=(i == 0), stop=(i == 7)),
                            inc=('t', 1))
            self.mark(f"qmm_{qq}_{eg}")
            self.wm('v', 't', f"qmm_{qq}_{eg}")
            E('v', lambda eg=eg, base=base: h['v'].tensor_copy(lv2("tmpa", eg),
                                                               self.psv(base)),
              inc=('v', 1))
            self.mark(f"qev_{qq}_{eg}")
            self.wm('s', 'v', f"qev_{qq}_{eg}")
            E('s', lambda eg=eg: h['s'].activation(lv("tmpb", eg), lv("tmpa", eg),
                                                   AF.Square), inc=('s', 1))
            self.mark(f"q2_{qq}_{eg}")
        self.wm('t', 's', f"q2_{qq}_7")
        for eg in range(8):
            for bb in range(2):
                for c0, cw in self.CHK:
                    E('t', lambda eg=eg, bb=bb, c0=c0, cw=cw: h['t'].matmul(
                        self.ps[0:16, 2048 + bb * 1024 + c0:2048 + bb * 1024 + c0 + cw],
                        self.hsel[:, 16 * eg:16 * (eg + 1)], lv("tmpb", eg, bb)[:, c0:c0 + cw],
                        start=(eg == 0), stop=(eg == 7)),
                        inc=('t', 1))
        self.mark(f"qsmm_{qq}")
        self.wm('v', 't', f"qsmm_{qq}")
        for eg in range(8):
            E('v', lambda eg=eg: h['v'].tensor_mul(lv("tmpb", eg, 0), lv("tmpa", eg, 0),
                                                   lv("tmpa", eg, 1)), inc=('v', 1))
        E('v', lambda: h['v'].tensor_copy(self.qsq[:], self.ps[0:16, 2048:2048 + QB]),
          inc=('v', 1))
        E('v', lambda: h['v'].scalar_tensor_tensor(
            self.qsq[:], self.ps[0:16, 3072:3072 + QB], 1.0, self.qsq[:],
            ALU.mult, ALU.add), inc=('v', 1))
        E('v', lambda: h['v'].tensor_scalar_mul(self.qsq[:], self.qsq[:], 0.125),
          inc=('v', 1))
        self.mark(f"qd_{qq}")
        self.wm('t', 'v', f"qd_{qq}")
        for eg in range(8):
            for c0, cw in self.CHK:
                E('t', lambda eg=eg, c0=c0, cw=cw: h['t'].matmul(
                    self.ps[0:16, 2048 + c0:2048 + c0 + cw],
                    self.hsel[:, 16 * eg:16 * (eg + 1)], lv("tmpb", eg, 0)[:, c0:c0 + cw],
                    start=(eg == 0), stop=(eg == 7)),
                    inc=('t', 1))
        self.mark(f"qdmm_{qq}")
        self.wm('v', 't', f"qdmm_{qq}")
        E('v', lambda: h['v'].scalar_tensor_tensor(
            self.sigg[:], self.ps[0:16, 2048:2048 + QB], -0.25,
            self.qsq[:], ALU.mult, ALU.add), inc=('v', 1))
        self.mark(f"gv_{qq}")
        self.wm('s', 'v', f"gv_{qq}")
        E('s', lambda: h['s'].activation(self.sigb[:], self.sigg[:], AF.Sigmoid),
          inc=('s', 1))
        self.mark(f"sg_{qq}")
        for eg in range(8):
            if eg == 0:
                self.wm('t', 'v', f"qd_{qq}")
                self.wm('t', 'v', "mmat")
            else:
                self.wm('t', 'v', f"zev_{qq}_{eg-1}")
            for hh in (2 * eg, 2 * eg + 1):
                po, pc = 64 * (hh % 2), 64 * (hh // 2)
                for bb in range(2):
                    for c0, cw in self.CHK:
                        E('t', lambda eg=eg, hh=hh, po=po, pc=pc, bb=bb, c0=c0, cw=cw:
                            h['t'].matmul(
                                self.ps[po:po + 64, bb * 1024 + c0:bb * 1024 + c0 + cw],
                                self.mmat[po:po + 64, pc:pc + 64],
                                lv("tmpa", eg, bb)[po:po + 64, c0:c0 + cw],
                                start=True, stop=True),
                            inc=('t', 1))
            self.mark(f"zmm_{qq}_{eg}")
            self.wm('v', 't', f"zmm_{qq}_{eg}")
            E('v', lambda eg=eg: h['v'].tensor_copy(lv2("tmpb", eg), self.psv(0)),
              inc=('v', 1))
            self.mark(f"zev_{qq}_{eg}")
        for eg in range(8):
            self.wm('t', 'v', f"zev_{qq}_{eg}")
            self.wm('t', 's', f"sg_{qq}")
            if eg >= 1:
                self.wm('t', 'v', f"mx_{qq}_{eg-1}")
            for c0, cw in self.CHK:
                E('t', lambda eg=eg, c0=c0, cw=cw: h['t'].matmul(
                    self.ps[:, 3072 + c0:3072 + c0 + cw],
                    self.ssel[:, 128 * eg:128 * (eg + 1)], self.sigb[:, c0:c0 + cw],
                    start=True, stop=True), inc=('t', 1))
            self.mark(f"smm_{qq}_{eg}")
            self.wm('v', 't', f"smm_{qq}_{eg}")
            E('v', lambda eg=eg: h['v'].tensor_sub(lv("tmpa", eg, 0), lv("tmpb", eg, 0),
                                                   lv("tmpb", eg, 1)), inc=('v', 1))
            E('v', lambda eg=eg: h['v'].tensor_mul(lv("tmpa", eg, 0),
                                                   self.ps[:, 3072:3072 + QB],
                                                   lv("tmpa", eg, 0)), inc=('v', 1))
            E('v', lambda eg=eg: h['v'].tensor_sub(lv("tmpc", eg, 1), lv("tmpb", eg, 0),
                                                   lv("tmpa", eg, 0)), inc=('v', 1))
            E('v', lambda eg=eg: h['v'].tensor_add(lv("tmpc", eg, 0), lv("tmpb", eg, 1),
                                                   lv("tmpa", eg, 0)), inc=('v', 1))
            self.mark(f"mx_{qq}_{eg}")
        for eg in range(8):
            self.wm('t', 'v', f"mx_{qq}_7")
            if eg >= 2:
                self.wm('t', 'v', f"r3_{qq}_{eg-2}")
            base = 2048 * (eg % 2)
            for i in range(8):
                for bb in range(2):
                    for c0, cw in self.CHK:
                        E('t', lambda eg=eg, i=i, bb=bb, c0=c0, cw=cw, base=base:
                            h['t'].matmul(
                                self.ps[:, base + bb * 1024 + c0:base + bb * 1024 + c0 + cw],
                                self.b("owb")[:, 1024 * i + 128 * eg:1024 * i + 128 * (eg + 1)],
                                lv("tmpc", i, bb)[:, c0:c0 + cw], start=(i == 0), stop=(i == 7)),
                            inc=('t', 1))
            self.mark(f"owmm_{qq}_{eg}")
            self.wm('v', 't', f"owmm_{qq}_{eg}")
            E('v', lambda eg=eg, base=base: h['v'].scalar_tensor_tensor(
                lv2("rres", eg), self.psv(base), self.outbc[:, eg:eg + 1],
                lv2("rres", eg), ALU.add, ALU.add), inc=('v', 1))
            self.mark(f"r3_{qq}_{eg}")

    def _finish(self):
        self.W('sp', 'out', self.cnt['out'])


def _prep_core_inputs(k, L, inp):
    JS = L // NC
    TB = JS + 2
    x = np.asarray(inp["L0_attn_a_x"], np.float32)
    xT = np.ascontiguousarray(x.transpose(2, 0, 1).reshape(D, 2 * L))
    xr = np.zeros((D, 2, TB), np.float32)
    j0 = JS * k
    xr[:, :, 2:] = x[:, j0:j0 + JS, :].transpose(2, 0, 1)
    if k > 0:
        xr[:, :, 0:2] = x[:, j0 - 2:j0, :].transpose(2, 0, 1)
    xri = np.ascontiguousarray(
        x[:, j0:j0 + JS, :].transpose(2, 1, 0).reshape(D, 2 * JS))
    W_in = np.asarray(inp["L0_attn_a_W_in"], np.float32)
    cols = list(range(128 * k, 128 * (k + 1))) + \
           list(range(D + 128 * k, D + 128 * (k + 1))) + \
           list(range(2 * D, 2 * D + 64))
    winT = np.ascontiguousarray(W_in.T[:, cols])
    alog = np.asarray(inp["L0_attn_a_A_log"], np.float32)
    alogp = np.zeros((DS, NST), np.float32)
    for ch in range(DS):
        n_ = ch % 32
        dg = ch // 32
        for t in range(NST):
            alogp[ch, t] = alog[n_, 128 * k + 4 * t + dg]
    d = {
        "xT": xT, "xres": np.ascontiguousarray(xr.reshape(D, 2 * TB)),
        "xresi": xri,
        "winT": winT,
        "woutT": np.ascontiguousarray(np.asarray(inp["L0_attn_a_W_out"], np.float32).T),
        "wrT": np.ascontiguousarray(np.asarray(inp["L0_ffn_W_r"], np.float32).T),
        "wkT": np.ascontiguousarray(np.asarray(inp["L0_ffn_W_k"], np.float32).T),
        "wvT": np.ascontiguousarray(np.asarray(inp["L0_ffn_W_v"], np.float32).T),
        "qw": np.ascontiguousarray(
            np.asarray(inp["L0_attn_b_q_weight"], np.float32).reshape(D, D)),
        "vw": np.ascontiguousarray(
            np.asarray(inp["L0_attn_b_v_weight"], np.float32).reshape(D, D)),
        "owT": np.ascontiguousarray(np.asarray(inp["L0_attn_b_out_w"], np.float32).T),
        "outb": np.ascontiguousarray(
            np.asarray(inp["L0_attn_b_out_b"], np.float32).reshape(8, 128).T),
        "mur": np.ascontiguousarray(np.asarray(inp["L0_ffn_mu_r"], np.float32).reshape(8, 128).T),
        "muk": np.ascontiguousarray(np.asarray(inp["L0_ffn_mu_k"], np.float32).reshape(8, 128).T),
        "alogp": alogp,
        "dcol": np.ascontiguousarray(
            np.asarray(inp["L0_attn_a_D_param"], np.float32)[128 * k:128 * (k + 1)].reshape(DS, 1)),
        "convw": np.ascontiguousarray(
            np.asarray(inp["L0_attn_a_conv_w"], np.float32)[128 * k:128 * (k + 1), 0, :]),
        "convb": np.ascontiguousarray(
            np.asarray(inp["L0_attn_a_conv_b"], np.float32)[128 * k:128 * (k + 1)].reshape(DS, 1)),
        "leadmask": np.full((DS, 1), 0.0 if k == 0 else 1.0, np.float32),
    }
    ones = np.ones((128, 128), BF)
    ysel = np.zeros((128, 32 * 128), BF)
    for t in range(32):
        for ch in range(128):
            ysel[ch, 128 * t + 4 * t + ch // 32] = 1
    hsel = np.zeros((128, 8 * 16), BF)
    for eg in range(8):
        for p in range(128):
            hsel[p, 16 * eg + 2 * eg + p // 64] = 1
    ssel = np.zeros((16, 8 * 128), BF)
    for eg in range(8):
        for p in range(128):
            ssel[2 * eg + p // 64, 128 * eg + p] = 1
    dsel = np.zeros((128, 32 * 128), np.float16)
    for t in range(32):
        for ch in range(128):
            dsel[4 * t + ch // 32, 128 * t + ch] = 1
    d.update({"ones128": ones, "ysel": ysel, "hsel": hsel, "ssel": ssel,
              "dsel": dsel})
    return d


def assemble_output(results, L):
    JS = L // NC
    QW = min(256, JS)
    NQ = JS // QW
    out = np.zeros((2, L, D), np.float32)
    for k in range(NC):
        o = np.asarray(results[k]["out"]).reshape(D, NQ, 2 * QW)
        for qq in range(NQ):
            blk = o[:, qq, :]
            if qq == 0:
                b = blk.reshape(D, 2, QW)
                out[:, JS * k:JS * k + QW, :] = b.transpose(1, 2, 0)
            else:
                b = blk.reshape(D, QW, 2)
                out[:, JS * k + qq * QW:JS * k + (qq + 1) * QW, :] = \
                    b.transpose(2, 1, 0)
    return out


def kernel(**inputs):
    L = inputs["L0_attn_a_x"].shape[1]
    kb = KB(L=L, SUPW=min(2048, L))
    nc = kb.build()
    in_maps = [_prep_core_inputs(k, L, inputs) for k in range(NC)]
    res = run_bass_kernel_spmd(nc, in_maps, core_ids=list(range(NC)))
    return assemble_output(res.results, L)
